# version 01
# baseline (speedup 1.0000x reference)
import numpy as np
import ml_dtypes
import concourse.bass as bass
import concourse.mybir as mybir
from concourse.bass_utils import run_bass_kernel_spmd

F32 = mybir.dt.float32
BF16 = mybir.dt.bfloat16
AF = mybir.ActivationFunctionType
ALU = mybir.AluOpType

NCORE = 8
B = 8
TL = 256
N = B * TL
E = 1024
H = 16
D = 64
NCH = 16
NFC = 8
BIG = 1e30
U = 16384
POOL_ELEMS = 98304

BUILD_UPTO = 9
SUBPH = 9
DEBUG_TAPS = ()


def build_consts():
    c = {}
    c["ident_f"] = np.eye(128, dtype=np.float32)
    c["ident_b"] = np.eye(128, dtype=ml_dtypes.bfloat16)
    tri = np.where(np.arange(128)[None, :] >= np.arange(128)[:, None], BIG, 0.0)
    c["triu_b"] = tri.astype(ml_dtypes.bfloat16)
    c["ones_row_b"] = np.ones((1, 128), dtype=ml_dtypes.bfloat16)
    pos = np.arange(B, dtype=np.float64)
    inv = 1.0 / 10000.0 ** (np.arange(0, D, 2) / D)
    fr = pos[:, None] * inv[None, :]
    cos = np.repeat(np.cos(fr), 2, axis=1)
    sin = np.repeat(np.sin(fr), 2, axis=1)
    sin_pm = sin.copy()
    sin_pm[:, 0::2] = -sin[:, 0::2]
    c["cos_bc"] = np.broadcast_to(cos[None, :, :], (128, B, D)).astype(ml_dtypes.bfloat16).copy()
    c["sin_bc"] = np.broadcast_to(sin_pm[None, :, :], (128, B, D)).astype(ml_dtypes.bfloat16).copy()
    blk = np.zeros((128, 2), dtype=ml_dtypes.bfloat16)
    blk[0:64, 0] = 1
    blk[64:128, 1] = 1
    c["blk2_b"] = blk
    bm = np.zeros((128, 8, 16), dtype=ml_dtypes.bfloat16)
    for mo in range(8):
        bm[0:64, mo, 2 * mo] = 1
        bm[64:128, mo, 2 * mo + 1] = 1
    c["blkmap_b"] = bm.reshape(128, 128)
    return c


class Ctr:
    def __init__(self, nc, name):
        self.sem = nc.alloc_semaphore(name)
        self.val = 0

    def inc(self, inst, n=16):
        inst.then_inc(self.sem, n)
        self.val += n
        return inst

    def wait_all(self, eng):
        eng.wait_ge(self.sem, self.val)

    def wait(self, eng, val):
        eng.wait_ge(self.sem, val)


def build_program():
    nc = bass.Bass("TRN2", target_bir_lowering=False, num_devices=NCORE,
                   detect_race_conditions=False)
    din = {}

    def inp(name, shape, dtype=F32):
        din[name] = nc.dram_tensor(name, list(shape), dtype, kind="ExternalInput")
        return din[name]

    x_in = inp("x", (B, TL, E))
    Wq_in = inp("Wq", (E, E))
    Wk_in = inp("Wk", (E, E))
    Wv_in = inp("Wv", (E, E))
    Wa_in = inp("Wa", (E, H))
    Wg_in = inp("Wg", (E, E))
    Wo_in = inp("Wo", (E, E))
    Alog_in = inp("A_log", (H, 1))
    dtb_in = inp("dt_bias", (H, 1))
    rmsw_in = inp("rms_w", (1, D))
    gk_in = inp("gate_kernel", (E, H))
    gb_in = inp("gate_bias", (1, H))
    wgate_in = inp("w_gate", (16, E, 512))
    wup_in = inp("w_up", (16, E, 512))
    wdown_in = inp("w_down", (16, 512, E))
    swg_in = inp("sw_gate", (E, E))
    swu_in = inp("sw_up", (E, E))
    swd_in = inp("sw_down", (E, E))
    Wqkv_in = inp("Wqkv", (E, 3 * E))
    bqkv_in = inp("bqkv", (1, 3 * E))
    Wout_in = inp("Wout", (E, E))
    bout_in = inp("bout", (1, E))
    wff_in = inp("wff", (E, E))
    disha_in = inp("disha", (D, E))
    pmask_in = inp("pmask", (8, 8))
    cid_f = inp("ident_f", (128, 128))
    cid_b = inp("ident_b", (128, 128), BF16)
    ctri_b = inp("triu_b", (128, 128), BF16)
    cones_b = inp("ones_row_b", (1, 128), BF16)
    ccos = inp("cos_bc", (128, B, D), BF16)
    csin = inp("sin_bc", (128, B, D), BF16)
    cblk2 = inp("blk2_b", (128, 2), BF16)
    cblkmap = inp("blkmap_b", (128, 128), BF16)

    o_out = nc.dram_tensor("o", [B, TL, E], F32, kind="ExternalOutput")

    qf_d = nc.dram_tensor("qf_d", [128, NFC, N], BF16, kind="Internal")
    kf_d = nc.dram_tensor("kf_d", [128, NFC, N], BF16, kind="Internal")
    vt_d = nc.dram_tensor("vt_d", [128, NCH, E], BF16, kind="Internal")
    cum_d = nc.dram_tensor("cum_d", [H, N], F32, kind="Internal")
    rqec_d = nc.dram_tensor("rqec_d", [H, N], F32, kind="Internal")
    dec_dram = nc.dram_tensor("dec_dram", [NCORE, B * H], F32, kind="Internal")
    c_d = nc.dram_tensor("c_d", [H, N], BF16, kind="Internal")
    wffT_d = nc.dram_tensor("wffT_d", [E, E], BF16, kind="Internal")
    PAY = B * 512 * 128
    PAYT = PAY + 8 * H
    cc_in = nc.dram_tensor("cc_in", [1, PAYT], F32, kind="Internal")
    cc_out = nc.dram_tensor("cc_out", [1, NCORE * PAYT], F32, kind="Internal")

    taps = {}

    def tap(name, shape, dtype=F32):
        if name in DEBUG_TAPS:
            taps[name] = nc.dram_tensor("tap_" + name, list(shape), dtype,
                                        kind="ExternalOutput")
            return taps[name]
        return None

    pool = nc.alloc_sbuf_tensor("pool", [128, POOL_ELEMS], BF16)

    def seg(off, nelem, dtype=BF16):
        assert off + nelem <= POOL_ELEMS, (off, nelem)
        v = pool[:, off:off + nelem]
        if dtype == F32:
            assert off % 2 == 0 and nelem % 2 == 0
            v = v.bitcast(F32)
        return v

    sb = nc.alloc_sbuf_tensor
    id_f = sb("id_f", [128, 128], F32)
    id_b = sb("id_b", [128, 128], BF16)
    tri_b = sb("tri_b", [128, 128], BF16)
    ones_b = sb("ones_b", [1, 128], BF16)
    cos_t = sb("cos_t", [128, B * D], BF16)
    sin_t = sb("sin_t", [128, B * D], BF16)
    blk2 = sb("blk2", [128, 2], BF16)
    blkmap = sb("blkmap", [128, 128], BF16)
    rw_bc = sb("rw_bc", [128, D], BF16)
    gb_bc = sb("gb_bc", [128, H], F32)
    Wa_sb = sb("Wa_sb", [128, NFC * H], BF16)
    eA = sb("eA", [H, 1], F32)
    dtb = sb("dtb", [H, 1], F32)
    eps_ln = sb("eps_ln", [128, 1], F32)
    eps_e8 = sb("eps_e8", [128, 1], F32)
    eps_rms = sb("eps_rms", [128, 1], F32)
    eps_w = sb("eps_w", [128, 1], F32)
    stat_sum = sb("stat_sum", [128, NCH], F32)
    stat_sq = sb("stat_sq", [128, NCH], F32)
    stat_m = sb("stat_m", [128, NCH], F32)
    stat_r = sb("stat_r", [128, NCH], F32)
    rq_tok = sb("rq_tok", [128, NCH * H], F32)
    rk_tok = sb("rk_tok", [128, NCH * H], F32)
    cum_tok = sb("cum_tok", [128, NCH * H], F32)
    w_tok = sb("w_tok", [128, NCH * H], BF16)
    rr_tok = sb("rr_tok", [128, NCH * H], F32)
    pmask = sb("pmask_sb", [8, 8], F32)
    CE_all = sb("CE_all", [8, NCORE * H], F32)
    dec_acc = sb("dec_acc", [8, H], F32)
    dec_exp = sb("dec_exp", [8, H], F32)
    dec_bcv = sb("dec_bcv", [128, 64], F32)
    rt_sc = sb("rt_sc", [128, NCH * H], F32)
    rt_ms = sb("rt_ms", [128, NCH * H], F32)
    rt_cnt = sb("rt_cnt", [128, NCH * H], F32)
    rt_sc2 = sb("rt_sc2", [128, NCH * H], F32)
    rt_g1 = sb("rt_g1", [128, NCH * 4], F32)
    rt_g2 = sb("rt_g2", [128, NCH * 4], F32)
    rt_w = sb("rt_w", [128, NCH], F32)
    rt_cmb = sb("rt_cmb", [128, NCH * 4], F32)
    ce_sm = sb("ce_sm", [H, B], F32)
    ceT_sm = sb("ceT_sm", [8, H], F32)
    brow = pool[0:1, 94208:97280]

    psA = nc.alloc_psum_tensor("psA", [128, 2048], F32)
    psB = nc.alloc_psum_tensor("psB", [128, 2048], F32)

    hw = Ctr(nc, "hw")
    hwa = Ctr(nc, "hwa")
    hwb = Ctr(nc, "hwb")
    hs = Ctr(nc, "hs")
    gp = Ctr(nc, "gp")
    gpb = Ctr(nc, "gpb")
    pe = Ctr(nc, "pe")
    ve = Ctr(nc, "ve")
    se = Ctr(nc, "se")
    cc = Ctr(nc, "cc")

    xn_fm = seg(0, U).rearrange("p (fc n) -> p fc n", fc=NFC)
    PR = seg(U, U)
    PRf = PR.rearrange("p (a n) -> p a n", a=NFC)
    PRt = PR.rearrange("p (k f) -> p k f", k=NCH)
    att = PR.rearrange("p (k f) -> p k f", k=NCH)
    xtok = seg(65536, U).rearrange("p (k f) -> p k f", k=NCH)
    xf32 = seg(90112, 4096, F32)
    ln_sq = seg(94208, 4096, F32)

    W_A = seg(32768, 8192).rearrange("p (kc o) -> p kc o", kc=NFC)
    W_B = seg(40960, 8192).rearrange("p (kc o) -> p kc o", kc=NFC)
    W_C = seg(49152, 8192).rearrange("p (kc o) -> p kc o", kc=NFC)
    sqb = seg(57344, 2048)
    ROW_LA = seg(59392, 4096, F32)[0:16, :]
    ROW_CUM = seg(63488, 4096, F32)[0:16, :]
    ROW_RNQ = seg(67584, 4096, F32)[0:16, :]
    ROW_NSQ = seg(71680, 4096, F32)[0:16, :]
    ROW_RN = seg(75776, 4096, F32)[0:16, :]
    ROW_DTT = seg(79872, 4096, F32)[0:16, :]
    ROW_SP = seg(83968, 4096, F32)[0:16, :]

    with nc.Block() as blk:
        @blk.sync
        def _(sync):
            hw.inc(sync.dma_start(id_f[:], cid_f[:]))
            hw.inc(sync.dma_start(id_b[:], cid_b[:]))
            hw.inc(sync.dma_start(tri_b[:], ctri_b[:]))
            hw.inc(sync.dma_start(ones_b[:], cones_b[:]))
            hw.inc(sync.dma_start(cos_t[:], ccos[:].rearrange("p b d -> p (b d)")))
            hw.inc(sync.dma_start(sin_t[:], csin[:].rearrange("p b d -> p (b d)")))
            hw.inc(sync.dma_start(blk2[:], cblk2[:]))
            hw.inc(sync.dma_start(blkmap[:], cblkmap[:]))
            hw.inc(sync.dma_start(gb_bc[:], gb_in[:].broadcast_to([128, H])))
            hw.inc(sync.dma_start(eA[:], Alog_in[:]))
            hw.inc(sync.dma_start(dtb[:], dtb_in[:]))
            hw.inc(sync.dma_start(pmask[:], pmask_in[:]))

        @blk.gpsimd
        def _(g):
            gp.inc(g.dma_start(rw_bc[:], rmsw_in[:].broadcast_to([128, D])))
            gp.inc(g.dma_start(W_A, Wq_in[:].rearrange("(kc p) o -> p kc o", p=128)))
            gp.inc(g.dma_start(W_B, Wk_in[:].rearrange("(kc p) o -> p kc o", p=128)))
            gp.inc(g.dma_start(W_C, Wv_in[:].rearrange("(kc p) o -> p kc o", p=128)))
            gp.inc(g.dma_start(Wa_sb[:].rearrange("p (kc h) -> p kc h", kc=NFC),
                               Wa_in[:].rearrange("(kc p) h -> p kc h", p=128)))

        @blk.vector
        def _(v):
            v.memset(eps_ln[:], 1e-5)
            v.memset(eps_e8[:], 1e-8)
            v.memset(eps_rms[:], 1e-6)
            v.memset(eps_w[:], 1e-20)

    with nc.Block() as blk:
        @blk.scalar
        def _(s):
            hw.wait_all(s)
            s.activation(eA[:], eA[:], AF.Exp)

    def ln_pass(load_chunk, dst_tok, nrm_extra=None):
        raise NotImplementedError

    def ln_from_dram(x_dram, dst_fm):
        base_ha = [hwa.val, hwb.val]
        base_ve = ve.val
        base_se = se.val

        with nc.Block() as blk:
            @blk.sync
            def _(sync):
                for k in range(NCH):
                    b, th = k // 2, k % 2
                    sl = [hwa, hwb][k % 2]
                    if k >= 2:
                        ve.wait(sync, base_ve + (k - 1))
                        se.wait(sync, base_se + (k - 1))
                    sl.inc(sync.dma_start(
                        xf32[:, (k % 2) * 1024:(k % 2 + 1) * 1024],
                        x_dram[b, th * 128:(th + 1) * 128, :]))

            @blk.vector
            def _(v):
                for k in range(NCH):
                    [hwa, hwb][k % 2].wait(v, base_ha[k % 2] + (k // 2 + 1) * 16)
                    xv = xf32[:, (k % 2) * 1024:(k % 2 + 1) * 1024]
                    ve.inc(v.reduce_sum(stat_sum[:, k:k + 1], xv,
                                        axis=mybir.AxisListType.X), 1)

            @blk.scalar
            def _(s):
                for k in range(NCH):
                    [hwa, hwb][k % 2].wait(s, base_ha[k % 2] + (k // 2 + 1) * 16)
                    xv = xf32[:, (k % 2) * 1024:(k % 2 + 1) * 1024]
                    sqv = ln_sq[:, (k % 2) * 1024:(k % 2 + 1) * 1024]
                    se.inc(s.activation(sqv, xv, AF.Square,
                                        accum_out=stat_sq[:, k:k + 1]), 1)

        stats_combine()

        base_ha2 = [hwa.val, hwb.val]
        base_ve2 = ve.val
        with nc.Block() as blk:
            @blk.sync
            def _(sync):
                for k in range(NCH):
                    b, th = k // 2, k % 2
                    sl = [hwa, hwb][k % 2]
                    if k >= 2:
                        ve.wait(sync, base_ve2 + (k - 1))
                    sl.inc(sync.dma_start(
                        xf32[:, (k % 2) * 1024:(k % 2 + 1) * 1024],
                        x_dram[b, th * 128:(th + 1) * 128, :]))

            @blk.vector
            def _(v):
                for k in range(NCH):
                    [hwa, hwb][k % 2].wait(v, base_ha2[k % 2] + (k // 2 + 1) * 16)
                    xv = xf32[:, (k % 2) * 1024:(k % 2 + 1) * 1024]
                    ve.inc(v.tensor_scalar(xtok[:, k, :], xv, stat_m[:, k:k + 1],
                                           stat_r[:, k:k + 1],
                                           op0=ALU.subtract, op1=ALU.mult), 1)

        transpose_tok2fm(xtok, dst_fm)

    def stats_combine():
        with nc.Block() as blk:
            @blk.vector
            def _(v):
                v.tensor_scalar_mul(stat_r[:], stat_sq[:], 1.0 / E)
                ve.inc(v.tensor_scalar_mul(stat_m[:], stat_sum[:], 1.0 / E), 1)
                ve.wait_all(v)
                ve.inc(v.tensor_tensor(stat_sum[:], stat_m[:], stat_m[:],
                                       op=ALU.mult), 1)
                ve.wait_all(v)
                v.tensor_sub(stat_r[:], stat_r[:], stat_sum[:])

        with nc.Block() as blk:
            @blk.scalar
            def _(s):
                s.activation(stat_r[:], stat_r[:], AF.Sqrt, bias=eps_ln[:, 0:1])

        with nc.Block() as blk:
            @blk.vector
            def _(v):
                v.reciprocal(stat_r[:], stat_r[:])

    def transpose_tok2fm(src_tok, dst_fm):
        base_pe = pe.val
        base_ve = ve.val
        psv = psA[:].bitcast(BF16)

        with nc.Block() as blk:
            @blk.tensor
            def _(te):
                for idx in range(NCH * NFC):
                    k, fc = idx // NFC, idx % NFC
                    slot = idx % 4
                    if idx >= 4:
                        ve.wait(te, base_ve + (idx - 3))
                    pe.inc(te.transpose(psv[:, slot * 1024:slot * 1024 + 128],
                                        src_tok[:, k, fc * 128:(fc + 1) * 128],
                                        id_b[:]), 1)

            @blk.vector
            def _(v):
                for idx in range(NCH * NFC):
                    k, fc = idx // NFC, idx % NFC
                    slot = idx % 4
                    pe.wait(v, base_pe + idx + 1)
                    ve.inc(v.tensor_copy(dst_fm[:, fc, k * 128:(k + 1) * 128],
                                         psv[:, slot * 1024:slot * 1024 + 128]), 1)

    if BUILD_UPTO >= 1:
        ln_from_dram(x_in, xn_fm)
        t = tap("xn_fm", (128, NFC * N), BF16)
        if t is not None:
            with nc.Block() as blk:
                @blk.sync
                def _(sync):
                    hw.inc(sync.dma_start(t[:], seg(0, U)))
                    hw.wait_all(sync)

    def transpose_rows16(src_rows, dst_tokbuf, cast_bf=False):
        base_pe = pe.val
        base_ve = ve.val
        dtv = dst_tokbuf[:].rearrange("p (k h) -> p k h", k=NCH)
        with nc.Block() as blk:
            @blk.tensor
            def _(te):
                for k in range(NCH):
                    slot = k % 4
                    if k >= 4:
                        ve.wait(te, base_ve + (k - 3))
                    pe.inc(te.transpose(psB[:, slot * 512:slot * 512 + 16],
                                        src_rows[:, k * 128:(k + 1) * 128],
                                        id_f[0:16, 0:16]), 1)

            @blk.vector
            def _(v):
                for k in range(NCH):
                    slot = k % 4
                    pe.wait(v, base_pe + k + 1)
                    ve.inc(v.tensor_copy(dtv[:, k, :],
                                         psB[:, slot * 512:slot * 512 + 16]), 1)

    def wproj_fm(Wv_, dram_out, r_tok_dst, save_rnq):
        for mo in range(NFC):
            base_pe = pe.val
            base_se = se.val
            with nc.Block() as blk:
                @blk.tensor
                def _(te):
                    for kc in range(NFC):
                        for Tt in range(4):
                            inst = te.matmul(psA[:, Tt * 512:(Tt + 1) * 512],
                                             Wv_[:, kc, mo * 128:(mo + 1) * 128],
                                             xn_fm[:, kc, Tt * 512:(Tt + 1) * 512],
                                             start=(kc == 0), stop=(kc == NFC - 1))
                    pe.inc(inst, 1)

                @blk.vector
                def _(v):
                    pe.wait(v, base_pe + 1)
                    for Tt in range(4):
                        ve.inc(v.tensor_copy(PRf[:, mo, Tt * 512:(Tt + 1) * 512],
                                             psA[:, Tt * 512:(Tt + 1) * 512]), 1)

                @blk.scalar
                def _(s):
                    pe.wait(s, base_pe + 1)
                    for Tt in range(4):
                        se.inc(s.activation(sqb[:, Tt * 512:(Tt + 1) * 512],
                                            psA[:, Tt * 512:(Tt + 1) * 512],
                                            AF.Square), 1)

                @blk.tensor
                def _(te):
                    se.wait(te, base_se + 4)
                    for Tt in range(4):
                        pe.inc(te.matmul(psB[0:H, Tt * 512:(Tt + 1) * 512],
                                         blkmap[:, mo * H:(mo + 1) * H],
                                         sqb[:, Tt * 512:(Tt + 1) * 512],
                                         start=(mo == 0), stop=(mo == NFC - 1)), 1)

        with nc.Block() as blk:
            @blk.vector
            def _(v):
                v.tensor_copy(ROW_NSQ[:], psB[0:H, :])

        with nc.Block() as blk:
            @blk.scalar
            def _(s):
                s.activation(ROW_RN[:], ROW_NSQ[:], AF.Sqrt)

        with nc.Block() as blk:
            @blk.vector
            def _(v):
                ve.inc(v.tensor_scalar_max(ROW_RN[:], ROW_RN[:], 1e-12), 1)
                ve.wait_all(v)
                v.reciprocal(ROW_RN[:], ROW_RN[:])
                if save_rnq:
                    v.tensor_copy(ROW_RNQ[:], ROW_RN[:])

        transpose_rows16(ROW_RN, r_tok_dst)

        with nc.Block() as blk:
            @blk.sync
            def _(sync):
                hw.inc(sync.dma_start(dram_out, PR))
                hw.wait_all(sync)

    def vproj_tok(Wv_, dram_out):
        for Tt in range(NCH):
            base_pe = pe.val
            with nc.Block() as blk:
                @blk.tensor
                def _(te):
                    for kc in range(NFC):
                        for no in range(2):
                            inst = te.matmul(psA[:, no * 512:(no + 1) * 512],
                                             xn_fm[:, kc, Tt * 128:(Tt + 1) * 128],
                                             Wv_[:, kc, no * 512:(no + 1) * 512],
                                             start=(kc == 0), stop=(kc == NFC - 1))
                    pe.inc(inst, 1)

                @blk.vector
                def _(v):
                    pe.wait(v, base_pe + 1)
                    for no in range(2):
                        rkap = rk_tok[:, Tt * H + no * 8:Tt * H + (no + 1) * 8]
                        rkap = rkap.unsqueeze(2).broadcast_to([128, 8, 64])
                        pv = psA[:, no * 512:(no + 1) * 512].rearrange(
                            "p (h d) -> p h d", h=8)
                        ov = PRt[:, Tt, no * 512:(no + 1) * 512].rearrange(
                            "p (h d) -> p h d", h=8)
                        ve.inc(v.tensor_tensor(ov, pv, rkap, op=ALU.mult), 1)

        with nc.Block() as blk:
            @blk.sync
            def _(sync):
                hw.inc(sync.dma_start(dram_out, PR))
                hw.wait_all(sync)

    def dt_cum_chain():
        base_pe = pe.val
        with nc.Block() as blk:
            @blk.tensor
            def _(te):
                for Tt in range(4):
                    for kc in range(NFC):
                        inst = te.matmul(psB[0:H, Tt * 512:(Tt + 1) * 512],
                                         Wa_sb[:, kc * H:(kc + 1) * H],
                                         xn_fm[:, kc, Tt * 512:(Tt + 1) * 512],
                                         start=(kc == 0), stop=(kc == NFC - 1))
                    pe.inc(inst, 1)

            @blk.vector
            def _(v):
                pe.wait(v, base_pe + 4)
                ve.inc(v.tensor_scalar(ROW_DTT[:], psB[0:H, :], dtb[:, 0:1], 0.0,
                                       op0=ALU.add, op1=ALU.max), 1)

            @blk.scalar
            def _(s):
                pe.wait(s, base_pe + 4)
                se.inc(s.activation(ROW_SP[:], psB[0:H, :], AF.Abs,
                                    bias=dtb[:, 0:1]), 1)

        with nc.Block() as blk:
            @blk.scalar
            def _(s):
                s.activation(ROW_SP[:], ROW_SP[:], AF.Exp, scale=-1.0)
                s.activation(ROW_SP[:], ROW_SP[:], AF.Ln, bias=1.0)

        with nc.Block() as blk:
            @blk.vector
            def _(v):
                v.tensor_add(ROW_DTT[:], ROW_DTT[:], ROW_SP[:])
                v.tensor_scalar_mul(ROW_DTT[:], ROW_DTT[:], eA[:, 0:1])

        with nc.Block() as blk:
            @blk.scalar
            def _(s):
                s.activation(ROW_LA[:], ROW_DTT[:], AF.Exp, scale=-1.0)
                s.activation(ROW_LA[:], ROW_LA[:], AF.Ln, bias=eps_e8[0:16, 0:1])

        with nc.Block() as blk:
            @blk.vector
            def _(v):
                for b in range(B):
                    inst = v.tensor_tensor_scan(ROW_CUM[:, b * TL:(b + 1) * TL],
                                                ROW_LA[:, b * TL:(b + 1) * TL],
                                                ROW_LA[:, b * TL:(b + 1) * TL],
                                                initial=0.0, op0=ALU.add, op1=ALU.bypass)
                ve.inc(inst, 1)
                ve.wait_all(v)
                ve.inc(v.tensor_copy(ce_sm[:], ROW_CUM.rearrange('h (b t) -> h b t', b=B)[:, :, TL - 1]), 1)

        transpose_rows16(ROW_CUM, cum_tok)

        with nc.Block() as blk:
            @blk.vector
            def _(v):
                cebc = ce_sm[:].unsqueeze(2).broadcast_to([H, B, TL])
                ve.inc(v.tensor_tensor(
                    ROW_SP[:].rearrange("h (b t) -> h b t", b=B),
                    cebc, ROW_CUM[:].rearrange("h (b t) -> h b t", b=B),
                    op=ALU.subtract), 1)

        with nc.Block() as blk:
            @blk.scalar
            def _(s):
                se.inc(s.activation(ROW_SP[:], ROW_SP[:], AF.Exp), 1)

        transpose_rows16(ROW_SP, w_tok)
        base_pe2 = pe.val
        with nc.Block() as blk:
            @blk.tensor
            def _(te):
                pe.inc(te.transpose(psB[0:8, 0:16], ce_sm[:], id_f[0:16, 0:16]), 1)

            @blk.vector
            def _(v):
                pe.wait(v, base_pe2 + 1)
                v.tensor_copy(ceT_sm[:], psB[0:8, 0:16])

        with nc.Block() as blk:
            @blk.scalar
            def _(s):
                s.activation(ROW_DTT[:], ROW_CUM[:], AF.Exp)

        with nc.Block() as blk:
            @blk.vector
            def _(v):
                v.tensor_mul(ROW_DTT[:], ROW_DTT[:], ROW_RNQ[:])

        with nc.Block() as blk:
            @blk.sync
            def _(sync):
                hw.inc(sync.dma_start(cum_d[:], ROW_CUM[:]))
                hw.inc(sync.dma_start(rqec_d[:], ROW_DTT[:]))
                hw.wait_all(sync)

    if BUILD_UPTO >= 2:
        with nc.Block() as blk:
            @blk.gpsimd
            def _(g):
                gp.wait_all(g)
        wproj_fm(W_A, qf_d[:].rearrange("p a n -> p (a n)"), rq_tok, True)
        wproj_fm(W_B, kf_d[:].rearrange("p a n -> p (a n)"), rk_tok, False)
        vproj_tok(W_C, vt_d[:].rearrange("p a n -> p (a n)"))
        dt_cum_chain()

        for nm, src in (("q_raw", qf_d), ("k_raw", kf_d), ("v_tok", vt_d)):
            tt = tap(nm, (128, NFC, N), BF16)
            if tt is not None:
                with nc.Block() as blk:
                    @blk.sync
                    def _(sync):
                        hw.inc(sync.dma_start(
                            tt[:].rearrange("p a n -> p (a n)"),
                            src[:].rearrange("p a n -> p (a n)")))
                        hw.wait_all(sync)
        tt = tap("rows", (16, 4 * N))
        if tt is not None:
            with nc.Block() as blk:
                @blk.sync
                def _(sync):
                    hw.inc(sync.dma_start(tt[0:16, 0:N], ROW_LA[:]))
                    hw.inc(sync.dma_start(tt[0:16, N:2 * N], ROW_CUM[:]))
                    hw.inc(sync.dma_start(tt[0:16, 2 * N:3 * N], ROW_RNQ[:]))
                    hw.inc(sync.dma_start(tt[0:16, 3 * N:4 * N], ROW_RN[:]))
                    hw.wait_all(sync)
        tt = tap("wdbg", (16, N + B))
        if tt is not None:
            with nc.Block() as blk:
                @blk.sync
                def _(sync):
                    hw.inc(sync.dma_start(tt[0:16, 0:N], ROW_SP[:]))
                    hw.inc(sync.dma_start(tt[0:16, N:N + B], ce_sm[:]))
                    hw.wait_all(sync)
        tt = tap("toks", (128, 4 * NCH * H))
        if tt is not None:
            with nc.Block() as blk:
                @blk.vector
                def _(v):
                    v.tensor_copy(xf32[:, 0:NCH * H], rq_tok[:])
                    v.tensor_copy(xf32[:, NCH * H:2 * NCH * H], rk_tok[:])
                    v.tensor_copy(xf32[:, 2 * NCH * H:3 * NCH * H], cum_tok[:])
                    ve.inc(v.tensor_copy(xf32[:, 3 * NCH * H:4 * NCH * H], w_tok[:]), 1)
            with nc.Block() as blk:
                @blk.sync
                def _(sync):
                    hw.inc(sync.dma_start(tt[:], xf32[:, 0:4 * NCH * H]))
                    hw.wait_all(sync)

    QB = seg(32768, 2048).rearrange("p (a t) -> p a t", a=NFC)
    KB = seg(34816, 2048).rearrange("p (a t) -> p a t", a=NFC)
    VB = seg(36864, 2048).rearrange("p (th f) -> p th f", th=2)
    KTB = seg(38912, 2048).rearrange("p (th f) -> p th f", th=2)
    SCT = [seg(40960, 2048), seg(43008, 2048), seg(45056, 2048)]
    DBC = seg(47104, 4096, F32)
    DBC2 = seg(83968, 4096, F32)
    DSUB = seg(51200, 4096, F32)
    S_C = seg(55296, 8192, F32).rearrange("p (b f) -> p b f", b=B)
    S_IN = seg(63488, 8192, F32)
    S_INB = seg(71680, 4096).rearrange("p (b f) -> p b f", b=B)
    S_STG = seg(75776, 8192, F32)
    WQB = seg(88064, 4096, F32)

    def hslice(ap3, h, fc_axis_len=None):
        par = h % 2
        return ap3[par * 64:(par + 1) * 64, h // 2, :]

    def l0_attention():
        for b in range(B):
            base_ha = [hwa.val, hwb.val]
            base_hs = hs.val
            base_pe = pe.val
            base_ve = ve.val
            base_se = se.val
            with nc.Block() as blk:
                @blk.sync
                def _(sync):
                    hwa.inc(sync.dma_start(QB, qf_d[:, :, b * TL:(b + 1) * TL]))
                    hwa.inc(sync.dma_start(KB, kf_d[:, :, b * TL:(b + 1) * TL]))
                    hwa.inc(sync.dma_start(VB, vt_d[:, 2 * b:2 * b + 2, :]))

                @blk.sync
                def _(sync):
                    for c, dst in ((0, DBC), (1, DBC2)):
                        sl = cum_d[:, b * TL + c * 128:b * TL + (c + 1) * 128]
                        sl = sl.unsqueeze(0).broadcast_to([128, H, 128])
                        hs.inc(sync.dma_start(
                            dst[:].rearrange("p (h t) -> p h t", h=H), sl))

                @blk.tensor
                def _(te):
                    hwa.wait(te, base_ha[0] + 32)
                    psv = psA[:].bitcast(BF16)
                    for idx in range(16):
                        th, fc = idx // 8, idx % 8
                        slot = idx % 4
                        if idx >= 4:
                            ve.wait(te, base_ve + (idx - 3))
                        pe.inc(te.transpose(psv[:, slot * 1024:slot * 1024 + 128],
                                            KB[:, fc, th * 128:(th + 1) * 128],
                                            id_b[:]), 1)

                @blk.vector
                def _(v):
                    psv = psA[:].bitcast(BF16)
                    for idx in range(16):
                        th, fc = idx // 8, idx % 8
                        slot = idx % 4
                        pe.wait(v, base_pe + idx + 1)
                        ve.inc(v.tensor_copy(KTB[:, th, fc * 128:(fc + 1) * 128],
                                             psv[:, slot * 1024:slot * 1024 + 128]), 1)
                    for th in range(2):
                        wap = w_tok[:, (2 * b + th) * H:(2 * b + th + 1) * H]
                        wap = wap.unsqueeze(2).broadcast_to([128, H, D])
                        ve.inc(v.tensor_tensor(
                            KTB[:, th, :].rearrange("p (h d) -> p h d", h=H),
                            KTB[:, th, :].rearrange("p (h d) -> p h d", h=H),
                            wap, op=ALU.mult), 1)

                if SUBPH < 1:
                    continue
                @blk.tensor
                def _(te):
                    ve.wait(te, base_ve + 18)
                    hwa.wait(te, base_ha[0] + 48)
                    for h in range(H):
                        par = h % 2
                        for th in range(2):
                            inst = te.matmul(
                                psB[par * 64:(par + 1) * 64,
                                    1536 + (h // 2) * 64:1536 + (h // 2) * 64 + 64],
                                KTB[:, th, h * D:(h + 1) * D],
                                VB[:, th, h * D:(h + 1) * D],
                                start=(th == 0), stop=(th == 1))
                    pe.inc(inst, 1)

                @blk.vector
                def _(v):
                    pe.wait(v, base_pe + 17)
                    ve.inc(v.tensor_copy(S_C[:, b, :], psB[:, 1536:2048]), 1)

                if SUBPH < 2:
                    continue
                if SUBPH >= 2.2:
                    @blk.tensor
                    def _(te):
                        for ip, (r, c) in enumerate(((0, 0), (0, 1), (1, 1))):
                            for par in range(2):
                                for h in range(par, H, 2):
                                    inst = te.matmul(
                                        psA[:, h * 128:(h + 1) * 128],
                                        KB[par * 64:(par + 1) * 64, h // 2,
                                           r * 128:(r + 1) * 128],
                                        QB[par * 64:(par + 1) * 64, h // 2,
                                           c * 128:(c + 1) * 128],
                                        start=True, stop=True)
                                pe.inc(inst, 1)
                                pe.wait_all(te)
                            if SUBPH >= 3 and ip == 0:
                                ve.wait(te, base_ve + 23)
                            elif SUBPH >= 3 and ip == 1:
                                ve.wait(te, base_ve + 26)

                if SUBPH >= 2.4:
                    @blk.vector
                    def _(v):
                        for ip, (r, c) in enumerate(((0, 0), (0, 1), (1, 1))):
                            dbc = DBC if c == 0 else DBC2
                            hs.wait(v, base_hs + (16 if c == 0 else 32))
                            cap = cum_tok[:, (2 * b + r) * H:(2 * b + r + 1) * H]
                            cap = cap.unsqueeze(2).broadcast_to([128, H, 128])
                            ve.inc(v.tensor_tensor(
                                DSUB[:].rearrange("p (h t) -> p h t", h=H),
                                dbc[:].rearrange("p (h t) -> p h t", h=H),
                                cap, op=ALU.subtract), 1)
                            ve.inc(v.tensor_scalar_min(DSUB[:], DSUB[:], 0.0), 1)
                            if SUBPH >= 2.6:
                                se.wait(v, base_se + ip + 1)
                                if r == c:
                                    tap_ = tri_b[:].unsqueeze(1).broadcast_to(
                                        [128, H, 128])
                                    ve.inc(v.tensor_tensor(
                                        SCT[ip].rearrange("p (h t) -> p h t", h=H),
                                        SCT[ip].rearrange("p (h t) -> p h t", h=H),
                                        tap_, op=ALU.min), 1)
                            else:
                                ve.inc(v.engine_nop(), 1) if r == c else None
                            if SUBPH >= 3:
                                pe.wait(v, base_pe + 17 + 2 * (ip + 1))
                                ve.inc(v.tensor_tensor(SCT[ip], SCT[ip], psA[:],
                                                       op=ALU.mult), 1)
                            else:
                                ve.inc(v.engine_nop(), 1)

                if SUBPH >= 2.6:
                    @blk.scalar
                    def _(s):
                        for ip, wv in ((0, 21), (1, 25), (2, 28)):
                            ve.wait(s, base_ve + wv)
                            se.inc(s.activation(SCT[ip], DSUB[:], AF.Exp), 1)

            if SUBPH < 3:
                continue
            base_pe2 = pe.val
            with nc.Block() as blk:
                @blk.tensor
                def _(te):
                    for h in range(H):
                        te.matmul(psB[:, h * D:(h + 1) * D],
                                  SCT[0][:, h * 128:(h + 1) * 128],
                                  VB[:, 0, h * D:(h + 1) * D],
                                  start=True, stop=True)
                    for h in range(H):
                        te.matmul(psB[:, 1024 + h * D:1024 + (h + 1) * D],
                                  SCT[1][:, h * 128:(h + 1) * 128],
                                  VB[:, 0, h * D:(h + 1) * D],
                                  start=True, stop=False)
                        inst = te.matmul(psB[:, 1024 + h * D:1024 + (h + 1) * D],
                                         SCT[2][:, h * 128:(h + 1) * 128],
                                         VB[:, 1, h * D:(h + 1) * D],
                                         start=False, stop=True)
                    pe.inc(inst, 1)

                @blk.vector
                def _(v):
                    pe.wait(v, base_pe2 + 1)
                    for th in range(2):
                        rap = rq_tok[:, (2 * b + th) * H:(2 * b + th + 1) * H]
                        rap = rap.unsqueeze(2).broadcast_to([128, H, D])
                        ve.inc(v.tensor_tensor(
                            att[:, 2 * b + th, :].rearrange("p (h d) -> p h d", h=H),
                            psB[:, th * 1024:(th + 1) * 1024].rearrange(
                                "p (h d) -> p h d", h=H),
                            rap, op=ALU.mult), 1)

        if BUILD_UPTO < 3.5:
            return
        with nc.Block() as blk:
            @blk.gpsimd
            def _(g):
                gp.inc(g.dma_start(
                    cc_in[0, 0:PAY].rearrange("(p n) -> p n", p=128),
                    S_C[:].rearrange("p b f -> p (b f)")))
                gp.inc(g.dma_start(
                    cc_in[0, PAY:PAYT].rearrange("(b h) -> b h", b=8), ceT_sm[:]))
                gp.wait_all(g)
                g.collective_compute(
                    "AllGather", ALU.bypass,
                    replica_groups=[list(range(NCORE))],
                    ins=[cc_in[:]], outs=[cc_out[:]]).then_inc(cc.sem, 1)
                cc.val += 1

    def l0_correction():
        with nc.Block() as blk:
            @blk.sync
            def _(sync):
                cc.wait_all(sync)
                for cp in range(NCORE):
                    off = cp * PAYT + PAY
                    hw.inc(sync.dma_start(
                        CE_all[:, cp * H:(cp + 1) * H],
                        cc_out[0, off:off + 8 * H].rearrange("(b h) -> b h", b=8)))
                hw.wait_all(sync)

        with nc.Block() as blk:
            @blk.vector
            def _(v):
                ve.inc(v.memset(dec_acc[:], 0.0), 1)

            for cp in range(6, -1, -1):
                @blk.scalar
                def _(s, cp=cp):
                    ve.wait_all(s)
                    se.inc(s.activation(dec_exp[:], dec_acc[:], AF.Exp), 1)

                @blk.vector
                def _(v, cp=cp):
                    se.wait_all(v)
                    ve.inc(v.tensor_scalar_mul(dec_exp[:], dec_exp[:],
                                               pmask[:, cp:cp + 1]), 1)
                    ve.wait_all(v)

                @blk.sync
                def _(sync, cp=cp):
                    ve.wait_all(sync)
                    hw.inc(sync.dma_start(
                        dec_dram[cp, :].rearrange("(b h) -> b h", b=8), dec_exp[:]))

                @blk.vector
                def _(v, cp=cp):
                    if cp > 0:
                        hw.wait_all(v)
                        ve.inc(v.tensor_scalar_mul(dec_exp[:],
                                                   CE_all[:, cp * H:(cp + 1) * H],
                                                   pmask[:, cp:cp + 1]), 1)
                        ve.wait_all(v)
                        ve.inc(v.tensor_add(dec_acc[:], dec_acc[:], dec_exp[:]), 1)
                        ve.wait_all(v)

        if SUBPH < 4.2:
            return
        with nc.Block() as blk:
            @blk.vector
            def _(v):
                ve.inc(v.memset(S_IN[:], 0.0), 1)

        for cp in range(7):
            base_ha = hwa.val
            base_hs = hs.val
            with nc.Block() as blk:
                @blk.sync
                def _(sync):
                    off = cp * PAYT
                    hwa.inc(sync.dma_start(
                        S_STG[:], cc_out[0, off:off + PAY].rearrange(
                            "(p n) -> p n", p=128)))

                @blk.scalar
                def _(s):
                    dv = dec_dram[cp, :].rearrange("(b fc2 par) -> b fc2 par", fc2=8, par=2)
                    for par in range(2):
                        sl = dv[:, :, par].unsqueeze(0).broadcast_to([64, 8, 8])
                        hs.inc(s.dma_start(
                            dec_bcv[par * 64:(par + 1) * 64, :].rearrange(
                                "p (b f) -> p b f", b=8), sl,
                            allow_slow_non_contiguous=True))

                @blk.vector
                def _(v):
                    hwa.wait(v, base_ha + 16)
                    hs.wait(v, base_hs + 32)
                    for half in range(2):
                        dap = dec_bcv[:, half * 32:(half + 1) * 32]
                        dap = dap.rearrange("p (b f) -> p b f", b=4)
                        dap = dap.unsqueeze(3).broadcast_to([128, 4, 8, 64])
                        sv = S_STG[:, half * 2048:(half + 1) * 2048].rearrange(
                            "p (b f d) -> p b f d", b=4, f=8)
                        dst = DSUB if half else DBC
                        ve.inc(v.tensor_tensor(
                            dst[:].rearrange("p (b f d) -> p b f d", b=4, f=8),
                            sv, dap, op=ALU.mult), 1)
                    ve.wait_all(v)
                    ve.inc(v.tensor_add(S_IN[:, 0:2048], S_IN[:, 0:2048], DBC[:]), 1)
                    ve.inc(v.tensor_add(S_IN[:, 2048:4096], S_IN[:, 2048:4096],
                                        DSUB[:]), 1)

        if SUBPH < 4.3:
            return
        with nc.Block() as blk:
            @blk.vector
            def _(v):
                ve.inc(v.tensor_copy(S_INB[:].rearrange("p b f -> p (b f)"),
                                     S_IN[:]), 1)

        if SUBPH < 4.4:
            return
        for b in range(B):
            base_ha = [hwa.val, hwb.val]
            base_hs = hs.val
            base_pe = pe.val
            base_ve = ve.val
            with nc.Block() as blk:
                @blk.sync
                def _(sync):
                    hwa.inc(sync.dma_start(QB, qf_d[:, :, b * TL:(b + 1) * TL]))

                @blk.sync
                def _(sync):
                    src = rqec_d[:].rearrange("(fc par) n -> par fc n", par=2)
                    for par in range(2):
                        sl = src[par, :, b * TL:(b + 1) * TL]
                        sl = sl.unsqueeze(0).broadcast_to([64, NFC, TL])
                        hs.inc(sync.dma_start(
                            WQB[par * 64:(par + 1) * 64, :].rearrange(
                                "p (a t) -> p a t", a=NFC), sl))

                @blk.vector
                def _(v):
                    hwa.wait(v, base_ha[0] + 16)
                    hs.wait(v, base_hs + 32)
                    ve.inc(v.tensor_tensor(
                        KB[:].rearrange("p a t -> p (a t)"),
                        QB[:].rearrange("p a t -> p (a t)"),
                        WQB[:], op=ALU.mult), 1)

                @blk.tensor
                def _(te):
                    if SUBPH < 4.6:
                        return
                    ve.wait(te, base_ve + 1)
                    for th in range(2):
                        for par in range(2):
                            for h in range(par, H, 2):
                                inst = te.matmul(
                                    psA[:, th * 1024 + h * D:th * 1024 + (h + 1) * D],
                                    KB[par * 64:(par + 1) * 64, h // 2,
                                       th * 128:(th + 1) * 128],
                                    S_INB[par * 64:(par + 1) * 64, b,
                                          (h // 2) * 64:(h // 2) * 64 + 64],
                                    start=True, stop=True)
                            pe.inc(inst, 1)
                            pe.wait_all(te)

                @blk.vector
                def _(v):
                    if SUBPH < 4.7:
                        return
                    for th in range(2):
                        pe.wait(v, base_pe + 2 * (th + 1))
                        ve.inc(v.tensor_tensor(
                            att[:, 2 * b + th, :], att[:, 2 * b + th, :],
                            psA[:, th * 1024:(th + 1) * 1024], op=ALU.add), 1)

    if BUILD_UPTO >= 3:
        l0_attention()
    if BUILD_UPTO >= 4:
        l0_correction()
        tt = tap("att", (128, NCH * E), BF16)
        if tt is not None:
            with nc.Block() as blk:
                @blk.sync
                def _(sync):
                    hw.inc(sync.dma_start(tt[:], seg(U, U)))
                    hw.wait_all(sync)
        tt = tap("pmdbg", (8, 8 + H))
        if tt is not None:
            with nc.Block() as blk:
                @blk.sync
                def _(sync):
                    hw.inc(sync.dma_start(tt[0:8, 0:8], pmask[:]))
                    hw.inc(sync.dma_start(tt[0:8, 8:8 + H], dec_acc[:]))
                    hw.wait_all(sync)
        tt = tap("decd", (8, NCORE * H))
        if tt is not None:
            with nc.Block() as blk:
                @blk.sync
                def _(sync):
                    for cp in range(NCORE):
                        hw.inc(sync.dma_start(
                            xf32[0:8, cp * H:(cp + 1) * H],
                            dec_dram[cp, :].rearrange("(b h) -> b h", b=8)))
                    hw.wait_all(sync)
            with nc.Block() as blk:
                @blk.sync
                def _(sync):
                    hw.inc(sync.dma_start(tt[:], xf32[0:8, 0:NCORE * H]))
                    hw.wait_all(sync)
        tt = tap("ce2", (8, NCORE * H))
        if tt is not None:
            with nc.Block() as blk:
                @blk.sync
                def _(sync):
                    hw.inc(sync.dma_start(tt[:], CE_all[:]))
                    hw.wait_all(sync)
        tt = tap("S_in", (128, B * 512))
        if tt is not None:
            with nc.Block() as blk:
                @blk.sync
                def _(sync):
                    hw.inc(sync.dma_start(tt[:], S_IN[:]))
                    hw.wait_all(sync)
        tt = tap("S_c", (128, B * 512))
        if tt is not None:
            with nc.Block() as blk:
                @blk.sync
                def _(sync):
                    hw.inc(sync.dma_start(tt[:], S_C[:].rearrange("p b f -> p (b f)")))
                    hw.wait_all(sync)

    GSC = seg(49152, 2048)
    FMT = seg(51200, 1024)
    s0_tok = att

    def l0_outproj():
        with nc.Block() as blk:
            @blk.gpsimd
            def _(g):
                gp.inc(g.dma_start(W_A, Wg_in[:].rearrange("(kc p) o -> p kc o", p=128)))
                gp.inc(g.dma_start(W_B, Wo_in[:].rearrange("(kc p) o -> p kc o", p=128)))

        base_ve = ve.val
        with nc.Block() as blk:
            @blk.vector
            def _(v):
                for k in range(NCH):
                    sqv = xf32[:, (k % 2) * 1024:(k % 2 + 1) * 1024]
                    ve.inc(v.tensor_tensor(sqv, att[:, k, :], att[:, k, :],
                                           op=ALU.mult), 1)
                    ve.inc(v.reduce_sum(
                        rr_tok[:, k * H:(k + 1) * H],
                        sqv.rearrange("p (h d) -> p h d", h=H),
                        axis=mybir.AxisListType.X), 1)

        with nc.Block() as blk:
            @blk.scalar
            def _(s):
                se.inc(s.activation(rr_tok[:], rr_tok[:], AF.Sqrt,
                                    scale=1.0 / D, bias=eps_rms[:, 0:1]), 1)

        with nc.Block() as blk:
            @blk.vector
            def _(v):
                ve.inc(v.reciprocal(rr_tok[:], rr_tok[:]), 1)
                ve.wait_all(v)
                for k in range(NCH):
                    rap = rr_tok[:, k * H:(k + 1) * H].unsqueeze(2)
                    rap = rap.broadcast_to([128, H, D])
                    av = att[:, k, :].rearrange("p (h d) -> p h d", h=H)
                    ve.inc(v.tensor_tensor(av, av, rap, op=ALU.mult), 1)
                    wap = rw_bc[:].unsqueeze(1).broadcast_to([128, H, D])
                    ve.inc(v.tensor_tensor(av, av, wap, op=ALU.mult), 1)

        ttn = tap("attn_n", (128, NCH * E), BF16)
        if ttn is not None:
            with nc.Block() as blk:
                @blk.sync
                def _(sync):
                    hw.inc(sync.dma_start(ttn[:], seg(U, U)))
                    hw.wait_all(sync)

        for k in range(NCH):
            base_pe = pe.val
            base_ve = ve.val
            base_se = se.val
            with nc.Block() as blk:
                @blk.tensor
                def _(te):
                    gp.wait_all(te)
                    for kc in range(NFC):
                        for no in range(2):
                            inst = te.matmul(psA[:, no * 512:(no + 1) * 512],
                                             xn_fm[:, kc, k * 128:(k + 1) * 128],
                                             W_A[:, kc, no * 512:(no + 1) * 512],
                                             start=(kc == 0), stop=(kc == NFC - 1))
                    pe.inc(inst, 1)

                @blk.scalar
                def _(s):
                    pe.wait(s, base_pe + 1)
                    se.inc(s.activation(GSC[:, 0:1024], psA[:, 0:1024], AF.Silu), 1)

                @blk.vector
                def _(v):
                    se.wait(v, base_se + 1)
                    ve.inc(v.tensor_tensor(GSC[:, 1024:2048], att[:, k, :],
                                           GSC[:, 0:1024], op=ALU.mult), 1)

                def jslot(fc):
                    pA = psA[:].bitcast(BF16)
                    pB = psB[:].bitcast(BF16)
                    return [pA[:, 2048:2176], pA[:, 3072:3200],
                            pB[:, 2048:2176], pB[:, 3072:3200]][fc % 4]

                @blk.tensor
                def _(te):
                    ve.wait(te, base_ve + 1)
                    for fc in range(NFC):
                        if fc >= 4:
                            ve.wait(te, base_ve + 1 + (fc - 3))
                        pe.inc(te.transpose(
                            jslot(fc),
                            GSC[:, 1024 + fc * 128:1024 + (fc + 1) * 128],
                            id_b[:]), 1)

                @blk.vector
                def _(v):
                    for fc in range(NFC):
                        pe.wait(v, base_pe + 2 + fc)
                        ve.inc(v.tensor_copy(
                            FMT[:, fc * 128:(fc + 1) * 128], jslot(fc)), 1)

                @blk.tensor
                def _(te):
                    ve.wait(te, base_ve + 9)
                    for kc in range(NFC):
                        for no in range(2):
                            inst = te.matmul(psB[:, no * 512:(no + 1) * 512],
                                             FMT[:, kc * 128:(kc + 1) * 128],
                                             W_B[:, kc, no * 512:(no + 1) * 512],
                                             start=(kc == 0), stop=(kc == NFC - 1))
                    pe.inc(inst, 1)

                @blk.vector
                def _(v):
                    pe.wait(v, base_pe + 10)
                    ve.inc(v.tensor_copy(s0_tok[:, k, :], psB[:, 0:1024]), 1)

            if k == 0:
                tgd = tap("gdbg", (128, 3072), BF16)
                if tgd is not None:
                    with nc.Block() as blk:
                        @blk.sync
                        def _(sync):
                            hw.inc(sync.dma_start(tgd[:, 0:2048], GSC[:]))
                            hw.inc(sync.dma_start(tgd[:, 2048:3072], FMT[:]))
                            hw.wait_all(sync)
                tpb = tap("psb0", (128, 1024))
                if tpb is not None:
                    with nc.Block() as blk:
                        @blk.vector
                        def _(v):
                            ve.inc(v.tensor_copy(xf32[:, 0:1024], psB[:, 0:1024]), 1)
                    with nc.Block() as blk:
                        @blk.sync
                        def _(sync):
                            hw.inc(sync.dma_start(tpb[:], xf32[:, 0:1024]))
                            hw.wait_all(sync)

    gk_sb = sb("gk_sb", [128, NFC * H], BF16)
    C_FM = seg(88064, 2048)[0:16, :]

    def vchain(v, inst):
        ve.inc(inst, 1)
        ve.wait_all(v)

    def moe_routing():
        with nc.Block() as blk:
            @blk.gpsimd
            def _(g):
                gp.inc(g.dma_start(gk_sb[:].rearrange("p (kc h) -> p kc h", kc=NFC),
                                   gk_in[:].rearrange("(kc p) h -> p kc h", p=128)))

        base_pe = pe.val
        base_ve0 = ve.val
        with nc.Block() as blk:
            @blk.tensor
            def _(te):
                gp.wait_all(te)
                for k in range(NCH):
                    if k >= 4:
                        ve.wait(te, base_ve0 + (k - 3))
                    for kc in range(NFC):
                        inst = te.matmul(psB[:, (k % 4) * 512:(k % 4) * 512 + H],
                                         xn_fm[:, kc, k * 128:(k + 1) * 128],
                                         gk_sb[:, kc * H:(kc + 1) * H],
                                         start=(kc == 0), stop=(kc == NFC - 1))
                    pe.inc(inst, 1)

            @blk.vector
            def _(v):
                for k in range(NCH):
                    pe.wait(v, base_pe + k + 1)
                    ve.inc(v.tensor_copy(rt_sc[:, k * H:(k + 1) * H],
                                         psB[:, (k % 4) * 512:(k % 4) * 512 + H]), 1)

        with nc.Block() as blk:
            @blk.scalar
            def _(s):
                se.inc(s.activation(rt_sc[:], rt_sc[:], AF.Sigmoid), 1)

        with nc.Block() as blk:
            @blk.vector
            def _(v):
                gv = gb_bc[:].unsqueeze(1).broadcast_to([128, NCH, H])
                sv = rt_sc[:].rearrange("p (k h) -> p k h", k=NCH)
                vchain(v, v.tensor_tensor(sv, sv, gv, op=ALU.add))
                pairs = [(0, 1), (0, 2), (0, 3), (1, 2), (1, 3), (2, 3)]
                s4 = rt_sc[:].rearrange("p (k g e) -> p k g e", k=NCH, g=4)
                first = True
                for i, j in pairs:
                    dst = rt_g2[:].rearrange("p (k g) -> p k g", k=NCH)
                    vchain(v, v.tensor_tensor(dst, s4[:, :, :, i], s4[:, :, :, j],
                                              op=ALU.add))
                    g1v = rt_g1[:].rearrange("p (k g) -> p k g", k=NCH)
                    if first:
                        vchain(v, v.tensor_copy(g1v, dst))
                        first = False
                    else:
                        vchain(v, v.tensor_tensor(g1v, g1v, dst, op=ALU.max))
                g1v = rt_g1[:].rearrange("p (k g) -> p k g", k=NCH)
                first = True
                for i, j in pairs:
                    dst = rt_g2[:].rearrange("p (k g) -> p k g", k=NCH)[:, :, 0:1]
                    tmp = rt_cnt[:, 0:NCH].rearrange("p (k o) -> p k o", k=NCH)
                    vchain(v, v.tensor_tensor(tmp, g1v[:, :, i:i + 1],
                                              g1v[:, :, j:j + 1], op=ALU.min))
                    if first:
                        vchain(v, v.tensor_copy(rt_w[:].unsqueeze(2), tmp))
                        first = False
                    else:
                        vchain(v, v.tensor_tensor(rt_w[:].unsqueeze(2),
                                                  rt_w[:].unsqueeze(2), tmp,
                                                  op=ALU.max))
                vchain(v, v.tensor_copy(rt_cmb[:], rt_g1[:]))
                gm2b = rt_w[:].unsqueeze(2).broadcast_to([128, NCH, 4])
                vchain(v, v.tensor_tensor(g1v, g1v, gm2b, op=ALU.is_ge))
                gmb = rt_g1[:].rearrange("p (k g) -> p k g", k=NCH)
                gmb = gmb.unsqueeze(3).broadcast_to([128, NCH, 4, 4])
                msv = rt_ms[:].rearrange("p (k g e) -> p k g e", k=NCH, g=4)
                vchain(v, v.tensor_tensor(msv, s4, gmb, op=ALU.mult))
                vchain(v, v.memset(rt_cnt[:], 0.0))
                mall = rt_ms[:].rearrange("p (k e) -> p k e", k=NCH)
                call = rt_cnt[:].rearrange("p (k e) -> p k e", k=NCH)
                for j in range(H):
                    mj = mall[:, :, j:j + 1].broadcast_to([128, NCH, H])
                    g2 = rt_g2[:].rearrange("p (k g) -> p k g", k=NCH)
                    tmp16 = rt_sc2[:].rearrange("p (k e) -> p k e", k=NCH)
                    vchain(v, v.tensor_tensor(tmp16, mj, mall, op=ALU.is_gt))
                    vchain(v, v.tensor_tensor(call, call, tmp16, op=ALU.add))
                vchain(v, v.tensor_single_scalar(call, call, 3.5, op=ALU.is_le))
                vchain(v, v.tensor_tensor(mall, mall, call, op=ALU.mult))
                vchain(v, v.reduce_sum(rt_w[:], mall, axis=mybir.AxisListType.X))
                vchain(v, v.tensor_scalar_add(rt_w[:], rt_w[:], 1e-20))
                vchain(v, v.reciprocal(rt_w[:], rt_w[:]))
                vchain(v, v.tensor_scalar_mul(rt_w[:], rt_w[:], 2.5))
                wb = rt_w[:].unsqueeze(2).broadcast_to([128, NCH, H])
                vchain(v, v.tensor_tensor(mall, mall, wb, op=ALU.mult))

        base_pe2 = pe.val
        base_ve2 = ve.val
        with nc.Block() as blk:
            @blk.tensor
            def _(te):
                for k in range(NCH):
                    if k >= 4:
                        ve.wait(te, base_ve2 + (k - 3))
                    pe.inc(te.transpose(psB[0:16, (k % 4) * 512:(k % 4) * 512 + 128],
                                        rt_ms[:, k * H:(k + 1) * H], id_f[:]), 1)

            @blk.vector
            def _(v):
                for k in range(NCH):
                    pe.wait(v, base_pe2 + k + 1)
                    ve.inc(v.tensor_copy(C_FM[:, k * 128:(k + 1) * 128],
                                         psB[0:16, (k % 4) * 512:(k % 4) * 512 + 128]), 1)

        with nc.Block() as blk:
            @blk.sync
            def _(sync):
                hw.inc(sync.dma_start(c_d[:], C_FM[:]))
                hw.wait_all(sync)

    WG = seg(65536, 4096).rearrange("p (kc m) -> p kc m", kc=NFC)
    WU = seg(69632, 4096).rearrange("p (kc m) -> p kc m", kc=NFC)
    WD = seg(73728, 4096).rearrange("p (kc m) -> p kc m", kc=4)
    INTER = seg(77824, 8192).rearrange("p (mc n) -> p mc n", mc=4)
    CBC = seg(86016, 2048)
    UCS = seg(90112, 2048)
    SIL = seg(92160, 2048)
    Y = seg(32768, 32768, F32).rearrange("p (mo n) -> p mo n", mo=NFC)

    DBG_E0MC0 = "gu0" in DEBUG_TAPS

    def moe_expert(e, wg_src, wu_src, wd_src, csrc, accumulate, nmc=4):
        with nc.Block() as blk:
            @blk.gpsimd
            def _(g):
                gp.inc(g.dma_start(WG, wg_src))
                gp.inc(g.dma_start(WU, wu_src))
                gp.inc(g.dma_start(WD, wd_src))

            @blk.sync
            def _(sync):
                if csrc is not None:
                    hw.inc(sync.dma_start(CBC[:], csrc.broadcast_to([128, N])))
        cbc_hw = hw.val

        for mc in range(nmc):
            base_pe = pe.val
            base_se = se.val
            base_ve = ve.val
            with nc.Block() as blk:
                @blk.tensor
                def _(te):
                    gp.wait_all(te)
                    for kc in range(NFC):
                        for Tt in range(4):
                            inst = te.matmul(psA[:, Tt * 512:(Tt + 1) * 512],
                                             WG[:, kc, mc * 128:(mc + 1) * 128],
                                             xn_fm[:, kc, Tt * 512:(Tt + 1) * 512],
                                             start=(kc == 0), stop=(kc == NFC - 1))
                    pe.inc(inst, 1)
                    for kc in range(NFC):
                        for Tt in range(4):
                            inst = te.matmul(psB[:, Tt * 512:(Tt + 1) * 512],
                                             WU[:, kc, mc * 128:(mc + 1) * 128],
                                             xn_fm[:, kc, Tt * 512:(Tt + 1) * 512],
                                             start=(kc == 0), stop=(kc == NFC - 1))
                    pe.inc(inst, 1)

                @blk.scalar
                def _(s):
                    pe.wait(s, base_pe + 1)
                    for Tt in range(4):
                        se.inc(s.activation(SIL[:, Tt * 512:(Tt + 1) * 512],
                                            psA[:, Tt * 512:(Tt + 1) * 512],
                                            AF.Silu), 1)

                @blk.vector
                def _(v):
                    pe.wait(v, base_pe + 2)
                    if csrc is not None:
                        hw.wait(v, cbc_hw)
                        ve.inc(v.tensor_tensor(UCS[:], psB[:], CBC[:],
                                               op=ALU.mult), 1)
                    else:
                        ve.inc(v.tensor_copy(UCS[:], psB[:]), 1)
                    se.wait(v, base_se + 4)
                    ve.inc(v.tensor_tensor(INTER[:, mc, :], SIL[:], UCS[:],
                                           op=ALU.mult), 1)
                    if DBG_E0MC0 and e == 0 and mc == 0:
                        ve.inc(v.tensor_copy(ln_sq[:, 0:512], psA[:, 1024:1536]), 1)
                        ve.inc(v.tensor_copy(ln_sq[:, 1024:1536], psB[:, 1024:1536]), 1)

        for mo in range(NFC):
            base_pe = pe.val
            with nc.Block() as blk:
                @blk.tensor
                def _(te):
                    for mc in range(nmc):
                        for Tt in range(4):
                            inst = te.matmul(psA[:, Tt * 512:(Tt + 1) * 512],
                                             WD[:, mc, mo * 128:(mo + 1) * 128],
                                             INTER[:, mc, Tt * 512:(Tt + 1) * 512],
                                             start=(mc == 0), stop=(mc == nmc - 1))
                    pe.inc(inst, 1)

                @blk.vector
                def _(v):
                    pe.wait(v, base_pe + 1)
                    for Tt in range(4):
                        if accumulate:
                            ve.inc(v.tensor_tensor(
                                Y[:, mo, Tt * 512:(Tt + 1) * 512],
                                Y[:, mo, Tt * 512:(Tt + 1) * 512],
                                psA[:, Tt * 512:(Tt + 1) * 512], op=ALU.add), 1)
                        else:
                            ve.inc(v.tensor_copy(
                                Y[:, mo, Tt * 512:(Tt + 1) * 512],
                                psA[:, Tt * 512:(Tt + 1) * 512]), 1)

    def moe_phase():
        for e in range(16):
            moe_expert(
                e,
                wgate_in[e].rearrange("(kc p) m -> p kc m", p=128),
                wup_in[e].rearrange("(kc p) m -> p kc m", p=128),
                wdown_in[e].rearrange("(kc p) m -> p kc m", p=128),
                c_d[e:e + 1, :],
                accumulate=(e > 0))
            if e == 0:
                tmi = tap("mi0", (128, 3 * 2048), BF16)
                if tmi is not None:
                    with nc.Block() as blk:
                        @blk.sync
                        def _(sync):
                            hw.inc(sync.dma_start(tmi[:, 0:2048], CBC[:]))
                            hw.inc(sync.dma_start(tmi[:, 2048:4096], SIL[:]))
                            hw.inc(sync.dma_start(tmi[:, 4096:6144], UCS[:]))
                            hw.wait_all(sync)
                tgu = tap("gu0", (128, 2048))
                if tgu is not None:
                    with nc.Block() as blk:
                        @blk.sync
                        def _(sync):
                            hw.inc(sync.dma_start(tgu[:], ln_sq[:]))
                            hw.wait_all(sync)
                te0 = tap("e0", (128, 4 * N), BF16)
                ty0 = tap("y0", (128, NFC * N))
                if te0 is not None:
                    with nc.Block() as blk:
                        @blk.sync
                        def _(sync):
                            hw.inc(sync.dma_start(te0[:], seg(77824, 8192)))
                            hw.inc(sync.dma_start(ty0[:], seg(32768, 32768, F32)))
                            hw.wait_all(sync)
        for half in range(2):
            moe_expert(
                100 + half,
                swg_in[:, half * 512:(half + 1) * 512].rearrange(
                    "(kc p) m -> p kc m", p=128),
                swu_in[:, half * 512:(half + 1) * 512].rearrange(
                    "(kc p) m -> p kc m", p=128),
                swd_in[half * 512:(half + 1) * 512, :].rearrange(
                    "(kc p) m -> p kc m", p=128),
                None, accumulate=True)

        ty = tap("ymoe", (128, NFC * N))
        if ty is not None:
            with nc.Block() as blk:
                @blk.sync
                def _(sync):
                    hw.inc(sync.dma_start(ty[:], seg(32768, 32768, F32)))
                    hw.wait_all(sync)

        G0FM = seg(65536, U).rearrange("p (fc n) -> p fc n", fc=NFC)
        with nc.Block() as blk:
            @blk.scalar
            def _(s):
                for mo in range(NFC):
                    for Tt in range(4):
                        se.inc(s.activation(
                            G0FM[:, mo, Tt * 512:(Tt + 1) * 512],
                            Y[:, mo, Tt * 512:(Tt + 1) * 512], AF.Silu), 1)

        G0TOK = seg(32768, U).rearrange("p (k f) -> p k f", k=NCH)
        transpose_tok2fm_inv(G0FM, G0TOK)

    def transpose_tok2fm_inv(src_fm, dst_tok):
        base_pe = pe.val
        base_ve = ve.val
        psv = psA[:].bitcast(BF16)
        with nc.Block() as blk:
            @blk.tensor
            def _(te):
                for idx in range(NCH * NFC):
                    k, fc = idx // NFC, idx % NFC
                    slot = idx % 4
                    if idx >= 4:
                        ve.wait(te, base_ve + (idx - 3))
                    pe.inc(te.transpose(psv[:, slot * 1024:slot * 1024 + 128],
                                        src_fm[:, fc, k * 128:(k + 1) * 128],
                                        id_b[:]), 1)

            @blk.vector
            def _(v):
                for idx in range(NCH * NFC):
                    k, fc = idx // NFC, idx % NFC
                    slot = idx % 4
                    pe.wait(v, base_pe + idx + 1)
                    ve.inc(v.tensor_copy(dst_tok[:, k, fc * 128:(fc + 1) * 128],
                                         psv[:, slot * 1024:slot * 1024 + 128]), 1)

    def residual_x1():
        G0TOK = seg(32768, U).rearrange("p (k f) -> p k f", k=NCH)
        X1 = seg(65536, U).rearrange("p (k f) -> p k f", k=NCH)
        base_ha = [hwa.val, hwb.val]
        base_ve = ve.val
        with nc.Block() as blk:
            @blk.sync
            def _(sync):
                for k in range(NCH):
                    b, th = k // 2, k % 2
                    sl = [hwa, hwb][k % 2]
                    if k >= 2:
                        ve.wait(sync, base_ve + 2 * (k - 1))
                    sl.inc(sync.dma_start(
                        xf32[:, (k % 2) * 1024:(k % 2 + 1) * 1024],
                        x_in[b, th * 128:(th + 1) * 128, :]))

            @blk.vector
            def _(v):
                for k in range(NCH):
                    [hwa, hwb][k % 2].wait(v, base_ha[k % 2] + (k // 2 + 1) * 16)
                    ve.inc(v.tensor_tensor(GSC[:, 0:1024], s0_tok[:, k, :],
                                           G0TOK[:, k, :], op=ALU.mult), 1)
                    ve.inc(v.tensor_tensor(
                        X1[:, k, :], xf32[:, (k % 2) * 1024:(k % 2 + 1) * 1024],
                        GSC[:, 0:1024], op=ALU.add), 1)

    if BUILD_UPTO >= 5:
        tt = tap("att_raw", (128, NCH * E), BF16)
        if tt is not None:
            with nc.Block() as blk:
                @blk.sync
                def _(sync):
                    hw.inc(sync.dma_start(tt[:], seg(U, U)))
                    hw.wait_all(sync)
        l0_outproj()
        if BUILD_UPTO >= 5.5:
            moe_routing()
        tt = tap("s0", (128, NCH * E), BF16)
        if tt is not None:
            with nc.Block() as blk:
                @blk.sync
                def _(sync):
                    hw.inc(sync.dma_start(tt[:], seg(U, U)))
                    hw.wait_all(sync)
        tt = tap("scb", (128, NCH * H))
        if tt is not None:
            with nc.Block() as blk:
                @blk.sync
                def _(sync):
                    hw.inc(sync.dma_start(tt[:], rt_cnt[:]))
                    hw.wait_all(sync)
        tt = tap("rdbg", (128, NCH * H + NCH * 4 + NCH * 4))
        if tt is not None:
            with nc.Block() as blk:
                @blk.sync
                def _(sync):
                    hw.inc(sync.dma_start(tt[:, 0:NCH * H], rt_sc[:]))
                    hw.inc(sync.dma_start(tt[:, NCH * H:NCH * H + NCH * 4], rt_g1[:]))
                    hw.inc(sync.dma_start(tt[:, NCH * H + NCH * 4:], rt_cmb[:]))
                    hw.wait_all(sync)
        tt = tap("cmb", (128, NCH * H))
        if tt is not None:
            with nc.Block() as blk:
                @blk.sync
                def _(sync):
                    hw.inc(sync.dma_start(tt[:], rt_ms[:]))
                    hw.wait_all(sync)
    if BUILD_UPTO >= 6:
        moe_phase()
        residual_x1()
        tt = tap("x1", (128, NCH * E), BF16)
        if tt is not None:
            with nc.Block() as blk:
                @blk.sync
                def _(sync):
                    hw.inc(sync.dma_start(
                        tt[:], seg(65536, U)))
                    hw.wait_all(sync)

    X1 = seg(65536, U).rearrange("p (k f) -> p k f", k=NCH)
    XT1 = seg(U, U).rearrange("p (k f) -> p k f", k=NCH)
    QKV = seg(U, 49152).rearrange("p (k f) -> p k f", k=NCH)
    W3RD = seg(81920, 8192).rearrange("p (kc o) -> p kc o", kc=NFC)
    SCR_F = seg(94208, 2048, F32)
    AV_T = seg(93184, 1024)
    T0_B = seg(94208, 1024)
    BQ_BC = seg(90112, 3072)
    BO_BC = seg(90112, 1024)
    scpl = seg(81920, 256, F32)
    wpl = seg(82176, 128)
    smx = seg(82304, 32, F32)
    ssum = seg(82336, 32, F32)
    pooledt = seg(96256, 2048, F32)
    pooledf = seg(94208, 2048)[0:64, :]
    disha_sb = seg(51200, 1024)[0:64, :]

    def ln1_phase():
        with nc.Block() as blk:
            @blk.vector
            def _(v):
                for k in range(NCH):
                    ve.inc(v.reduce_sum(stat_sum[:, k:k + 1], X1[:, k, :],
                                        axis=mybir.AxisListType.X), 1)

            @blk.scalar
            def _(s):
                for k in range(NCH):
                    sqv = ln_sq[:, (k % 2) * 1024:(k % 2 + 1) * 1024]
                    se.inc(s.activation(sqv, X1[:, k, :], AF.Square,
                                        accum_out=stat_sq[:, k:k + 1]), 1)

        stats_combine()

        with nc.Block() as blk:
            @blk.vector
            def _(v):
                pv = pooledt[:].rearrange("p (k d) -> p k d", k=NCH)
                for k in range(NCH):
                    ve.inc(v.tensor_scalar(XT1[:, k, :], X1[:, k, :],
                                           stat_m[:, k:k + 1], stat_r[:, k:k + 1],
                                           op0=ALU.subtract, op1=ALU.mult), 1)
                    v.reduce_sum(pv[:, k, :],
                                 XT1[:, k, :].rearrange("p (s d) -> p d s", s=16),
                                 axis=mybir.AxisListType.X)

        transpose_tok2fm(XT1, xn_fm)

    def qkv_phase():
        with nc.Block() as blk:
            @blk.gpsimd
            def _(g):
                gp.inc(g.dma_start(BQ_BC[:], bqkv_in[:].broadcast_to([128, 3 * E])))
                gp.inc(g.dma_start(wffT_d[:], wff_in[:]))

        for p_ in range(3):
            with nc.Block() as blk:
                @blk.gpsimd
                def _(g):
                    gp.inc(g.dma_start(
                        W3RD, Wqkv_in[:, p_ * E:(p_ + 1) * E].rearrange(
                            "(kc p) o -> p kc o", p=128)))

            base_pe = pe.val
            base_ve = ve.val
            nv = 1 if p_ == 2 else 4
            with nc.Block() as blk:
                @blk.tensor
                def _(te):
                    gp.wait_all(te)
                    for Tt in range(NCH):
                        ps = psA if Tt % 2 == 0 else psB
                        if Tt >= 2:
                            ve.wait(te, base_ve + (Tt - 1) * nv)
                        for kc in range(NFC):
                            for no in range(2):
                                inst = te.matmul(
                                    ps[:, no * 512:(no + 1) * 512],
                                    xn_fm[:, kc, Tt * 128:(Tt + 1) * 128],
                                    W3RD[:, kc, no * 512:(no + 1) * 512],
                                    start=(kc == 0), stop=(kc == NFC - 1))
                        pe.inc(inst, 1)

                @blk.vector
                def _(v):
                    for Tt in range(NCH):
                        ps = psA if Tt % 2 == 0 else psB
                        b = Tt // 2
                        pe.wait(v, base_pe + Tt + 1)
                        dst = QKV[:, Tt, p_ * E:(p_ + 1) * E]
                        bia = BQ_BC[:, p_ * E:(p_ + 1) * E]
                        if p_ < 2:
                            ve.inc(v.tensor_tensor(T0_B[:], ps[:, 0:1024], bia,
                                                   op=ALU.add), 1)
                            cosap = cos_t[:, b * D:(b + 1) * D].unsqueeze(1)
                            cosap = cosap.broadcast_to([128, H, D])
                            tv = T0_B[:].rearrange("p (h d) -> p h d", h=H)
                            sw4 = T0_B[:].rearrange(
                                "p (h pr two) -> p h pr two", h=H, two=2)[:, :, :, ::-1]
                            sin4 = sin_t[:, b * D:(b + 1) * D].rearrange(
                                "p (pr two) -> p pr two", two=2)
                            sin4 = sin4.unsqueeze(1).broadcast_to([128, H, D // 2, 2])
                            av4 = AV_T[:].rearrange(
                                "p (h pr two) -> p h pr two", h=H, two=2)
                            dv = dst.rearrange("p (h d) -> p h d", h=H)
                            ve.inc(v.tensor_tensor(dv, tv, cosap, op=ALU.mult), 1)
                            ve.inc(v.tensor_tensor(av4, sw4, sin4, op=ALU.mult), 1)
                            ve.inc(v.tensor_tensor(dst, dst, AV_T[:], op=ALU.add), 1)
                        else:
                            ve.inc(v.tensor_tensor(dst, ps[:, 0:1024], bia,
                                                   op=ALU.add), 1)

    def attn1_phase():
        for b in range(B):
            for th in range(2):
                k = 2 * b + th
                with nc.Block() as blk:
                    @blk.vector
                    def _(v):
                        qc = QKV[:, k, 0:E]
                        for b2 in range(b + 1):
                            kc2 = QKV[:, 2 * b2 + th, E:2 * E]
                            vchain(v, v.tensor_tensor(
                                SCR_F[:, 0:1024], qc, kc2, op=ALU.mult))
                            vchain(v, v.reduce_sum(
                                scpl[:, b2 * H:(b2 + 1) * H],
                                SCR_F[:, 0:1024].rearrange("p (h d) -> p h d", h=H),
                                axis=mybir.AxisListType.X))
                        vchain(v, v.tensor_copy(smx[:], scpl[:, 0:H]))
                        for b2 in range(1, b + 1):
                            vchain(v, v.tensor_tensor(
                                smx[:], smx[:], scpl[:, b2 * H:(b2 + 1) * H],
                                op=ALU.max))
                        for b2 in range(b + 1):
                            vchain(v, v.tensor_tensor(
                                scpl[:, b2 * H:(b2 + 1) * H],
                                scpl[:, b2 * H:(b2 + 1) * H], smx[:], op=ALU.subtract))

                    @blk.scalar
                    def _(s):
                        ve.wait_all(s)
                        for b2 in range(b + 1):
                            se.inc(s.activation(scpl[:, b2 * H:(b2 + 1) * H],
                                                scpl[:, b2 * H:(b2 + 1) * H],
                                                AF.Exp, scale=0.125), 1)

                    @blk.vector
                    def _(v):
                        se.wait_all(v)
                        vchain(v, v.tensor_copy(ssum[:], scpl[:, 0:H]))
                        for b2 in range(1, b + 1):
                            vchain(v, v.tensor_tensor(
                                ssum[:], ssum[:], scpl[:, b2 * H:(b2 + 1) * H],
                                op=ALU.add))
                        vchain(v, v.reciprocal(ssum[:], ssum[:]))
                        for b2 in range(b + 1):
                            vchain(v, v.tensor_tensor(
                                wpl[:, b2 * H:(b2 + 1) * H],
                                scpl[:, b2 * H:(b2 + 1) * H], ssum[:], op=ALU.mult))
                        out = QKV[:, k, 0:E]
                        for b2 in range(b + 1):
                            wap = wpl[:, b2 * H:(b2 + 1) * H].unsqueeze(2)
                            wap = wap.broadcast_to([128, H, D])
                            vc = QKV[:, 2 * b2 + th, 2 * E:3 * E].rearrange(
                                "p (h d) -> p h d", h=H)
                            if b2 == 0:
                                vchain(v, v.tensor_tensor(
                                    out.rearrange("p (h d) -> p h d", h=H),
                                    vc, wap, op=ALU.mult))
                            else:
                                vchain(v, v.tensor_tensor(
                                    AV_T[:].rearrange("p (h d) -> p h d", h=H),
                                    vc, wap, op=ALU.mult))
                                vchain(v, v.tensor_tensor(out, out, AV_T[:],
                                                          op=ALU.add))

    def wout_phase():
        FMT2 = seg(51200, 1024)
        with nc.Block() as blk:
            @blk.gpsimd
            def _(g):
                gp.inc(g.dma_start(W3RD, Wout_in[:].rearrange(
                    "(kc p) o -> p kc o", p=128)))
                gp.inc(g.dma_start(BO_BC[:], bout_in[:].broadcast_to([128, E])))

        for k in range(NCH):
            base_pe = pe.val
            base_ve = ve.val
            with nc.Block() as blk:
                def jslot(fc):
                    pA = psA[:].bitcast(BF16)
                    pB = psB[:].bitcast(BF16)
                    return [pA[:, 2048:2176], pA[:, 3072:3200],
                            pB[:, 2048:2176], pB[:, 3072:3200]][fc % 4]

                @blk.tensor
                def _(te):
                    gp.wait_all(te)
                    for fc in range(NFC):
                        if fc >= 4:
                            ve.wait(te, base_ve + (fc - 3))
                        pe.inc(te.transpose(
                            jslot(fc), QKV[:, k, fc * 128:(fc + 1) * 128],
                            id_b[:]), 1)

                @blk.vector
                def _(v):
                    for fc in range(NFC):
                        pe.wait(v, base_pe + fc + 1)
                        ve.inc(v.tensor_copy(
                            FMT2[:, fc * 128:(fc + 1) * 128], jslot(fc)), 1)

                @blk.tensor
                def _(te):
                    ve.wait(te, base_ve + NFC)
                    for kc in range(NFC):
                        for no in range(2):
                            inst = te.matmul(
                                psA[:, no * 512:(no + 1) * 512],
                                FMT2[:, kc * 128:(kc + 1) * 128],
                                W3RD[:, kc, no * 512:(no + 1) * 512],
                                start=(kc == 0), stop=(kc == NFC - 1))
                    pe.inc(inst, 1)

                @blk.vector
                def _(v):
                    pe.wait(v, base_pe + NFC + 1)
                    ve.inc(v.tensor_tensor(QKV[:, k, 0:E], psA[:, 0:1024],
                                           BO_BC[:], op=ALU.add), 1)

    def ffn1_phase():
        WT = W3RD
        with nc.Block() as blk:
            @blk.gpsimd
            def _(g):
                gp.inc(g.dma_start(disha_sb[0:64, :], disha_in[:]))

            @blk.sync
            def _(sync):
                gp.wait(sync, gp.val)
                for dc in range(NFC):
                    hw.inc(sync.dma_start_transpose(
                        WT[:, dc, :], wffT_d[0:E, dc * 128:(dc + 1) * 128]))

        base_pe = pe.val
        base_ve = ve.val
        with nc.Block() as blk:
            @blk.tensor
            def _(te):
                for k in range(NCH):
                    slot = k % 4
                    if k >= 4:
                        ve.wait(te, base_ve + (k - 3))
                    pe.inc(te.transpose(psB[0:64, slot * 512:slot * 512 + 128],
                                        pooledt[:, k * D:(k + 1) * D], id_f[:]), 1)

            @blk.vector
            def _(v):
                for k in range(NCH):
                    slot = k % 4
                    pe.wait(v, base_pe + k + 1)
                    ve.inc(v.tensor_copy(pooledf[0:64, k * 128:(k + 1) * 128],
                                         psB[0:64, slot * 512:slot * 512 + 128]), 1)

        base_ha = [hwa.val, hwb.val]
        for k in range(NCH):
            b, th = k // 2, k % 2
            base_pe = pe.val
            base_se = se.val
            with nc.Block() as blk:
                @blk.tensor
                def _(te):
                    hw.wait_all(te)
                    gp.wait_all(te)
                    for kc in range(NFC):
                        for no in range(2):
                            inst = te.matmul(psA[:, no * 512:(no + 1) * 512],
                                             xn_fm[:, kc, k * 128:(k + 1) * 128],
                                             WT[:, kc, no * 512:(no + 1) * 512],
                                             start=(kc == 0), stop=False)
                    pe.inc(inst, 1)
                    pe.wait(te, base_pe + 1)
                    for no in range(2):
                        inst = te.matmul(psA[:, no * 512:(no + 1) * 512],
                                         pooledf[0:64, k * 128:(k + 1) * 128],
                                         disha_sb[0:64, no * 512:(no + 1) * 512],
                                         start=False, stop=True)
                    pe.inc(inst, 1)

                @blk.scalar
                def _(s):
                    pe.wait(s, base_pe + 2)
                    se.inc(s.activation(GSC[:, 0:1024], psA[:, 0:1024], AF.Silu), 1)

                @blk.vector
                def _(v):
                    se.wait(v, base_se + 1)
                    ve.inc(v.tensor_tensor(GSC[:, 0:1024], QKV[:, k, 0:E],
                                           GSC[:, 0:1024], op=ALU.mult), 1)
                    ve.wait_all(v)
                    if k >= 2:
                        [hwa, hwb][k % 2].wait(v, base_ha[k % 2] + (k // 2) * 16)
                    ve.inc(v.tensor_tensor(
                        xf32[:, (k % 2) * 1024:(k % 2 + 1) * 1024],
                        X1[:, k, :], GSC[:, 0:1024], op=ALU.add), 1)

                @blk.sync
                def _(sync):
                    ve.wait_all(sync)
                    [hwa, hwb][k % 2].inc(sync.dma_start(
                        o_out[b, th * 128:(th + 1) * 128, :],
                        xf32[:, (k % 2) * 1024:(k % 2 + 1) * 1024]))

    if BUILD_UPTO >= 7:
        ln1_phase()
        tt = tap("xn1_fm", (128, NFC * N), BF16)
        if tt is not None:
            with nc.Block() as blk:
                @blk.sync
                def _(sync):
                    hw.inc(sync.dma_start(tt[:], seg(0, U)))
                    hw.wait_all(sync)
    if BUILD_UPTO >= 7.5:
        qkv_phase()
        tt = tap("qkv", (128, NCH * 3 * E), BF16)
        if tt is not None:
            with nc.Block() as blk:
                @blk.sync
                def _(sync):
                    hw.inc(sync.dma_start(tt[:], seg(U, 49152)))
                    hw.wait_all(sync)
    if BUILD_UPTO >= 8:
        attn1_phase()
        tt = tap("attn1", (128, NCH, E), BF16)
        if tt is not None:
            with nc.Block() as blk:
                @blk.sync
                def _(sync):
                    hw.inc(sync.dma_start(tt[:], QKV[:, :, 0:E]))
                    hw.wait_all(sync)
    if BUILD_UPTO >= 8.5:
        wout_phase()
        tt = tap("s1", (128, NCH, E), BF16)
        if tt is not None:
            with nc.Block() as blk:
                @blk.sync
                def _(sync):
                    hw.inc(sync.dma_start(tt[:], QKV[:, :, 0:E]))
                    hw.wait_all(sync)
    if BUILD_UPTO >= 9:
        ffn1_phase()

    with nc.Block() as blk:
        @blk.sync
        def _(sync):
            hw.wait_all(sync)
            hwa.wait_all(sync)
            hwb.wait_all(sync)

        @blk.gpsimd
        def _(g):
            gp.wait_all(g)

        @blk.scalar
        def _(s):
            hs.wait_all(s)

    return nc, taps


def host_inputs(inputs, core):
    consts = build_consts()
    m = {
        "Wq": inputs["L0_attn_Wq"], "Wk": inputs["L0_attn_Wk"], "Wv": inputs["L0_attn_Wv"],
        "Wa": inputs["L0_attn_Wa"], "Wg": inputs["L0_attn_Wg"], "Wo": inputs["L0_attn_Wo"],
        "A_log": np.asarray(inputs["L0_attn_A_log"]).reshape(H, 1),
        "dt_bias": np.asarray(inputs["L0_attn_dt_bias"]).reshape(H, 1),
        "rms_w": np.asarray(inputs["L0_attn_rms_w"]).reshape(1, D),
        "gate_kernel": inputs["L0_ffn_gate_kernel"],
        "gate_bias": np.asarray(inputs["L0_ffn_gate_bias"]).reshape(1, H),
        "w_gate": inputs["L0_ffn_w_gate"], "w_up": inputs["L0_ffn_w_up"],
        "w_down": inputs["L0_ffn_w_down"],
        "sw_gate": inputs["L0_ffn_sw_gate"], "sw_up": inputs["L0_ffn_sw_up"],
        "sw_down": inputs["L0_ffn_sw_down"],
        "Wqkv": inputs["L1_attn_Wqkv"],
        "bqkv": np.asarray(inputs["L1_attn_bqkv"]).reshape(1, 3 * E),
        "Wout": inputs["L1_attn_Wout"],
        "bout": np.asarray(inputs["L1_attn_bout"]).reshape(1, E),
        "wff": inputs["L1_ffn_weight"], "disha": inputs["L1_ffn_disha"],
    }
    m = {k: np.ascontiguousarray(np.asarray(v, dtype=np.float32)) for k, v in m.items()}
    pm = np.broadcast_to((np.arange(8) < core).astype(np.float32)[None, :], (8, 8))
    m["pmask"] = np.ascontiguousarray(pm)
    m.update(consts)
    x = np.asarray(inputs["L0_attn_x"])
    m["x"] = np.ascontiguousarray(x[:, core * TL:(core + 1) * TL, :], dtype=np.float32)
    return m


def kernel(**inputs):
    nc, taps = build_program()
    in_maps = [host_inputs(inputs, c) for c in range(NCORE)]
    res = run_bass_kernel_spmd(nc, in_maps, core_ids=list(range(NCORE)))
    outs = [np.asarray(r["o"]) for r in res.results]
    full = np.concatenate(outs, axis=1)
    if DEBUG_TAPS:
        kernel.tap_results = [
            {k: np.asarray(v) for k, v in r.items() if k.startswith("tap_")}
            for r in res.results]
    return full


# version 04 (final)
# speedup vs baseline: 1.1855x; 1.1855x over previous
import numpy as np
import ml_dtypes
import concourse.bass as bass
import concourse.mybir as mybir
from concourse.bass_utils import run_bass_kernel_spmd

F32 = mybir.dt.float32
BF16 = mybir.dt.bfloat16
AF = mybir.ActivationFunctionType
ALU = mybir.AluOpType

NCORE = 8
B = 8
TL = 256
N = B * TL
E = 1024
H = 16
D = 64
NCH = 16
NFC = 8
BIG = 1e30
U = 16384
POOL_ELEMS = 98304

BUILD_UPTO = 9
SUBPH = 9
DEBUG_TAPS = ()


def build_consts():
    c = {}
    c["ident_f"] = np.eye(128, dtype=np.float32)
    c["ident_b"] = np.eye(128, dtype=ml_dtypes.bfloat16)
    tri = np.where(np.arange(128)[None, :] >= np.arange(128)[:, None], BIG, 0.0)
    c["triu_b"] = tri.astype(ml_dtypes.bfloat16)
    c["ones_row_b"] = np.ones((1, 128), dtype=ml_dtypes.bfloat16)
    pos = np.arange(B, dtype=np.float64)
    inv = 1.0 / 10000.0 ** (np.arange(0, D, 2) / D)
    fr = pos[:, None] * inv[None, :]
    cos = np.repeat(np.cos(fr), 2, axis=1)
    sin = np.repeat(np.sin(fr), 2, axis=1)
    sin_pm = sin.copy()
    sin_pm[:, 0::2] = -sin[:, 0::2]
    c["cos_bc"] = np.broadcast_to(cos[None, :, :], (128, B, D)).astype(ml_dtypes.bfloat16).copy()
    c["sin_bc"] = np.broadcast_to(sin_pm[None, :, :], (128, B, D)).astype(ml_dtypes.bfloat16).copy()
    blk = np.zeros((128, 2), dtype=ml_dtypes.bfloat16)
    blk[0:64, 0] = 1
    blk[64:128, 1] = 1
    c["blk2_b"] = blk
    bm = np.zeros((128, 8, 16), dtype=ml_dtypes.bfloat16)
    for mo in range(8):
        bm[0:64, mo, 2 * mo] = 1
        bm[64:128, mo, 2 * mo + 1] = 1
    c["blkmap_b"] = bm.reshape(128, 128)
    return c


class Ctr:
    def __init__(self, nc, name):
        self.sem = nc.alloc_semaphore(name)
        self.val = 0

    def inc(self, inst, n=16):
        inst.then_inc(self.sem, n)
        self.val += n
        return inst

    def wait_all(self, eng):
        eng.wait_ge(self.sem, self.val)

    def wait(self, eng, val):
        eng.wait_ge(self.sem, val)


def build_program():
    nc = bass.Bass("TRN2", target_bir_lowering=False, num_devices=NCORE,
                   detect_race_conditions=False)
    din = {}

    def inp(name, shape, dtype=F32):
        din[name] = nc.dram_tensor(name, list(shape), dtype, kind="ExternalInput")
        return din[name]

    x_in = inp("x", (B, TL, E))
    Wq_in = inp("Wq", (E, E))
    Wk_in = inp("Wk", (E, E))
    Wv_in = inp("Wv", (E, E))
    Wa_in = inp("Wa", (E, H))
    Wg_in = inp("Wg", (E, E))
    Wo_in = inp("Wo", (E, E))
    Alog_in = inp("A_log", (H, 1))
    dtb_in = inp("dt_bias", (H, 1))
    rmsw_in = inp("rms_w", (1, D))
    gk_in = inp("gate_kernel", (E, H))
    gb_in = inp("gate_bias", (1, H))
    wgate_in = inp("w_gate", (16, E, 512))
    wup_in = inp("w_up", (16, E, 512))
    wdown_in = inp("w_down", (16, 512, E))
    swg_in = inp("sw_gate", (E, E))
    swu_in = inp("sw_up", (E, E))
    swd_in = inp("sw_down", (E, E))
    Wqkv_in = inp("Wqkv", (E, 3 * E))
    bqkv_in = inp("bqkv", (1, 3 * E))
    Wout_in = inp("Wout", (E, E))
    bout_in = inp("bout", (1, E))
    wff_in = inp("wff", (E, E))
    disha_in = inp("disha", (D, E))
    pmask_in = inp("pmask", (8, 8))
    cid_f = inp("ident_f", (128, 128))
    cid_b = inp("ident_b", (128, 128), BF16)
    ctri_b = inp("triu_b", (128, 128), BF16)
    cones_b = inp("ones_row_b", (1, 128), BF16)
    ccos = inp("cos_bc", (128, B, D), BF16)
    csin = inp("sin_bc", (128, B, D), BF16)
    cblk2 = inp("blk2_b", (128, 2), BF16)
    cblkmap = inp("blkmap_b", (128, 128), BF16)

    o_out = nc.dram_tensor("o", [B, TL, E], F32, kind="ExternalOutput")

    qf_d = nc.dram_tensor("qf_d", [128, NFC, N], BF16, kind="Internal")
    kf_d = nc.dram_tensor("kf_d", [128, NFC, N], BF16, kind="Internal")
    vt_d = nc.dram_tensor("vt_d", [128, NCH, E], BF16, kind="Internal")
    cum_d = nc.dram_tensor("cum_d", [H, N], F32, kind="Internal")
    rqec_d = nc.dram_tensor("rqec_d", [H, N], F32, kind="Internal")
    dec_dram = nc.dram_tensor("dec_dram", [NCORE, B * H], F32, kind="Internal")
    c_d = nc.dram_tensor("c_d", [H, N], BF16, kind="Internal")
    wffT_d = nc.dram_tensor("wffT_d", [E, E], BF16, kind="Internal")
    PAY = B * 512 * 128
    PAYT = PAY + 8 * H
    cc_in = nc.dram_tensor("cc_in", [1, PAYT], F32, kind="Internal")
    cc_out = nc.dram_tensor("cc_out", [1, NCORE * PAYT], F32, kind="Internal")

    taps = {}

    def tap(name, shape, dtype=F32):
        if name in DEBUG_TAPS:
            taps[name] = nc.dram_tensor("tap_" + name, list(shape), dtype,
                                        kind="ExternalOutput")
            return taps[name]
        return None

    pool = nc.alloc_sbuf_tensor("pool", [128, POOL_ELEMS], BF16)

    def seg(off, nelem, dtype=BF16):
        assert off + nelem <= POOL_ELEMS, (off, nelem)
        v = pool[:, off:off + nelem]
        if dtype == F32:
            assert off % 2 == 0 and nelem % 2 == 0
            v = v.bitcast(F32)
        return v

    sb = nc.alloc_sbuf_tensor
    id_f = sb("id_f", [128, 128], F32)
    id_b = sb("id_b", [128, 128], BF16)
    tri_b = sb("tri_b", [128, 128], BF16)
    ones_b = sb("ones_b", [1, 128], BF16)
    cos_t = sb("cos_t", [128, B * D], BF16)
    sin_t = sb("sin_t", [128, B * D], BF16)
    blk2 = sb("blk2", [128, 2], BF16)
    blkmap = sb("blkmap", [128, 128], BF16)
    rw_bc = sb("rw_bc", [128, D], BF16)
    gb_bc = sb("gb_bc", [128, H], F32)
    Wa_sb = sb("Wa_sb", [128, NFC * H], BF16)
    eA = sb("eA", [H, 1], F32)
    dtb = sb("dtb", [H, 1], F32)
    eps_ln = sb("eps_ln", [128, 1], F32)
    eps_e8 = sb("eps_e8", [128, 1], F32)
    eps_rms = sb("eps_rms", [128, 1], F32)
    eps_w = sb("eps_w", [128, 1], F32)
    stat_sum = sb("stat_sum", [128, NCH], F32)
    stat_sq = sb("stat_sq", [128, NCH], F32)
    stat_m = sb("stat_m", [128, NCH], F32)
    stat_r = sb("stat_r", [128, NCH], F32)
    rq_tok = sb("rq_tok", [128, NCH * H], F32)
    rk_tok = sb("rk_tok", [128, NCH * H], F32)
    cum_tok = sb("cum_tok", [128, NCH * H], F32)
    w_tok = sb("w_tok", [128, NCH * H], BF16)
    rr_tok = sb("rr_tok", [128, NCH * H], F32)
    pmask = sb("pmask_sb", [8, 8], F32)
    CE_all = sb("CE_all", [8, NCORE * H], F32)
    dec_acc = sb("dec_acc", [8, H], F32)
    dec_exp = sb("dec_exp", [8, H], F32)
    dec_bcv = sb("dec_bcv", [128, 64], F32)
    rt_sc = sb("rt_sc", [128, NCH * H], F32)
    rt_ms = sb("rt_ms", [128, NCH * H], F32)
    rt_cnt = sb("rt_cnt", [128, NCH * H], F32)
    rt_sc2 = sb("rt_sc2", [128, NCH * H], F32)
    rt_g1 = sb("rt_g1", [128, NCH * 4], F32)
    rt_g2 = sb("rt_g2", [128, NCH * 4], F32)
    rt_w = sb("rt_w", [128, NCH], F32)
    rt_cmb = sb("rt_cmb", [128, NCH * 4], F32)
    ce_sm = sb("ce_sm", [H, B], F32)
    ceT_sm = sb("ceT_sm", [8, H], F32)
    brow = pool[0:1, 94208:97280]

    psA = nc.alloc_psum_tensor("psA", [128, 2048], F32)
    psB = nc.alloc_psum_tensor("psB", [128, 2048], F32)

    hw = Ctr(nc, "hw")
    hwa = Ctr(nc, "hwa")
    hwb = Ctr(nc, "hwb")
    hs = Ctr(nc, "hs")
    gp = Ctr(nc, "gp")
    gpb = Ctr(nc, "gpb")
    pe = Ctr(nc, "pe")
    ve = Ctr(nc, "ve")
    se = Ctr(nc, "se")
    cc = Ctr(nc, "cc")

    xn_fm = seg(0, U).rearrange("p (fc n) -> p fc n", fc=NFC)
    PR = seg(U, U)
    PRf = PR.rearrange("p (a n) -> p a n", a=NFC)
    PRt = PR.rearrange("p (k f) -> p k f", k=NCH)
    att = PR.rearrange("p (k f) -> p k f", k=NCH)
    xtok = seg(65536, U).rearrange("p (k f) -> p k f", k=NCH)
    xf32 = seg(90112, 4096, F32)
    ln_sq = seg(94208, 4096, F32)

    W_A = seg(32768, 8192).rearrange("p (kc o) -> p kc o", kc=NFC)
    W_B = seg(40960, 8192).rearrange("p (kc o) -> p kc o", kc=NFC)
    W_C = seg(49152, 8192).rearrange("p (kc o) -> p kc o", kc=NFC)
    sqb = seg(57344, 2048)
    ROW_LA = seg(59392, 4096, F32)[0:16, :]
    ROW_CUM = seg(63488, 4096, F32)[0:16, :]
    ROW_RNQ = seg(67584, 4096, F32)[0:16, :]
    ROW_NSQ = seg(71680, 4096, F32)[0:16, :]
    ROW_RN = seg(75776, 4096, F32)[0:16, :]
    ROW_DTT = seg(79872, 4096, F32)[0:16, :]
    ROW_SP = seg(83968, 4096, F32)[0:16, :]

    with nc.Block() as blk:
        @blk.sync
        def _(sync):
            hw.inc(sync.dma_start(id_f[:], cid_f[:]))
            hw.inc(sync.dma_start(id_b[:], cid_b[:]))
            hw.inc(sync.dma_start(tri_b[:], ctri_b[:]))
            hw.inc(sync.dma_start(ones_b[:], cones_b[:]))
            hw.inc(sync.dma_start(cos_t[:], ccos[:].rearrange("p b d -> p (b d)")))
            hw.inc(sync.dma_start(sin_t[:], csin[:].rearrange("p b d -> p (b d)")))
            hw.inc(sync.dma_start(blk2[:], cblk2[:]))
            hw.inc(sync.dma_start(blkmap[:], cblkmap[:]))
            hw.inc(sync.dma_start(gb_bc[:], gb_in[:].broadcast_to([128, H])))
            hw.inc(sync.dma_start(eA[:], Alog_in[:]))
            hw.inc(sync.dma_start(dtb[:], dtb_in[:]))
            hw.inc(sync.dma_start(pmask[:], pmask_in[:]))

        @blk.gpsimd
        def _(g):
            gp.inc(g.dma_start(rw_bc[:], rmsw_in[:].broadcast_to([128, D])))
            gp.inc(g.dma_start(W_A, Wq_in[:].rearrange("(kc p) o -> p kc o", p=128)))
            gp.inc(g.dma_start(W_B, Wk_in[:].rearrange("(kc p) o -> p kc o", p=128)))
            gp.inc(g.dma_start(W_C, Wv_in[:].rearrange("(kc p) o -> p kc o", p=128)))
            gp.inc(g.dma_start(Wa_sb[:].rearrange("p (kc h) -> p kc h", kc=NFC),
                               Wa_in[:].rearrange("(kc p) h -> p kc h", p=128)))

        @blk.vector
        def _(v):
            v.memset(eps_ln[:], 1e-5)
            v.memset(eps_e8[:], 1e-8)
            v.memset(eps_rms[:], 1e-6)
            v.memset(eps_w[:], 1e-20)

    with nc.Block() as blk:
        @blk.scalar
        def _(s):
            hw.wait_all(s)
            s.activation(eA[:], eA[:], AF.Exp)

    def ln_pass(load_chunk, dst_tok, nrm_extra=None):
        raise NotImplementedError

    def ln_from_dram(x_dram, dst_fm):
        base_ha = [hwa.val, hwb.val]
        base_ve = ve.val
        base_se = se.val

        with nc.Block() as blk:
            @blk.sync
            def _(sync):
                for k in range(NCH):
                    b, th = k // 2, k % 2
                    sl = [hwa, hwb][k % 2]
                    if k >= 2:
                        ve.wait(sync, base_ve + (k - 1))
                        se.wait(sync, base_se + (k - 1))
                    sl.inc(sync.dma_start(
                        xf32[:, (k % 2) * 1024:(k % 2 + 1) * 1024],
                        x_dram[b, th * 128:(th + 1) * 128, :]))

            @blk.vector
            def _(v):
                for k in range(NCH):
                    [hwa, hwb][k % 2].wait(v, base_ha[k % 2] + (k // 2 + 1) * 16)
                    xv = xf32[:, (k % 2) * 1024:(k % 2 + 1) * 1024]
                    ve.inc(v.reduce_sum(stat_sum[:, k:k + 1], xv,
                                        axis=mybir.AxisListType.X), 1)

            @blk.scalar
            def _(s):
                for k in range(NCH):
                    [hwa, hwb][k % 2].wait(s, base_ha[k % 2] + (k // 2 + 1) * 16)
                    xv = xf32[:, (k % 2) * 1024:(k % 2 + 1) * 1024]
                    sqv = ln_sq[:, (k % 2) * 1024:(k % 2 + 1) * 1024]
                    se.inc(s.activation(sqv, xv, AF.Square,
                                        accum_out=stat_sq[:, k:k + 1]), 1)

        stats_combine()

        base_ha2 = [hwa.val, hwb.val]
        base_ve2 = ve.val
        with nc.Block() as blk:
            @blk.sync
            def _(sync):
                for k in range(NCH):
                    b, th = k // 2, k % 2
                    sl = [hwa, hwb][k % 2]
                    if k >= 2:
                        ve.wait(sync, base_ve2 + (k - 1))
                    sl.inc(sync.dma_start(
                        xf32[:, (k % 2) * 1024:(k % 2 + 1) * 1024],
                        x_dram[b, th * 128:(th + 1) * 128, :]))

            @blk.vector
            def _(v):
                for k in range(NCH):
                    [hwa, hwb][k % 2].wait(v, base_ha2[k % 2] + (k // 2 + 1) * 16)
                    xv = xf32[:, (k % 2) * 1024:(k % 2 + 1) * 1024]
                    ve.inc(v.tensor_scalar(xtok[:, k, :], xv, stat_m[:, k:k + 1],
                                           stat_r[:, k:k + 1],
                                           op0=ALU.subtract, op1=ALU.mult), 1)

        transpose_tok2fm(xtok, dst_fm)

    def stats_combine():
        with nc.Block() as blk:
            @blk.vector
            def _(v):
                v.tensor_scalar_mul(stat_r[:], stat_sq[:], 1.0 / E)
                ve.inc(v.tensor_scalar_mul(stat_m[:], stat_sum[:], 1.0 / E), 1)
                ve.wait_all(v)
                ve.inc(v.tensor_tensor(stat_sum[:], stat_m[:], stat_m[:],
                                       op=ALU.mult), 1)
                ve.wait_all(v)
                v.tensor_sub(stat_r[:], stat_r[:], stat_sum[:])

        with nc.Block() as blk:
            @blk.scalar
            def _(s):
                s.activation(stat_r[:], stat_r[:], AF.Sqrt, bias=eps_ln[:, 0:1])

        with nc.Block() as blk:
            @blk.vector
            def _(v):
                v.reciprocal(stat_r[:], stat_r[:])

    def transpose_tok2fm(src_tok, dst_fm):
        base_pe = pe.val
        base_ve = ve.val
        psv = psA[:].bitcast(BF16)

        with nc.Block() as blk:
            @blk.tensor
            def _(te):
                for idx in range(NCH * NFC):
                    k, fc = idx // NFC, idx % NFC
                    slot = idx % 4
                    if idx >= 4:
                        ve.wait(te, base_ve + (idx - 3))
                    pe.inc(te.transpose(psv[:, slot * 1024:slot * 1024 + 128],
                                        src_tok[:, k, fc * 128:(fc + 1) * 128],
                                        id_b[:]), 1)

            @blk.vector
            def _(v):
                for idx in range(NCH * NFC):
                    k, fc = idx // NFC, idx % NFC
                    slot = idx % 4
                    pe.wait(v, base_pe + idx + 1)
                    ve.inc(v.tensor_copy(dst_fm[:, fc, k * 128:(k + 1) * 128],
                                         psv[:, slot * 1024:slot * 1024 + 128]), 1)

    if BUILD_UPTO >= 1:
        ln_from_dram(x_in, xn_fm)
        t = tap("xn_fm", (128, NFC * N), BF16)
        if t is not None:
            with nc.Block() as blk:
                @blk.sync
                def _(sync):
                    hw.inc(sync.dma_start(t[:], seg(0, U)))
                    hw.wait_all(sync)

    def transpose_rows16(src_rows, dst_tokbuf, cast_bf=False):
        base_pe = pe.val
        base_ve = ve.val
        dtv = dst_tokbuf[:].rearrange("p (k h) -> p k h", k=NCH)
        with nc.Block() as blk:
            @blk.tensor
            def _(te):
                for k in range(NCH):
                    slot = k % 4
                    if k >= 4:
                        ve.wait(te, base_ve + (k - 3))
                    pe.inc(te.transpose(psB[:, slot * 512:slot * 512 + 16],
                                        src_rows[:, k * 128:(k + 1) * 128],
                                        id_f[0:16, 0:16]), 1)

            @blk.vector
            def _(v):
                for k in range(NCH):
                    slot = k % 4
                    pe.wait(v, base_pe + k + 1)
                    ve.inc(v.tensor_copy(dtv[:, k, :],
                                         psB[:, slot * 512:slot * 512 + 16]), 1)

    def wproj_fm(Wv_, dram_out, r_tok_dst, save_rnq):
        for mo in range(NFC):
            base_pe = pe.val
            base_se = se.val
            with nc.Block() as blk:
                @blk.tensor
                def _(te):
                    for kc in range(NFC):
                        for Tt in range(4):
                            inst = te.matmul(psA[:, Tt * 512:(Tt + 1) * 512],
                                             Wv_[:, kc, mo * 128:(mo + 1) * 128],
                                             xn_fm[:, kc, Tt * 512:(Tt + 1) * 512],
                                             start=(kc == 0), stop=(kc == NFC - 1))
                    pe.inc(inst, 1)

                @blk.vector
                def _(v):
                    pe.wait(v, base_pe + 1)
                    for Tt in range(4):
                        ve.inc(v.tensor_copy(PRf[:, mo, Tt * 512:(Tt + 1) * 512],
                                             psA[:, Tt * 512:(Tt + 1) * 512]), 1)

                @blk.scalar
                def _(s):
                    pe.wait(s, base_pe + 1)
                    for Tt in range(4):
                        se.inc(s.activation(sqb[:, Tt * 512:(Tt + 1) * 512],
                                            psA[:, Tt * 512:(Tt + 1) * 512],
                                            AF.Square), 1)

                @blk.tensor
                def _(te):
                    se.wait(te, base_se + 4)
                    for Tt in range(4):
                        pe.inc(te.matmul(psB[0:H, Tt * 512:(Tt + 1) * 512],
                                         blkmap[:, mo * H:(mo + 1) * H],
                                         sqb[:, Tt * 512:(Tt + 1) * 512],
                                         start=(mo == 0), stop=(mo == NFC - 1)), 1)

        with nc.Block() as blk:
            @blk.vector
            def _(v):
                v.tensor_copy(ROW_NSQ[:], psB[0:H, :])

        with nc.Block() as blk:
            @blk.scalar
            def _(s):
                s.activation(ROW_RN[:], ROW_NSQ[:], AF.Sqrt)

        with nc.Block() as blk:
            @blk.vector
            def _(v):
                ve.inc(v.tensor_scalar_max(ROW_RN[:], ROW_RN[:], 1e-12), 1)
                ve.wait_all(v)
                v.reciprocal(ROW_RN[:], ROW_RN[:])
                if save_rnq:
                    v.tensor_copy(ROW_RNQ[:], ROW_RN[:])

        transpose_rows16(ROW_RN, r_tok_dst)

        with nc.Block() as blk:
            @blk.sync
            def _(sync):
                hw.inc(sync.dma_start(dram_out, PR))
                hw.wait_all(sync)

    def vproj_tok(Wv_, dram_out):
        for Tt in range(NCH):
            base_pe = pe.val
            with nc.Block() as blk:
                @blk.tensor
                def _(te):
                    for kc in range(NFC):
                        for no in range(2):
                            inst = te.matmul(psA[:, no * 512:(no + 1) * 512],
                                             xn_fm[:, kc, Tt * 128:(Tt + 1) * 128],
                                             Wv_[:, kc, no * 512:(no + 1) * 512],
                                             start=(kc == 0), stop=(kc == NFC - 1))
                    pe.inc(inst, 1)

                @blk.vector
                def _(v):
                    pe.wait(v, base_pe + 1)
                    for no in range(2):
                        rkap = rk_tok[:, Tt * H + no * 8:Tt * H + (no + 1) * 8]
                        rkap = rkap.unsqueeze(2).broadcast_to([128, 8, 64])
                        pv = psA[:, no * 512:(no + 1) * 512].rearrange(
                            "p (h d) -> p h d", h=8)
                        ov = PRt[:, Tt, no * 512:(no + 1) * 512].rearrange(
                            "p (h d) -> p h d", h=8)
                        ve.inc(v.tensor_tensor(ov, pv, rkap, op=ALU.mult), 1)

        with nc.Block() as blk:
            @blk.sync
            def _(sync):
                hw.inc(sync.dma_start(dram_out, PR))
                hw.wait_all(sync)

    def dt_cum_chain():
        base_pe = pe.val
        with nc.Block() as blk:
            @blk.tensor
            def _(te):
                for Tt in range(4):
                    for kc in range(NFC):
                        inst = te.matmul(psB[0:H, Tt * 512:(Tt + 1) * 512],
                                         Wa_sb[:, kc * H:(kc + 1) * H],
                                         xn_fm[:, kc, Tt * 512:(Tt + 1) * 512],
                                         start=(kc == 0), stop=(kc == NFC - 1))
                    pe.inc(inst, 1)

            @blk.vector
            def _(v):
                pe.wait(v, base_pe + 4)
                ve.inc(v.tensor_scalar(ROW_DTT[:], psB[0:H, :], dtb[:, 0:1], 0.0,
                                       op0=ALU.add, op1=ALU.max), 1)

            @blk.scalar
            def _(s):
                pe.wait(s, base_pe + 4)
                se.inc(s.activation(ROW_SP[:], psB[0:H, :], AF.Abs,
                                    bias=dtb[:, 0:1]), 1)

        with nc.Block() as blk:
            @blk.scalar
            def _(s):
                s.activation(ROW_SP[:], ROW_SP[:], AF.Exp, scale=-1.0)
                s.activation(ROW_SP[:], ROW_SP[:], AF.Ln, bias=1.0)

        with nc.Block() as blk:
            @blk.vector
            def _(v):
                v.tensor_add(ROW_DTT[:], ROW_DTT[:], ROW_SP[:])
                v.tensor_scalar_mul(ROW_DTT[:], ROW_DTT[:], eA[:, 0:1])

        with nc.Block() as blk:
            @blk.scalar
            def _(s):
                s.activation(ROW_LA[:], ROW_DTT[:], AF.Exp, scale=-1.0)
                s.activation(ROW_LA[:], ROW_LA[:], AF.Ln, bias=eps_e8[0:16, 0:1])

        with nc.Block() as blk:
            @blk.vector
            def _(v):
                for b in range(B):
                    inst = v.tensor_tensor_scan(ROW_CUM[:, b * TL:(b + 1) * TL],
                                                ROW_LA[:, b * TL:(b + 1) * TL],
                                                ROW_LA[:, b * TL:(b + 1) * TL],
                                                initial=0.0, op0=ALU.add, op1=ALU.bypass)
                ve.inc(inst, 1)
                ve.wait_all(v)
                ve.inc(v.tensor_copy(ce_sm[:], ROW_CUM.rearrange('h (b t) -> h b t', b=B)[:, :, TL - 1]), 1)

        transpose_rows16(ROW_CUM, cum_tok)

        with nc.Block() as blk:
            @blk.vector
            def _(v):
                cebc = ce_sm[:].unsqueeze(2).broadcast_to([H, B, TL])
                ve.inc(v.tensor_tensor(
                    ROW_SP[:].rearrange("h (b t) -> h b t", b=B),
                    cebc, ROW_CUM[:].rearrange("h (b t) -> h b t", b=B),
                    op=ALU.subtract), 1)

        with nc.Block() as blk:
            @blk.scalar
            def _(s):
                se.inc(s.activation(ROW_SP[:], ROW_SP[:], AF.Exp), 1)

        transpose_rows16(ROW_SP, w_tok)
        base_pe2 = pe.val
        with nc.Block() as blk:
            @blk.tensor
            def _(te):
                pe.inc(te.transpose(psB[0:8, 0:16], ce_sm[:], id_f[0:16, 0:16]), 1)

            @blk.vector
            def _(v):
                pe.wait(v, base_pe2 + 1)
                v.tensor_copy(ceT_sm[:], psB[0:8, 0:16])

        with nc.Block() as blk:
            @blk.scalar
            def _(s):
                s.activation(ROW_DTT[:], ROW_CUM[:], AF.Exp)

        with nc.Block() as blk:
            @blk.vector
            def _(v):
                v.tensor_mul(ROW_DTT[:], ROW_DTT[:], ROW_RNQ[:])

        with nc.Block() as blk:
            @blk.sync
            def _(sync):
                hw.inc(sync.dma_start(cum_d[:], ROW_CUM[:]))
                hw.inc(sync.dma_start(rqec_d[:], ROW_DTT[:]))
                hw.wait_all(sync)

    if BUILD_UPTO >= 2:
        with nc.Block() as blk:
            @blk.gpsimd
            def _(g):
                gp.wait_all(g)
        wproj_fm(W_A, qf_d[:].rearrange("p a n -> p (a n)"), rq_tok, True)
        wproj_fm(W_B, kf_d[:].rearrange("p a n -> p (a n)"), rk_tok, False)
        vproj_tok(W_C, vt_d[:].rearrange("p a n -> p (a n)"))
        dt_cum_chain()

        for nm, src in (("q_raw", qf_d), ("k_raw", kf_d), ("v_tok", vt_d)):
            tt = tap(nm, (128, NFC, N), BF16)
            if tt is not None:
                with nc.Block() as blk:
                    @blk.sync
                    def _(sync):
                        hw.inc(sync.dma_start(
                            tt[:].rearrange("p a n -> p (a n)"),
                            src[:].rearrange("p a n -> p (a n)")))
                        hw.wait_all(sync)
        tt = tap("rows", (16, 4 * N))
        if tt is not None:
            with nc.Block() as blk:
                @blk.sync
                def _(sync):
                    hw.inc(sync.dma_start(tt[0:16, 0:N], ROW_LA[:]))
                    hw.inc(sync.dma_start(tt[0:16, N:2 * N], ROW_CUM[:]))
                    hw.inc(sync.dma_start(tt[0:16, 2 * N:3 * N], ROW_RNQ[:]))
                    hw.inc(sync.dma_start(tt[0:16, 3 * N:4 * N], ROW_RN[:]))
                    hw.wait_all(sync)
        tt = tap("wdbg", (16, N + B))
        if tt is not None:
            with nc.Block() as blk:
                @blk.sync
                def _(sync):
                    hw.inc(sync.dma_start(tt[0:16, 0:N], ROW_SP[:]))
                    hw.inc(sync.dma_start(tt[0:16, N:N + B], ce_sm[:]))
                    hw.wait_all(sync)
        tt = tap("toks", (128, 4 * NCH * H))
        if tt is not None:
            with nc.Block() as blk:
                @blk.vector
                def _(v):
                    v.tensor_copy(xf32[:, 0:NCH * H], rq_tok[:])
                    v.tensor_copy(xf32[:, NCH * H:2 * NCH * H], rk_tok[:])
                    v.tensor_copy(xf32[:, 2 * NCH * H:3 * NCH * H], cum_tok[:])
                    ve.inc(v.tensor_copy(xf32[:, 3 * NCH * H:4 * NCH * H], w_tok[:]), 1)
            with nc.Block() as blk:
                @blk.sync
                def _(sync):
                    hw.inc(sync.dma_start(tt[:], xf32[:, 0:4 * NCH * H]))
                    hw.wait_all(sync)

    QB = seg(32768, 2048).rearrange("p (a t) -> p a t", a=NFC)
    KB = seg(34816, 2048).rearrange("p (a t) -> p a t", a=NFC)
    VB = seg(36864, 2048).rearrange("p (th f) -> p th f", th=2)
    KTB = seg(38912, 2048).rearrange("p (th f) -> p th f", th=2)
    SCT = [seg(40960, 2048), seg(43008, 2048), seg(45056, 2048)]
    DBC = seg(47104, 4096, F32)
    DBC2 = seg(83968, 4096, F32)
    DSUB = seg(51200, 4096, F32)
    S_C = seg(55296, 8192, F32).rearrange("p (b f) -> p b f", b=B)
    S_IN = seg(63488, 8192, F32)
    S_INB = seg(71680, 4096).rearrange("p (b f) -> p b f", b=B)
    S_STG = seg(75776, 8192, F32)
    WQB = seg(88064, 4096, F32)

    def hslice(ap3, h, fc_axis_len=None):
        par = h % 2
        return ap3[par * 64:(par + 1) * 64, h // 2, :]

    def l0_attention():
        for b in range(B):
            base_ha = [hwa.val, hwb.val]
            base_hs = hs.val
            base_pe = pe.val
            base_ve = ve.val
            base_se = se.val
            with nc.Block() as blk:
                @blk.sync
                def _(sync):
                    hwa.inc(sync.dma_start(QB, qf_d[:, :, b * TL:(b + 1) * TL]))
                    hwa.inc(sync.dma_start(KB, kf_d[:, :, b * TL:(b + 1) * TL]))
                    hwa.inc(sync.dma_start(VB, vt_d[:, 2 * b:2 * b + 2, :]))

                @blk.sync
                def _(sync):
                    for c, dst in ((0, DBC), (1, DBC2)):
                        sl = cum_d[:, b * TL + c * 128:b * TL + (c + 1) * 128]
                        sl = sl.unsqueeze(0).broadcast_to([128, H, 128])
                        hs.inc(sync.dma_start(
                            dst[:].rearrange("p (h t) -> p h t", h=H), sl))

                @blk.tensor
                def _(te):
                    hwa.wait(te, base_ha[0] + 32)
                    psv = psA[:].bitcast(BF16)
                    for idx in range(16):
                        th, fc = idx // 8, idx % 8
                        slot = idx % 4
                        if idx >= 4:
                            ve.wait(te, base_ve + (idx - 3))
                        pe.inc(te.transpose(psv[:, slot * 1024:slot * 1024 + 128],
                                            KB[:, fc, th * 128:(th + 1) * 128],
                                            id_b[:]), 1)

                @blk.vector
                def _(v):
                    psv = psA[:].bitcast(BF16)
                    for idx in range(16):
                        th, fc = idx // 8, idx % 8
                        slot = idx % 4
                        pe.wait(v, base_pe + idx + 1)
                        ve.inc(v.tensor_copy(KTB[:, th, fc * 128:(fc + 1) * 128],
                                             psv[:, slot * 1024:slot * 1024 + 128]), 1)
                    for th in range(2):
                        wap = w_tok[:, (2 * b + th) * H:(2 * b + th + 1) * H]
                        wap = wap.unsqueeze(2).broadcast_to([128, H, D])
                        ve.inc(v.tensor_tensor(
                            KTB[:, th, :].rearrange("p (h d) -> p h d", h=H),
                            KTB[:, th, :].rearrange("p (h d) -> p h d", h=H),
                            wap, op=ALU.mult), 1)

                if SUBPH < 1:
                    continue
                @blk.tensor
                def _(te):
                    ve.wait(te, base_ve + 18)
                    hwa.wait(te, base_ha[0] + 48)
                    for h in range(H):
                        par = h % 2
                        for th in range(2):
                            inst = te.matmul(
                                psB[par * 64:(par + 1) * 64,
                                    1536 + (h // 2) * 64:1536 + (h // 2) * 64 + 64],
                                KTB[:, th, h * D:(h + 1) * D],
                                VB[:, th, h * D:(h + 1) * D],
                                start=(th == 0), stop=(th == 1))
                    pe.inc(inst, 1)

                @blk.vector
                def _(v):
                    pe.wait(v, base_pe + 17)
                    ve.inc(v.tensor_copy(S_C[:, b, :], psB[:, 1536:2048]), 1)

                if SUBPH < 2:
                    continue
                if SUBPH >= 2.2:
                    @blk.tensor
                    def _(te):
                        for ip, (r, c) in enumerate(((0, 0), (0, 1), (1, 1))):
                            for par in range(2):
                                for h in range(par, H, 2):
                                    inst = te.matmul(
                                        psA[:, h * 128:(h + 1) * 128],
                                        KB[par * 64:(par + 1) * 64, h // 2,
                                           r * 128:(r + 1) * 128],
                                        QB[par * 64:(par + 1) * 64, h // 2,
                                           c * 128:(c + 1) * 128],
                                        start=True, stop=True)
                                pe.inc(inst, 1)
                                pe.wait_all(te)
                            if SUBPH >= 3 and ip == 0:
                                ve.wait(te, base_ve + 23)
                            elif SUBPH >= 3 and ip == 1:
                                ve.wait(te, base_ve + 26)

                if SUBPH >= 2.4:
                    @blk.vector
                    def _(v):
                        for ip, (r, c) in enumerate(((0, 0), (0, 1), (1, 1))):
                            dbc = DBC if c == 0 else DBC2
                            hs.wait(v, base_hs + (16 if c == 0 else 32))
                            cap = cum_tok[:, (2 * b + r) * H:(2 * b + r + 1) * H]
                            cap = cap.unsqueeze(2).broadcast_to([128, H, 128])
                            ve.inc(v.tensor_tensor(
                                DSUB[:].rearrange("p (h t) -> p h t", h=H),
                                dbc[:].rearrange("p (h t) -> p h t", h=H),
                                cap, op=ALU.subtract), 1)
                            ve.inc(v.tensor_scalar_min(DSUB[:], DSUB[:], 0.0), 1)
                            if SUBPH >= 2.6:
                                se.wait(v, base_se + ip + 1)
                                if r == c:
                                    tap_ = tri_b[:].unsqueeze(1).broadcast_to(
                                        [128, H, 128])
                                    ve.inc(v.tensor_tensor(
                                        SCT[ip].rearrange("p (h t) -> p h t", h=H),
                                        SCT[ip].rearrange("p (h t) -> p h t", h=H),
                                        tap_, op=ALU.min), 1)
                            else:
                                ve.inc(v.engine_nop(), 1) if r == c else None
                            if SUBPH >= 3:
                                pe.wait(v, base_pe + 17 + 2 * (ip + 1))
                                ve.inc(v.tensor_tensor(SCT[ip], SCT[ip], psA[:],
                                                       op=ALU.mult), 1)
                            else:
                                ve.inc(v.engine_nop(), 1)

                if SUBPH >= 2.6:
                    @blk.scalar
                    def _(s):
                        for ip, wv in ((0, 21), (1, 25), (2, 28)):
                            ve.wait(s, base_ve + wv)
                            se.inc(s.activation(SCT[ip], DSUB[:], AF.Exp), 1)

            if SUBPH < 3:
                continue
            base_pe2 = pe.val
            with nc.Block() as blk:
                @blk.tensor
                def _(te):
                    for h in range(H):
                        te.matmul(psB[:, h * D:(h + 1) * D],
                                  SCT[0][:, h * 128:(h + 1) * 128],
                                  VB[:, 0, h * D:(h + 1) * D],
                                  start=True, stop=True)
                    for h in range(H):
                        te.matmul(psB[:, 1024 + h * D:1024 + (h + 1) * D],
                                  SCT[1][:, h * 128:(h + 1) * 128],
                                  VB[:, 0, h * D:(h + 1) * D],
                                  start=True, stop=False)
                        inst = te.matmul(psB[:, 1024 + h * D:1024 + (h + 1) * D],
                                         SCT[2][:, h * 128:(h + 1) * 128],
                                         VB[:, 1, h * D:(h + 1) * D],
                                         start=False, stop=True)
                    pe.inc(inst, 1)

                @blk.vector
                def _(v):
                    pe.wait(v, base_pe2 + 1)
                    for th in range(2):
                        rap = rq_tok[:, (2 * b + th) * H:(2 * b + th + 1) * H]
                        rap = rap.unsqueeze(2).broadcast_to([128, H, D])
                        ve.inc(v.tensor_tensor(
                            att[:, 2 * b + th, :].rearrange("p (h d) -> p h d", h=H),
                            psB[:, th * 1024:(th + 1) * 1024].rearrange(
                                "p (h d) -> p h d", h=H),
                            rap, op=ALU.mult), 1)

        if BUILD_UPTO < 3.5:
            return
        with nc.Block() as blk:
            @blk.gpsimd
            def _(g):
                gp.inc(g.dma_start(
                    cc_in[0, 0:PAY].rearrange("(p n) -> p n", p=128),
                    S_C[:].rearrange("p b f -> p (b f)")))
                gp.inc(g.dma_start(
                    cc_in[0, PAY:PAYT].rearrange("(b h) -> b h", b=8), ceT_sm[:]))
                gp.wait_all(g)
                g.collective_compute(
                    "AllGather", ALU.bypass,
                    replica_groups=[list(range(NCORE))],
                    ins=[cc_in[:]], outs=[cc_out[:]]).then_inc(cc.sem, 1)
                cc.val += 1

    def l0_correction():
        with nc.Block() as blk:
            @blk.sync
            def _(sync):
                cc.wait_all(sync)
                for cp in range(NCORE):
                    off = cp * PAYT + PAY
                    hw.inc(sync.dma_start(
                        CE_all[:, cp * H:(cp + 1) * H],
                        cc_out[0, off:off + 8 * H].rearrange("(b h) -> b h", b=8)))
                hw.wait_all(sync)

        with nc.Block() as blk:
            @blk.vector
            def _(v):
                ve.inc(v.memset(dec_acc[:], 0.0), 1)

            for cp in range(6, -1, -1):
                @blk.scalar
                def _(s, cp=cp):
                    ve.wait_all(s)
                    se.inc(s.activation(dec_exp[:], dec_acc[:], AF.Exp), 1)

                @blk.vector
                def _(v, cp=cp):
                    se.wait_all(v)
                    ve.inc(v.tensor_scalar_mul(dec_exp[:], dec_exp[:],
                                               pmask[:, cp:cp + 1]), 1)
                    ve.wait_all(v)

                @blk.sync
                def _(sync, cp=cp):
                    ve.wait_all(sync)
                    hw.inc(sync.dma_start(
                        dec_dram[cp, :].rearrange("(b h) -> b h", b=8), dec_exp[:]))

                @blk.vector
                def _(v, cp=cp):
                    if cp > 0:
                        hw.wait_all(v)
                        ve.inc(v.tensor_scalar_mul(dec_exp[:],
                                                   CE_all[:, cp * H:(cp + 1) * H],
                                                   pmask[:, cp:cp + 1]), 1)
                        ve.wait_all(v)
                        ve.inc(v.tensor_add(dec_acc[:], dec_acc[:], dec_exp[:]), 1)
                        ve.wait_all(v)

        if SUBPH < 4.2:
            return
        with nc.Block() as blk:
            @blk.vector
            def _(v):
                ve.inc(v.memset(S_IN[:], 0.0), 1)

        for cp in range(7):
            base_ha = hwa.val
            base_hs = hs.val
            with nc.Block() as blk:
                @blk.sync
                def _(sync):
                    off = cp * PAYT
                    hwa.inc(sync.dma_start(
                        S_STG[:], cc_out[0, off:off + PAY].rearrange(
                            "(p n) -> p n", p=128)))

                @blk.scalar
                def _(s):
                    dv = dec_dram[cp, :].rearrange("(b fc2 par) -> b fc2 par", fc2=8, par=2)
                    for par in range(2):
                        sl = dv[:, :, par].unsqueeze(0).broadcast_to([64, 8, 8])
                        hs.inc(s.dma_start(
                            dec_bcv[par * 64:(par + 1) * 64, :].rearrange(
                                "p (b f) -> p b f", b=8), sl,
                            allow_slow_non_contiguous=True))

                @blk.vector
                def _(v):
                    hwa.wait(v, base_ha + 16)
                    hs.wait(v, base_hs + 32)
                    for half in range(2):
                        dap = dec_bcv[:, half * 32:(half + 1) * 32]
                        dap = dap.rearrange("p (b f) -> p b f", b=4)
                        dap = dap.unsqueeze(3).broadcast_to([128, 4, 8, 64])
                        sv = S_STG[:, half * 2048:(half + 1) * 2048].rearrange(
                            "p (b f d) -> p b f d", b=4, f=8)
                        dst = DSUB if half else DBC
                        ve.inc(v.tensor_tensor(
                            dst[:].rearrange("p (b f d) -> p b f d", b=4, f=8),
                            sv, dap, op=ALU.mult), 1)
                    ve.wait_all(v)
                    ve.inc(v.tensor_add(S_IN[:, 0:2048], S_IN[:, 0:2048], DBC[:]), 1)
                    ve.inc(v.tensor_add(S_IN[:, 2048:4096], S_IN[:, 2048:4096],
                                        DSUB[:]), 1)

        if SUBPH < 4.3:
            return
        with nc.Block() as blk:
            @blk.vector
            def _(v):
                ve.inc(v.tensor_copy(S_INB[:].rearrange("p b f -> p (b f)"),
                                     S_IN[:]), 1)

        if SUBPH < 4.4:
            return
        for b in range(B):
            base_ha = [hwa.val, hwb.val]
            base_hs = hs.val
            base_pe = pe.val
            base_ve = ve.val
            with nc.Block() as blk:
                @blk.sync
                def _(sync):
                    hwa.inc(sync.dma_start(QB, qf_d[:, :, b * TL:(b + 1) * TL]))

                @blk.sync
                def _(sync):
                    src = rqec_d[:].rearrange("(fc par) n -> par fc n", par=2)
                    for par in range(2):
                        sl = src[par, :, b * TL:(b + 1) * TL]
                        sl = sl.unsqueeze(0).broadcast_to([64, NFC, TL])
                        hs.inc(sync.dma_start(
                            WQB[par * 64:(par + 1) * 64, :].rearrange(
                                "p (a t) -> p a t", a=NFC), sl))

                @blk.vector
                def _(v):
                    hwa.wait(v, base_ha[0] + 16)
                    hs.wait(v, base_hs + 32)
                    ve.inc(v.tensor_tensor(
                        KB[:].rearrange("p a t -> p (a t)"),
                        QB[:].rearrange("p a t -> p (a t)"),
                        WQB[:], op=ALU.mult), 1)

                @blk.tensor
                def _(te):
                    if SUBPH < 4.6:
                        return
                    ve.wait(te, base_ve + 1)
                    for th in range(2):
                        for par in range(2):
                            for h in range(par, H, 2):
                                inst = te.matmul(
                                    psA[:, th * 1024 + h * D:th * 1024 + (h + 1) * D],
                                    KB[par * 64:(par + 1) * 64, h // 2,
                                       th * 128:(th + 1) * 128],
                                    S_INB[par * 64:(par + 1) * 64, b,
                                          (h // 2) * 64:(h // 2) * 64 + 64],
                                    start=True, stop=True)
                            pe.inc(inst, 1)
                            pe.wait_all(te)

                @blk.vector
                def _(v):
                    if SUBPH < 4.7:
                        return
                    for th in range(2):
                        pe.wait(v, base_pe + 2 * (th + 1))
                        ve.inc(v.tensor_tensor(
                            att[:, 2 * b + th, :], att[:, 2 * b + th, :],
                            psA[:, th * 1024:(th + 1) * 1024], op=ALU.add), 1)

    if BUILD_UPTO >= 3:
        l0_attention()
    if BUILD_UPTO >= 4:
        l0_correction()
        tt = tap("att", (128, NCH * E), BF16)
        if tt is not None:
            with nc.Block() as blk:
                @blk.sync
                def _(sync):
                    hw.inc(sync.dma_start(tt[:], seg(U, U)))
                    hw.wait_all(sync)
        tt = tap("pmdbg", (8, 8 + H))
        if tt is not None:
            with nc.Block() as blk:
                @blk.sync
                def _(sync):
                    hw.inc(sync.dma_start(tt[0:8, 0:8], pmask[:]))
                    hw.inc(sync.dma_start(tt[0:8, 8:8 + H], dec_acc[:]))
                    hw.wait_all(sync)
        tt = tap("decd", (8, NCORE * H))
        if tt is not None:
            with nc.Block() as blk:
                @blk.sync
                def _(sync):
                    for cp in range(NCORE):
                        hw.inc(sync.dma_start(
                            xf32[0:8, cp * H:(cp + 1) * H],
                            dec_dram[cp, :].rearrange("(b h) -> b h", b=8)))
                    hw.wait_all(sync)
            with nc.Block() as blk:
                @blk.sync
                def _(sync):
                    hw.inc(sync.dma_start(tt[:], xf32[0:8, 0:NCORE * H]))
                    hw.wait_all(sync)
        tt = tap("ce2", (8, NCORE * H))
        if tt is not None:
            with nc.Block() as blk:
                @blk.sync
                def _(sync):
                    hw.inc(sync.dma_start(tt[:], CE_all[:]))
                    hw.wait_all(sync)
        tt = tap("S_in", (128, B * 512))
        if tt is not None:
            with nc.Block() as blk:
                @blk.sync
                def _(sync):
                    hw.inc(sync.dma_start(tt[:], S_IN[:]))
                    hw.wait_all(sync)
        tt = tap("S_c", (128, B * 512))
        if tt is not None:
            with nc.Block() as blk:
                @blk.sync
                def _(sync):
                    hw.inc(sync.dma_start(tt[:], S_C[:].rearrange("p b f -> p (b f)")))
                    hw.wait_all(sync)

    GSC = seg(49152, 2048)
    FMT = seg(51200, 1024)
    s0_tok = att

    def l0_outproj():
        with nc.Block() as blk:
            @blk.gpsimd
            def _(g):
                gp.inc(g.dma_start(W_A, Wg_in[:].rearrange("(kc p) o -> p kc o", p=128)))
                gp.inc(g.dma_start(W_B, Wo_in[:].rearrange("(kc p) o -> p kc o", p=128)))

        base_ve = ve.val
        with nc.Block() as blk:
            @blk.vector
            def _(v):
                for k in range(NCH):
                    sqv = xf32[:, (k % 2) * 1024:(k % 2 + 1) * 1024]
                    ve.inc(v.tensor_tensor(sqv, att[:, k, :], att[:, k, :],
                                           op=ALU.mult), 1)
                    ve.inc(v.reduce_sum(
                        rr_tok[:, k * H:(k + 1) * H],
                        sqv.rearrange("p (h d) -> p h d", h=H),
                        axis=mybir.AxisListType.X), 1)

        with nc.Block() as blk:
            @blk.scalar
            def _(s):
                se.inc(s.activation(rr_tok[:], rr_tok[:], AF.Sqrt,
                                    scale=1.0 / D, bias=eps_rms[:, 0:1]), 1)

        with nc.Block() as blk:
            @blk.vector
            def _(v):
                ve.inc(v.reciprocal(rr_tok[:], rr_tok[:]), 1)
                ve.wait_all(v)
                for k in range(NCH):
                    rap = rr_tok[:, k * H:(k + 1) * H].unsqueeze(2)
                    rap = rap.broadcast_to([128, H, D])
                    av = att[:, k, :].rearrange("p (h d) -> p h d", h=H)
                    ve.inc(v.tensor_tensor(av, av, rap, op=ALU.mult), 1)
                    wap = rw_bc[:].unsqueeze(1).broadcast_to([128, H, D])
                    ve.inc(v.tensor_tensor(av, av, wap, op=ALU.mult), 1)

        ttn = tap("attn_n", (128, NCH * E), BF16)
        if ttn is not None:
            with nc.Block() as blk:
                @blk.sync
                def _(sync):
                    hw.inc(sync.dma_start(ttn[:], seg(U, U)))
                    hw.wait_all(sync)

        for k in range(NCH):
            base_pe = pe.val
            base_ve = ve.val
            base_se = se.val
            with nc.Block() as blk:
                @blk.tensor
                def _(te):
                    gp.wait_all(te)
                    for kc in range(NFC):
                        for no in range(2):
                            inst = te.matmul(psA[:, no * 512:(no + 1) * 512],
                                             xn_fm[:, kc, k * 128:(k + 1) * 128],
                                             W_A[:, kc, no * 512:(no + 1) * 512],
                                             start=(kc == 0), stop=(kc == NFC - 1))
                    pe.inc(inst, 1)

                @blk.scalar
                def _(s):
                    pe.wait(s, base_pe + 1)
                    se.inc(s.activation(GSC[:, 0:1024], psA[:, 0:1024], AF.Silu), 1)

                @blk.vector
                def _(v):
                    se.wait(v, base_se + 1)
                    ve.inc(v.tensor_tensor(GSC[:, 1024:2048], att[:, k, :],
                                           GSC[:, 0:1024], op=ALU.mult), 1)

                def jslot(fc):
                    pA = psA[:].bitcast(BF16)
                    pB = psB[:].bitcast(BF16)
                    return [pA[:, 2048:2176], pA[:, 3072:3200],
                            pB[:, 2048:2176], pB[:, 3072:3200]][fc % 4]

                @blk.tensor
                def _(te):
                    ve.wait(te, base_ve + 1)
                    for fc in range(NFC):
                        if fc >= 4:
                            ve.wait(te, base_ve + 1 + (fc - 3))
                        pe.inc(te.transpose(
                            jslot(fc),
                            GSC[:, 1024 + fc * 128:1024 + (fc + 1) * 128],
                            id_b[:]), 1)

                @blk.vector
                def _(v):
                    for fc in range(NFC):
                        pe.wait(v, base_pe + 2 + fc)
                        ve.inc(v.tensor_copy(
                            FMT[:, fc * 128:(fc + 1) * 128], jslot(fc)), 1)

                @blk.tensor
                def _(te):
                    ve.wait(te, base_ve + 9)
                    for kc in range(NFC):
                        for no in range(2):
                            inst = te.matmul(psB[:, no * 512:(no + 1) * 512],
                                             FMT[:, kc * 128:(kc + 1) * 128],
                                             W_B[:, kc, no * 512:(no + 1) * 512],
                                             start=(kc == 0), stop=(kc == NFC - 1))
                    pe.inc(inst, 1)

                @blk.vector
                def _(v):
                    pe.wait(v, base_pe + 10)
                    ve.inc(v.tensor_copy(s0_tok[:, k, :], psB[:, 0:1024]), 1)

            if k == 0:
                tgd = tap("gdbg", (128, 3072), BF16)
                if tgd is not None:
                    with nc.Block() as blk:
                        @blk.sync
                        def _(sync):
                            hw.inc(sync.dma_start(tgd[:, 0:2048], GSC[:]))
                            hw.inc(sync.dma_start(tgd[:, 2048:3072], FMT[:]))
                            hw.wait_all(sync)
                tpb = tap("psb0", (128, 1024))
                if tpb is not None:
                    with nc.Block() as blk:
                        @blk.vector
                        def _(v):
                            ve.inc(v.tensor_copy(xf32[:, 0:1024], psB[:, 0:1024]), 1)
                    with nc.Block() as blk:
                        @blk.sync
                        def _(sync):
                            hw.inc(sync.dma_start(tpb[:], xf32[:, 0:1024]))
                            hw.wait_all(sync)

    gk_sb = sb("gk_sb", [128, NFC * H], BF16)
    C_FM = seg(88064, 2048)[0:16, :]

    def vchain(v, inst):
        ve.inc(inst, 1)
        ve.wait_all(v)

    def moe_routing():
        with nc.Block() as blk:
            @blk.gpsimd
            def _(g):
                gp.inc(g.dma_start(gk_sb[:].rearrange("p (kc h) -> p kc h", kc=NFC),
                                   gk_in[:].rearrange("(kc p) h -> p kc h", p=128)))

        base_pe = pe.val
        base_ve0 = ve.val
        with nc.Block() as blk:
            @blk.tensor
            def _(te):
                gp.wait_all(te)
                for k in range(NCH):
                    if k >= 4:
                        ve.wait(te, base_ve0 + (k - 3))
                    for kc in range(NFC):
                        inst = te.matmul(psB[:, (k % 4) * 512:(k % 4) * 512 + H],
                                         xn_fm[:, kc, k * 128:(k + 1) * 128],
                                         gk_sb[:, kc * H:(kc + 1) * H],
                                         start=(kc == 0), stop=(kc == NFC - 1))
                    pe.inc(inst, 1)

            @blk.vector
            def _(v):
                for k in range(NCH):
                    pe.wait(v, base_pe + k + 1)
                    ve.inc(v.tensor_copy(rt_sc[:, k * H:(k + 1) * H],
                                         psB[:, (k % 4) * 512:(k % 4) * 512 + H]), 1)

        with nc.Block() as blk:
            @blk.scalar
            def _(s):
                se.inc(s.activation(rt_sc[:], rt_sc[:], AF.Sigmoid), 1)

        with nc.Block() as blk:
            @blk.vector
            def _(v):
                gv = gb_bc[:].unsqueeze(1).broadcast_to([128, NCH, H])
                sv = rt_sc[:].rearrange("p (k h) -> p k h", k=NCH)
                vchain(v, v.tensor_tensor(sv, sv, gv, op=ALU.add))
                pairs = [(0, 1), (0, 2), (0, 3), (1, 2), (1, 3), (2, 3)]
                s4 = rt_sc[:].rearrange("p (k g e) -> p k g e", k=NCH, g=4)
                first = True
                for i, j in pairs:
                    dst = rt_g2[:].rearrange("p (k g) -> p k g", k=NCH)
                    vchain(v, v.tensor_tensor(dst, s4[:, :, :, i], s4[:, :, :, j],
                                              op=ALU.add))
                    g1v = rt_g1[:].rearrange("p (k g) -> p k g", k=NCH)
                    if first:
                        vchain(v, v.tensor_copy(g1v, dst))
                        first = False
                    else:
                        vchain(v, v.tensor_tensor(g1v, g1v, dst, op=ALU.max))
                g1v = rt_g1[:].rearrange("p (k g) -> p k g", k=NCH)
                first = True
                for i, j in pairs:
                    dst = rt_g2[:].rearrange("p (k g) -> p k g", k=NCH)[:, :, 0:1]
                    tmp = rt_cnt[:, 0:NCH].rearrange("p (k o) -> p k o", k=NCH)
                    vchain(v, v.tensor_tensor(tmp, g1v[:, :, i:i + 1],
                                              g1v[:, :, j:j + 1], op=ALU.min))
                    if first:
                        vchain(v, v.tensor_copy(rt_w[:].unsqueeze(2), tmp))
                        first = False
                    else:
                        vchain(v, v.tensor_tensor(rt_w[:].unsqueeze(2),
                                                  rt_w[:].unsqueeze(2), tmp,
                                                  op=ALU.max))
                vchain(v, v.tensor_copy(rt_cmb[:], rt_g1[:]))
                gm2b = rt_w[:].unsqueeze(2).broadcast_to([128, NCH, 4])
                vchain(v, v.tensor_tensor(g1v, g1v, gm2b, op=ALU.is_ge))
                gmb = rt_g1[:].rearrange("p (k g) -> p k g", k=NCH)
                gmb = gmb.unsqueeze(3).broadcast_to([128, NCH, 4, 4])
                msv = rt_ms[:].rearrange("p (k g e) -> p k g e", k=NCH, g=4)
                vchain(v, v.tensor_tensor(msv, s4, gmb, op=ALU.mult))
                vchain(v, v.memset(rt_cnt[:], 0.0))
                mall = rt_ms[:].rearrange("p (k e) -> p k e", k=NCH)
                call = rt_cnt[:].rearrange("p (k e) -> p k e", k=NCH)
                for j in range(H):
                    mj = mall[:, :, j:j + 1].broadcast_to([128, NCH, H])
                    g2 = rt_g2[:].rearrange("p (k g) -> p k g", k=NCH)
                    tmp16 = rt_sc2[:].rearrange("p (k e) -> p k e", k=NCH)
                    vchain(v, v.tensor_tensor(tmp16, mj, mall, op=ALU.is_gt))
                    vchain(v, v.tensor_tensor(call, call, tmp16, op=ALU.add))
                vchain(v, v.tensor_single_scalar(call, call, 3.5, op=ALU.is_le))
                vchain(v, v.tensor_tensor(mall, mall, call, op=ALU.mult))
                vchain(v, v.reduce_sum(rt_w[:], mall, axis=mybir.AxisListType.X))
                vchain(v, v.tensor_scalar_add(rt_w[:], rt_w[:], 1e-20))
                vchain(v, v.reciprocal(rt_w[:], rt_w[:]))
                vchain(v, v.tensor_scalar_mul(rt_w[:], rt_w[:], 2.5))
                wb = rt_w[:].unsqueeze(2).broadcast_to([128, NCH, H])
                vchain(v, v.tensor_tensor(mall, mall, wb, op=ALU.mult))

        base_pe2 = pe.val
        base_ve2 = ve.val
        with nc.Block() as blk:
            @blk.tensor
            def _(te):
                for k in range(NCH):
                    if k >= 4:
                        ve.wait(te, base_ve2 + (k - 3))
                    pe.inc(te.transpose(psB[0:16, (k % 4) * 512:(k % 4) * 512 + 128],
                                        rt_ms[:, k * H:(k + 1) * H], id_f[:]), 1)

            @blk.vector
            def _(v):
                for k in range(NCH):
                    pe.wait(v, base_pe2 + k + 1)
                    ve.inc(v.tensor_copy(C_FM[:, k * 128:(k + 1) * 128],
                                         psB[0:16, (k % 4) * 512:(k % 4) * 512 + 128]), 1)

        with nc.Block() as blk:
            @blk.sync
            def _(sync):
                hw.inc(sync.dma_start(c_d[:], C_FM[:]))
                hw.wait_all(sync)

    WG = seg(65536, 4096).rearrange("p (kc m) -> p kc m", kc=NFC)
    WU = seg(69632, 4096).rearrange("p (kc m) -> p kc m", kc=NFC)
    WD = seg(73728, 4096).rearrange("p (kc m) -> p kc m", kc=4)
    INTER = seg(77824, 8192).rearrange("p (mc n) -> p mc n", mc=4)
    CBC = seg(86016, 2048)
    UCS = seg(90112, 2048)
    SIL = seg(92160, 2048)
    Y = seg(32768, 32768, F32).rearrange("p (mo n) -> p mo n", mo=NFC)

    WU2 = seg(94208, 4096).rearrange("p (kc m) -> p kc m", kc=NFC)

    def moe_phase():
        srcs = []
        for e in range(16):
            srcs.append((wgate_in[e].rearrange("(kc p) m -> p kc m", p=128),
                         wup_in[e].rearrange("(kc p) m -> p kc m", p=128),
                         wdown_in[e].rearrange("(kc p) m -> p kc m", p=128),
                         c_d[e:e + 1, :]))
        for half in range(2):
            srcs.append((
                swg_in[:, half * 512:(half + 1) * 512].rearrange(
                    "(kc p) m -> p kc m", p=128),
                swu_in[:, half * 512:(half + 1) * 512].rearrange(
                    "(kc p) m -> p kc m", p=128),
                swd_in[half * 512:(half + 1) * 512, :].rearrange(
                    "(kc p) m -> p kc m", p=128),
                None))
        WUb = [WU, WU2]

        with nc.Block() as blk:
            @blk.gpsimd
            def _(g):
                gp.inc(g.dma_start(WG, srcs[0][0]))
                gp.inc(g.dma_start(WUb[0], srcs[0][1]))
                gp.inc(g.dma_start(WD, srcs[0][2]))

            @blk.sync
            def _(sync):
                hw.inc(sync.dma_start(CBC[:], srcs[0][3].broadcast_to([128, N])))
        wgwu_ready = gp.val - 16
        wd_ready = gp.val
        cbc_ready = (hw, hw.val)

        for i, (wgs, wus, wds, csrc) in enumerate(srcs):
            accumulate = i > 0
            base_pe, base_se, base_ve = pe.val, se.val, ve.val
            wg_i, wu_i = WG, WUb[i % 2]
            my_wgwu, my_wd, my_cbc = wgwu_ready, wd_ready, cbc_ready
            last = i + 1 >= len(srcs)
            with nc.Block() as blk:
                @blk.tensor
                def _(te):
                    gp.wait(te, my_wgwu)
                    for mc in range(4):
                        if mc >= 1:
                            se.wait(te, base_se + 4 * mc)
                            ve.wait(te, base_ve + 2 * mc - 1)
                        for kc in range(NFC):
                            for Tt in range(4):
                                inst = te.matmul(
                                    psA[:, Tt * 512:(Tt + 1) * 512],
                                    wg_i[:, kc, mc * 128:(mc + 1) * 128],
                                    xn_fm[:, kc, Tt * 512:(Tt + 1) * 512],
                                    start=(kc == 0), stop=(kc == NFC - 1))
                        pe.inc(inst, 1)
                        for kc in range(NFC):
                            for Tt in range(4):
                                inst = te.matmul(
                                    psB[:, Tt * 512:(Tt + 1) * 512],
                                    wu_i[:, kc, mc * 128:(mc + 1) * 128],
                                    xn_fm[:, kc, Tt * 512:(Tt + 1) * 512],
                                    start=(kc == 0), stop=(kc == NFC - 1))
                        pe.inc(inst, 1)
                    gp.wait(te, my_wd)
                    ve.wait(te, base_ve + 8)
                    se.wait(te, base_se + 16)
                    for mo in range(NFC):
                        ps = psA if mo % 2 == 0 else psB
                        if mo >= 2:
                            ve.wait(te, base_ve + 8 + mo - 1)
                        for mc in range(4):
                            for Tt in range(4):
                                inst = te.matmul(
                                    ps[:, Tt * 512:(Tt + 1) * 512],
                                    WD[:, mc, mo * 128:(mo + 1) * 128],
                                    INTER[:, mc, Tt * 512:(Tt + 1) * 512],
                                    start=(mc == 0), stop=(mc == 3))
                        pe.inc(inst, 1)

                @blk.scalar
                def _(s):
                    for mc in range(4):
                        pe.wait(s, base_pe + 2 * mc + 1)
                        if mc >= 1:
                            ve.wait(s, base_ve + 2 * mc)
                        for Tt in range(4):
                            se.inc(s.activation(SIL[:, Tt * 512:(Tt + 1) * 512],
                                                psA[:, Tt * 512:(Tt + 1) * 512],
                                                AF.Silu), 1)

                @blk.vector
                def _(v):
                    for mc in range(4):
                        pe.wait(v, base_pe + 2 * mc + 2)
                        if csrc is not None:
                            if mc == 0:
                                my_cbc[0].wait(v, my_cbc[1])
                            ve.inc(v.tensor_tensor(UCS[:], psB[:], CBC[:],
                                                   op=ALU.mult), 1)
                        else:
                            ve.inc(v.tensor_copy(UCS[:], psB[:]), 1)
                        se.wait(v, base_se + 4 * (mc + 1))
                        ve.inc(v.tensor_tensor(INTER[:, mc, :], SIL[:], UCS[:],
                                               op=ALU.mult), 1)
                    for mo in range(NFC):
                        ps = psA if mo % 2 == 0 else psB
                        pe.wait(v, base_pe + 8 + mo + 1)
                        if accumulate:
                            ve.inc(v.tensor_tensor(Y[:, mo, :], Y[:, mo, :],
                                                   ps[:], op=ALU.add), 1)
                        else:
                            ve.inc(v.tensor_copy(Y[:, mo, :], ps[:]), 1)

                if not last:
                    ngs, nus, nds, ncs = srcs[i + 1]

                    @blk.gpsimd
                    def _(g):
                        gp.inc(g.dma_start(WUb[(i + 1) % 2], nus))
                        g.wait_ge(pe.sem, base_pe + 7)
                        gp.inc(g.dma_start(WG, ngs))
                        if ncs is not None:
                            g.wait_ge(ve.sem, base_ve + 7)
                            gp.inc(g.dma_start(
                                CBC[:], ncs.broadcast_to([128, N])))
                        g.wait_ge(pe.sem, base_pe + 16)
                        gp.inc(g.dma_start(WD, nds))
            if not last:
                wd_ready = gp.val
                wgwu_ready = wd_ready - (32 if srcs[i + 1][3] is not None else 16)
                cbc_ready = (gp, wd_ready - 16)

        G0FM = seg(65536, U).rearrange("p (fc n) -> p fc n", fc=NFC)
        with nc.Block() as blk:
            @blk.scalar
            def _(s):
                for mo in range(NFC):
                    for Tt in range(4):
                        se.inc(s.activation(
                            G0FM[:, mo, Tt * 512:(Tt + 1) * 512],
                            Y[:, mo, Tt * 512:(Tt + 1) * 512], AF.Silu), 1)

        G0TOK = seg(32768, U).rearrange("p (k f) -> p k f", k=NCH)
        transpose_tok2fm_inv(G0FM, G0TOK)

    def transpose_tok2fm_inv(src_fm, dst_tok):
        base_pe = pe.val
        base_ve = ve.val
        psv = psA[:].bitcast(BF16)
        with nc.Block() as blk:
            @blk.tensor
            def _(te):
                for idx in range(NCH * NFC):
                    k, fc = idx // NFC, idx % NFC
                    slot = idx % 4
                    if idx >= 4:
                        ve.wait(te, base_ve + (idx - 3))
                    pe.inc(te.transpose(psv[:, slot * 1024:slot * 1024 + 128],
                                        src_fm[:, fc, k * 128:(k + 1) * 128],
                                        id_b[:]), 1)

            @blk.vector
            def _(v):
                for idx in range(NCH * NFC):
                    k, fc = idx // NFC, idx % NFC
                    slot = idx % 4
                    pe.wait(v, base_pe + idx + 1)
                    ve.inc(v.tensor_copy(dst_tok[:, k, fc * 128:(fc + 1) * 128],
                                         psv[:, slot * 1024:slot * 1024 + 128]), 1)

    def residual_x1():
        G0TOK = seg(32768, U).rearrange("p (k f) -> p k f", k=NCH)
        X1 = seg(65536, U).rearrange("p (k f) -> p k f", k=NCH)
        base_ha = [hwa.val, hwb.val]
        base_ve = ve.val
        with nc.Block() as blk:
            @blk.sync
            def _(sync):
                for k in range(NCH):
                    b, th = k // 2, k % 2
                    sl = [hwa, hwb][k % 2]
                    if k >= 2:
                        ve.wait(sync, base_ve + 2 * (k - 1))
                    sl.inc(sync.dma_start(
                        xf32[:, (k % 2) * 1024:(k % 2 + 1) * 1024],
                        x_in[b, th * 128:(th + 1) * 128, :]))

            @blk.vector
            def _(v):
                for k in range(NCH):
                    [hwa, hwb][k % 2].wait(v, base_ha[k % 2] + (k // 2 + 1) * 16)
                    ve.inc(v.tensor_tensor(GSC[:, 0:1024], s0_tok[:, k, :],
                                           G0TOK[:, k, :], op=ALU.mult), 1)
                    ve.inc(v.tensor_tensor(
                        X1[:, k, :], xf32[:, (k % 2) * 1024:(k % 2 + 1) * 1024],
                        GSC[:, 0:1024], op=ALU.add), 1)

    if BUILD_UPTO >= 5:
        tt = tap("att_raw", (128, NCH * E), BF16)
        if tt is not None:
            with nc.Block() as blk:
                @blk.sync
                def _(sync):
                    hw.inc(sync.dma_start(tt[:], seg(U, U)))
                    hw.wait_all(sync)
        l0_outproj()
        if BUILD_UPTO >= 5.5:
            moe_routing()
        tt = tap("s0", (128, NCH * E), BF16)
        if tt is not None:
            with nc.Block() as blk:
                @blk.sync
                def _(sync):
                    hw.inc(sync.dma_start(tt[:], seg(U, U)))
                    hw.wait_all(sync)
        tt = tap("scb", (128, NCH * H))
        if tt is not None:
            with nc.Block() as blk:
                @blk.sync
                def _(sync):
                    hw.inc(sync.dma_start(tt[:], rt_cnt[:]))
                    hw.wait_all(sync)
        tt = tap("rdbg", (128, NCH * H + NCH * 4 + NCH * 4))
        if tt is not None:
            with nc.Block() as blk:
                @blk.sync
                def _(sync):
                    hw.inc(sync.dma_start(tt[:, 0:NCH * H], rt_sc[:]))
                    hw.inc(sync.dma_start(tt[:, NCH * H:NCH * H + NCH * 4], rt_g1[:]))
                    hw.inc(sync.dma_start(tt[:, NCH * H + NCH * 4:], rt_cmb[:]))
                    hw.wait_all(sync)
        tt = tap("cmb", (128, NCH * H))
        if tt is not None:
            with nc.Block() as blk:
                @blk.sync
                def _(sync):
                    hw.inc(sync.dma_start(tt[:], rt_ms[:]))
                    hw.wait_all(sync)
    if BUILD_UPTO >= 6:
        moe_phase()
        residual_x1()
        tt = tap("x1", (128, NCH * E), BF16)
        if tt is not None:
            with nc.Block() as blk:
                @blk.sync
                def _(sync):
                    hw.inc(sync.dma_start(
                        tt[:], seg(65536, U)))
                    hw.wait_all(sync)

    X1 = seg(65536, U).rearrange("p (k f) -> p k f", k=NCH)
    XT1 = seg(U, U).rearrange("p (k f) -> p k f", k=NCH)
    QKV = seg(U, 49152).rearrange("p (k f) -> p k f", k=NCH)
    W3RD = seg(81920, 8192).rearrange("p (kc o) -> p kc o", kc=NFC)
    SCR_F = seg(94208, 2048, F32)
    AV_T = seg(93184, 1024)
    T0_B = seg(94208, 1024)
    BQ_BC = seg(90112, 3072)
    BO_BC = seg(90112, 1024)
    scpl = seg(81920, 256, F32)
    wpl = seg(82176, 128)
    smx = seg(82304, 32, F32)
    ssum = seg(82336, 32, F32)
    pooledt = seg(96256, 2048, F32)
    pooledf = seg(94208, 2048)[0:64, :]
    disha_sb = seg(51200, 1024)[0:64, :]

    def ln1_phase():
        with nc.Block() as blk:
            @blk.vector
            def _(v):
                for k in range(NCH):
                    ve.inc(v.reduce_sum(stat_sum[:, k:k + 1], X1[:, k, :],
                                        axis=mybir.AxisListType.X), 1)

            @blk.scalar
            def _(s):
                for k in range(NCH):
                    sqv = ln_sq[:, (k % 2) * 1024:(k % 2 + 1) * 1024]
                    se.inc(s.activation(sqv, X1[:, k, :], AF.Square,
                                        accum_out=stat_sq[:, k:k + 1]), 1)

        stats_combine()

        with nc.Block() as blk:
            @blk.vector
            def _(v):
                pv = pooledt[:].rearrange("p (k d) -> p k d", k=NCH)
                for k in range(NCH):
                    ve.inc(v.tensor_scalar(XT1[:, k, :], X1[:, k, :],
                                           stat_m[:, k:k + 1], stat_r[:, k:k + 1],
                                           op0=ALU.subtract, op1=ALU.mult), 1)
                    v.reduce_sum(pv[:, k, :],
                                 XT1[:, k, :].rearrange("p (s d) -> p d s", s=16),
                                 axis=mybir.AxisListType.X)

        transpose_tok2fm(XT1, xn_fm)

    def qkv_phase():
        with nc.Block() as blk:
            @blk.gpsimd
            def _(g):
                gp.inc(g.dma_start(BQ_BC[:], bqkv_in[:].broadcast_to([128, 3 * E])))
                gp.inc(g.dma_start(wffT_d[:], wff_in[:]))

        for p_ in range(3):
            with nc.Block() as blk:
                @blk.gpsimd
                def _(g):
                    gp.inc(g.dma_start(
                        W3RD, Wqkv_in[:, p_ * E:(p_ + 1) * E].rearrange(
                            "(kc p) o -> p kc o", p=128)))

            base_pe = pe.val
            base_ve = ve.val
            nv = 1 if p_ == 2 else 4
            with nc.Block() as blk:
                @blk.tensor
                def _(te):
                    gp.wait_all(te)
                    for Tt in range(NCH):
                        ps = psA if Tt % 2 == 0 else psB
                        if Tt >= 2:
                            ve.wait(te, base_ve + (Tt - 1) * nv)
                        for kc in range(NFC):
                            for no in range(2):
                                inst = te.matmul(
                                    ps[:, no * 512:(no + 1) * 512],
                                    xn_fm[:, kc, Tt * 128:(Tt + 1) * 128],
                                    W3RD[:, kc, no * 512:(no + 1) * 512],
                                    start=(kc == 0), stop=(kc == NFC - 1))
                        pe.inc(inst, 1)

                @blk.vector
                def _(v):
                    for Tt in range(NCH):
                        ps = psA if Tt % 2 == 0 else psB
                        b = Tt // 2
                        pe.wait(v, base_pe + Tt + 1)
                        dst = QKV[:, Tt, p_ * E:(p_ + 1) * E]
                        bia = BQ_BC[:, p_ * E:(p_ + 1) * E]
                        if p_ < 2:
                            ve.inc(v.tensor_tensor(T0_B[:], ps[:, 0:1024], bia,
                                                   op=ALU.add), 1)
                            cosap = cos_t[:, b * D:(b + 1) * D].unsqueeze(1)
                            cosap = cosap.broadcast_to([128, H, D])
                            tv = T0_B[:].rearrange("p (h d) -> p h d", h=H)
                            sw4 = T0_B[:].rearrange(
                                "p (h pr two) -> p h pr two", h=H, two=2)[:, :, :, ::-1]
                            sin4 = sin_t[:, b * D:(b + 1) * D].rearrange(
                                "p (pr two) -> p pr two", two=2)
                            sin4 = sin4.unsqueeze(1).broadcast_to([128, H, D // 2, 2])
                            av4 = AV_T[:].rearrange(
                                "p (h pr two) -> p h pr two", h=H, two=2)
                            dv = dst.rearrange("p (h d) -> p h d", h=H)
                            ve.inc(v.tensor_tensor(dv, tv, cosap, op=ALU.mult), 1)
                            ve.inc(v.tensor_tensor(av4, sw4, sin4, op=ALU.mult), 1)
                            ve.inc(v.tensor_tensor(dst, dst, AV_T[:], op=ALU.add), 1)
                        else:
                            ve.inc(v.tensor_tensor(dst, ps[:, 0:1024], bia,
                                                   op=ALU.add), 1)

    def attn1_phase():
        for b in range(B):
            for th in range(2):
                k = 2 * b + th
                with nc.Block() as blk:
                    @blk.vector
                    def _(v):
                        qc = QKV[:, k, 0:E]
                        for b2 in range(b + 1):
                            kc2 = QKV[:, 2 * b2 + th, E:2 * E]
                            vchain(v, v.tensor_tensor(
                                SCR_F[:, 0:1024], qc, kc2, op=ALU.mult))
                            vchain(v, v.reduce_sum(
                                scpl[:, b2 * H:(b2 + 1) * H],
                                SCR_F[:, 0:1024].rearrange("p (h d) -> p h d", h=H),
                                axis=mybir.AxisListType.X))
                        vchain(v, v.tensor_copy(smx[:], scpl[:, 0:H]))
                        for b2 in range(1, b + 1):
                            vchain(v, v.tensor_tensor(
                                smx[:], smx[:], scpl[:, b2 * H:(b2 + 1) * H],
                                op=ALU.max))
                        for b2 in range(b + 1):
                            vchain(v, v.tensor_tensor(
                                scpl[:, b2 * H:(b2 + 1) * H],
                                scpl[:, b2 * H:(b2 + 1) * H], smx[:], op=ALU.subtract))

                    @blk.scalar
                    def _(s):
                        ve.wait_all(s)
                        for b2 in range(b + 1):
                            se.inc(s.activation(scpl[:, b2 * H:(b2 + 1) * H],
                                                scpl[:, b2 * H:(b2 + 1) * H],
                                                AF.Exp, scale=0.125), 1)

                    @blk.vector
                    def _(v):
                        se.wait_all(v)
                        vchain(v, v.tensor_copy(ssum[:], scpl[:, 0:H]))
                        for b2 in range(1, b + 1):
                            vchain(v, v.tensor_tensor(
                                ssum[:], ssum[:], scpl[:, b2 * H:(b2 + 1) * H],
                                op=ALU.add))
                        vchain(v, v.reciprocal(ssum[:], ssum[:]))
                        for b2 in range(b + 1):
                            vchain(v, v.tensor_tensor(
                                wpl[:, b2 * H:(b2 + 1) * H],
                                scpl[:, b2 * H:(b2 + 1) * H], ssum[:], op=ALU.mult))
                        out = QKV[:, k, 0:E]
                        for b2 in range(b + 1):
                            wap = wpl[:, b2 * H:(b2 + 1) * H].unsqueeze(2)
                            wap = wap.broadcast_to([128, H, D])
                            vc = QKV[:, 2 * b2 + th, 2 * E:3 * E].rearrange(
                                "p (h d) -> p h d", h=H)
                            if b2 == 0:
                                vchain(v, v.tensor_tensor(
                                    out.rearrange("p (h d) -> p h d", h=H),
                                    vc, wap, op=ALU.mult))
                            else:
                                vchain(v, v.tensor_tensor(
                                    AV_T[:].rearrange("p (h d) -> p h d", h=H),
                                    vc, wap, op=ALU.mult))
                                vchain(v, v.tensor_tensor(out, out, AV_T[:],
                                                          op=ALU.add))

    def wout_phase():
        FMT2 = seg(51200, 1024)
        with nc.Block() as blk:
            @blk.gpsimd
            def _(g):
                gp.inc(g.dma_start(W3RD, Wout_in[:].rearrange(
                    "(kc p) o -> p kc o", p=128)))
                gp.inc(g.dma_start(BO_BC[:], bout_in[:].broadcast_to([128, E])))

        for k in range(NCH):
            base_pe = pe.val
            base_ve = ve.val
            with nc.Block() as blk:
                def jslot(fc):
                    pA = psA[:].bitcast(BF16)
                    pB = psB[:].bitcast(BF16)
                    return [pA[:, 2048:2176], pA[:, 3072:3200],
                            pB[:, 2048:2176], pB[:, 3072:3200]][fc % 4]

                @blk.tensor
                def _(te):
                    gp.wait_all(te)
                    for fc in range(NFC):
                        if fc >= 4:
                            ve.wait(te, base_ve + (fc - 3))
                        pe.inc(te.transpose(
                            jslot(fc), QKV[:, k, fc * 128:(fc + 1) * 128],
                            id_b[:]), 1)

                @blk.vector
                def _(v):
                    for fc in range(NFC):
                        pe.wait(v, base_pe + fc + 1)
                        ve.inc(v.tensor_copy(
                            FMT2[:, fc * 128:(fc + 1) * 128], jslot(fc)), 1)

                @blk.tensor
                def _(te):
                    ve.wait(te, base_ve + NFC)
                    for kc in range(NFC):
                        for no in range(2):
                            inst = te.matmul(
                                psA[:, no * 512:(no + 1) * 512],
                                FMT2[:, kc * 128:(kc + 1) * 128],
                                W3RD[:, kc, no * 512:(no + 1) * 512],
                                start=(kc == 0), stop=(kc == NFC - 1))
                    pe.inc(inst, 1)

                @blk.vector
                def _(v):
                    pe.wait(v, base_pe + NFC + 1)
                    ve.inc(v.tensor_tensor(QKV[:, k, 0:E], psA[:, 0:1024],
                                           BO_BC[:], op=ALU.add), 1)

    def ffn1_phase():
        WT = W3RD
        with nc.Block() as blk:
            @blk.gpsimd
            def _(g):
                gp.inc(g.dma_start(disha_sb[0:64, :], disha_in[:]))

            @blk.sync
            def _(sync):
                gp.wait(sync, gp.val)
                for dc in range(NFC):
                    hw.inc(sync.dma_start_transpose(
                        WT[:, dc, :], wffT_d[0:E, dc * 128:(dc + 1) * 128]))

        base_pe = pe.val
        base_ve = ve.val
        with nc.Block() as blk:
            @blk.tensor
            def _(te):
                for k in range(NCH):
                    slot = k % 4
                    if k >= 4:
                        ve.wait(te, base_ve + (k - 3))
                    pe.inc(te.transpose(psB[0:64, slot * 512:slot * 512 + 128],
                                        pooledt[:, k * D:(k + 1) * D], id_f[:]), 1)

            @blk.vector
            def _(v):
                for k in range(NCH):
                    slot = k % 4
                    pe.wait(v, base_pe + k + 1)
                    ve.inc(v.tensor_copy(pooledf[0:64, k * 128:(k + 1) * 128],
                                         psB[0:64, slot * 512:slot * 512 + 128]), 1)

        base_ha = [hwa.val, hwb.val]
        for k in range(NCH):
            b, th = k // 2, k % 2
            base_pe = pe.val
            base_se = se.val
            with nc.Block() as blk:
                @blk.tensor
                def _(te):
                    hw.wait_all(te)
                    gp.wait_all(te)
                    for kc in range(NFC):
                        for no in range(2):
                            inst = te.matmul(psA[:, no * 512:(no + 1) * 512],
                                             xn_fm[:, kc, k * 128:(k + 1) * 128],
                                             WT[:, kc, no * 512:(no + 1) * 512],
                                             start=(kc == 0), stop=False)
                    pe.inc(inst, 1)
                    pe.wait(te, base_pe + 1)
                    for no in range(2):
                        inst = te.matmul(psA[:, no * 512:(no + 1) * 512],
                                         pooledf[0:64, k * 128:(k + 1) * 128],
                                         disha_sb[0:64, no * 512:(no + 1) * 512],
                                         start=False, stop=True)
                    pe.inc(inst, 1)

                @blk.scalar
                def _(s):
                    pe.wait(s, base_pe + 2)
                    se.inc(s.activation(GSC[:, 0:1024], psA[:, 0:1024], AF.Silu), 1)

                @blk.vector
                def _(v):
                    se.wait(v, base_se + 1)
                    ve.inc(v.tensor_tensor(GSC[:, 0:1024], QKV[:, k, 0:E],
                                           GSC[:, 0:1024], op=ALU.mult), 1)
                    ve.wait_all(v)
                    if k >= 2:
                        [hwa, hwb][k % 2].wait(v, base_ha[k % 2] + (k // 2) * 16)
                    ve.inc(v.tensor_tensor(
                        xf32[:, (k % 2) * 1024:(k % 2 + 1) * 1024],
                        X1[:, k, :], GSC[:, 0:1024], op=ALU.add), 1)

                @blk.sync
                def _(sync):
                    ve.wait_all(sync)
                    [hwa, hwb][k % 2].inc(sync.dma_start(
                        o_out[b, th * 128:(th + 1) * 128, :],
                        xf32[:, (k % 2) * 1024:(k % 2 + 1) * 1024]))

    if BUILD_UPTO >= 7:
        ln1_phase()
        tt = tap("xn1_fm", (128, NFC * N), BF16)
        if tt is not None:
            with nc.Block() as blk:
                @blk.sync
                def _(sync):
                    hw.inc(sync.dma_start(tt[:], seg(0, U)))
                    hw.wait_all(sync)
    if BUILD_UPTO >= 7.5:
        qkv_phase()
        tt = tap("qkv", (128, NCH * 3 * E), BF16)
        if tt is not None:
            with nc.Block() as blk:
                @blk.sync
                def _(sync):
                    hw.inc(sync.dma_start(tt[:], seg(U, 49152)))
                    hw.wait_all(sync)
    if BUILD_UPTO >= 8:
        attn1_phase()
        tt = tap("attn1", (128, NCH, E), BF16)
        if tt is not None:
            with nc.Block() as blk:
                @blk.sync
                def _(sync):
                    hw.inc(sync.dma_start(tt[:], QKV[:, :, 0:E]))
                    hw.wait_all(sync)
    if BUILD_UPTO >= 8.5:
        wout_phase()
        tt = tap("s1", (128, NCH, E), BF16)
        if tt is not None:
            with nc.Block() as blk:
                @blk.sync
                def _(sync):
                    hw.inc(sync.dma_start(tt[:], QKV[:, :, 0:E]))
                    hw.wait_all(sync)
    if BUILD_UPTO >= 9:
        ffn1_phase()

    with nc.Block() as blk:
        @blk.sync
        def _(sync):
            hw.wait_all(sync)
            hwa.wait_all(sync)
            hwb.wait_all(sync)

        @blk.gpsimd
        def _(g):
            gp.wait_all(g)

        @blk.scalar
        def _(s):
            hs.wait_all(s)

    return nc, taps


def host_inputs(inputs, core):
    consts = build_consts()
    m = {
        "Wq": inputs["L0_attn_Wq"], "Wk": inputs["L0_attn_Wk"], "Wv": inputs["L0_attn_Wv"],
        "Wa": inputs["L0_attn_Wa"], "Wg": inputs["L0_attn_Wg"], "Wo": inputs["L0_attn_Wo"],
        "A_log": np.asarray(inputs["L0_attn_A_log"]).reshape(H, 1),
        "dt_bias": np.asarray(inputs["L0_attn_dt_bias"]).reshape(H, 1),
        "rms_w": np.asarray(inputs["L0_attn_rms_w"]).reshape(1, D),
        "gate_kernel": inputs["L0_ffn_gate_kernel"],
        "gate_bias": np.asarray(inputs["L0_ffn_gate_bias"]).reshape(1, H),
        "w_gate": inputs["L0_ffn_w_gate"], "w_up": inputs["L0_ffn_w_up"],
        "w_down": inputs["L0_ffn_w_down"],
        "sw_gate": inputs["L0_ffn_sw_gate"], "sw_up": inputs["L0_ffn_sw_up"],
        "sw_down": inputs["L0_ffn_sw_down"],
        "Wqkv": inputs["L1_attn_Wqkv"],
        "bqkv": np.asarray(inputs["L1_attn_bqkv"]).reshape(1, 3 * E),
        "Wout": inputs["L1_attn_Wout"],
        "bout": np.asarray(inputs["L1_attn_bout"]).reshape(1, E),
        "wff": inputs["L1_ffn_weight"], "disha": inputs["L1_ffn_disha"],
    }
    m = {k: np.ascontiguousarray(np.asarray(v, dtype=np.float32)) for k, v in m.items()}
    pm = np.broadcast_to((np.arange(8) < core).astype(np.float32)[None, :], (8, 8))
    m["pmask"] = np.ascontiguousarray(pm)
    m.update(consts)
    x = np.asarray(inputs["L0_attn_x"])
    m["x"] = np.ascontiguousarray(x[:, core * TL:(core + 1) * TL, :], dtype=np.float32)
    return m


def kernel(**inputs):
    nc, taps = build_program()
    in_maps = [host_inputs(inputs, c) for c in range(NCORE)]
    res = run_bass_kernel_spmd(nc, in_maps, core_ids=list(range(NCORE)))
    outs = [np.asarray(r["o"]) for r in res.results]
    full = np.concatenate(outs, axis=1)
    if DEBUG_TAPS:
        kernel.tap_results = [
            {k: np.asarray(v) for k, v in r.items() if k.startswith("tap_")}
            for r in res.results]
    return full


# version 05
# speedup vs baseline: 1.1899x; 1.0037x over previous
import numpy as np
import ml_dtypes
import concourse.bass as bass
import concourse.mybir as mybir
from concourse.bass_utils import run_bass_kernel_spmd

F32 = mybir.dt.float32
BF16 = mybir.dt.bfloat16
AF = mybir.ActivationFunctionType
ALU = mybir.AluOpType

NCORE = 8
B = 8
TL = 256
N = B * TL
E = 1024
H = 16
D = 64
NCH = 16
NFC = 8
BIG = 1e30
U = 16384
POOL_ELEMS = 98304

BUILD_UPTO = 9
SUBPH = 9
DEBUG_TAPS = ()


def build_consts():
    c = {}
    c["ident_f"] = np.eye(128, dtype=np.float32)
    c["ident_b"] = np.eye(128, dtype=ml_dtypes.bfloat16)
    tri = np.where(np.arange(128)[None, :] >= np.arange(128)[:, None], BIG, 0.0)
    c["triu_b"] = tri.astype(ml_dtypes.bfloat16)
    c["ones_row_b"] = np.ones((1, 128), dtype=ml_dtypes.bfloat16)
    pos = np.arange(B, dtype=np.float64)
    inv = 1.0 / 10000.0 ** (np.arange(0, D, 2) / D)
    fr = pos[:, None] * inv[None, :]
    cos = np.repeat(np.cos(fr), 2, axis=1)
    sin = np.repeat(np.sin(fr), 2, axis=1)
    sin_pm = sin.copy()
    sin_pm[:, 0::2] = -sin[:, 0::2]
    c["cos_bc"] = np.broadcast_to(cos[None, :, :], (128, B, D)).astype(ml_dtypes.bfloat16).copy()
    c["sin_bc"] = np.broadcast_to(sin_pm[None, :, :], (128, B, D)).astype(ml_dtypes.bfloat16).copy()
    blk = np.zeros((128, 2), dtype=ml_dtypes.bfloat16)
    blk[0:64, 0] = 1
    blk[64:128, 1] = 1
    c["blk2_b"] = blk
    bm = np.zeros((128, 8, 16), dtype=ml_dtypes.bfloat16)
    for mo in range(8):
        bm[0:64, mo, 2 * mo] = 1
        bm[64:128, mo, 2 * mo + 1] = 1
    c["blkmap_b"] = bm.reshape(128, 128)
    return c


class Ctr:
    def __init__(self, nc, name):
        self.sem = nc.alloc_semaphore(name)
        self.val = 0

    def inc(self, inst, n=16):
        inst.then_inc(self.sem, n)
        self.val += n
        return inst

    def wait_all(self, eng):
        eng.wait_ge(self.sem, self.val)

    def wait(self, eng, val):
        eng.wait_ge(self.sem, val)


def build_program():
    nc = bass.Bass("TRN2", target_bir_lowering=False, num_devices=NCORE,
                   detect_race_conditions=False)
    din = {}

    def inp(name, shape, dtype=F32):
        din[name] = nc.dram_tensor(name, list(shape), dtype, kind="ExternalInput")
        return din[name]

    x_in = inp("x", (B, TL, E))
    Wq_in = inp("Wq", (E, E))
    Wk_in = inp("Wk", (E, E))
    Wv_in = inp("Wv", (E, E))
    Wa_in = inp("Wa", (E, H))
    Wg_in = inp("Wg", (E, E))
    Wo_in = inp("Wo", (E, E))
    Alog_in = inp("A_log", (H, 1))
    dtb_in = inp("dt_bias", (H, 1))
    rmsw_in = inp("rms_w", (1, D))
    gk_in = inp("gate_kernel", (E, H))
    gb_in = inp("gate_bias", (1, H))
    wgate_in = inp("w_gate", (16, E, 512))
    wup_in = inp("w_up", (16, E, 512))
    wdown_in = inp("w_down", (16, 512, E))
    swg_in = inp("sw_gate", (E, E))
    swu_in = inp("sw_up", (E, E))
    swd_in = inp("sw_down", (E, E))
    Wqkv_in = inp("Wqkv", (E, 3 * E))
    bqkv_in = inp("bqkv", (1, 3 * E))
    Wout_in = inp("Wout", (E, E))
    bout_in = inp("bout", (1, E))
    wff_in = inp("wff", (E, E))
    disha_in = inp("disha", (D, E))
    pmask_in = inp("pmask", (8, 8))
    cid_f = inp("ident_f", (128, 128))
    cid_b = inp("ident_b", (128, 128), BF16)
    ctri_b = inp("triu_b", (128, 128), BF16)
    cones_b = inp("ones_row_b", (1, 128), BF16)
    ccos = inp("cos_bc", (128, B, D), BF16)
    csin = inp("sin_bc", (128, B, D), BF16)
    cblk2 = inp("blk2_b", (128, 2), BF16)
    cblkmap = inp("blkmap_b", (128, 128), BF16)

    o_out = nc.dram_tensor("o", [B, TL, E], F32, kind="ExternalOutput")

    qf_d = nc.dram_tensor("qf_d", [128, NFC, N], BF16, kind="Internal")
    kf_d = nc.dram_tensor("kf_d", [128, NFC, N], BF16, kind="Internal")
    vt_d = nc.dram_tensor("vt_d", [128, NCH, E], BF16, kind="Internal")
    cum_d = nc.dram_tensor("cum_d", [H, N], F32, kind="Internal")
    rqec_d = nc.dram_tensor("rqec_d", [H, N], F32, kind="Internal")
    dec_dram = nc.dram_tensor("dec_dram", [NCORE, B * H], F32, kind="Internal")
    c_d = nc.dram_tensor("c_d", [H, N], BF16, kind="Internal")
    wffT_d = nc.dram_tensor("wffT_d", [E, E], BF16, kind="Internal")
    PAY = B * 512 * 128
    PAYT = PAY + 8 * H
    cc_in = nc.dram_tensor("cc_in", [1, PAYT], F32, kind="Internal")
    cc_out = nc.dram_tensor("cc_out", [1, NCORE * PAYT], F32, kind="Internal")

    taps = {}

    def tap(name, shape, dtype=F32):
        if name in DEBUG_TAPS:
            taps[name] = nc.dram_tensor("tap_" + name, list(shape), dtype,
                                        kind="ExternalOutput")
            return taps[name]
        return None

    pool = nc.alloc_sbuf_tensor("pool", [128, POOL_ELEMS], BF16)

    def seg(off, nelem, dtype=BF16):
        assert off + nelem <= POOL_ELEMS, (off, nelem)
        v = pool[:, off:off + nelem]
        if dtype == F32:
            assert off % 2 == 0 and nelem % 2 == 0
            v = v.bitcast(F32)
        return v

    sb = nc.alloc_sbuf_tensor
    id_f = sb("id_f", [128, 128], F32)
    id_b = sb("id_b", [128, 128], BF16)
    tri_b = sb("tri_b", [128, 128], BF16)
    ones_b = sb("ones_b", [1, 128], BF16)
    cos_t = sb("cos_t", [128, B * D], BF16)
    sin_t = sb("sin_t", [128, B * D], BF16)
    blk2 = sb("blk2", [128, 2], BF16)
    blkmap = sb("blkmap", [128, 128], BF16)
    rw_bc = sb("rw_bc", [128, D], BF16)
    gb_bc = sb("gb_bc", [128, H], F32)
    Wa_sb = sb("Wa_sb", [128, NFC * H], BF16)
    eA = sb("eA", [H, 1], F32)
    dtb = sb("dtb", [H, 1], F32)
    eps_ln = sb("eps_ln", [128, 1], F32)
    eps_e8 = sb("eps_e8", [128, 1], F32)
    eps_rms = sb("eps_rms", [128, 1], F32)
    eps_w = sb("eps_w", [128, 1], F32)
    stat_sum = sb("stat_sum", [128, NCH], F32)
    stat_sq = sb("stat_sq", [128, NCH], F32)
    stat_m = sb("stat_m", [128, NCH], F32)
    stat_r = sb("stat_r", [128, NCH], F32)
    rq_tok = sb("rq_tok", [128, NCH * H], F32)
    rk_tok = sb("rk_tok", [128, NCH * H], F32)
    cum_tok = sb("cum_tok", [128, NCH * H], F32)
    w_tok = sb("w_tok", [128, NCH * H], BF16)
    rr_tok = sb("rr_tok", [128, NCH * H], F32)
    pmask = sb("pmask_sb", [8, 8], F32)
    CE_all = sb("CE_all", [8, NCORE * H], F32)
    dec_acc = sb("dec_acc", [8, H], F32)
    dec_exp = sb("dec_exp", [8, H], F32)
    dec_bcv = sb("dec_bcv", [128, 64], F32)
    rt_sc = sb("rt_sc", [128, NCH * H], F32)
    rt_ms = sb("rt_ms", [128, NCH * H], F32)
    rt_cnt = sb("rt_cnt", [128, NCH * H], F32)
    rt_sc2 = sb("rt_sc2", [128, NCH * H], F32)
    rt_g1 = sb("rt_g1", [128, NCH * 4], F32)
    rt_g2 = sb("rt_g2", [128, NCH * 4], F32)
    rt_w = sb("rt_w", [128, NCH], F32)
    rt_cmb = sb("rt_cmb", [128, NCH * 4], F32)
    ce_sm = sb("ce_sm", [H, B], F32)
    ceT_sm = sb("ceT_sm", [8, H], F32)
    brow = pool[0:1, 94208:97280]

    psA = nc.alloc_psum_tensor("psA", [128, 2048], F32)
    psB = nc.alloc_psum_tensor("psB", [128, 2048], F32)

    hw = Ctr(nc, "hw")
    hwa = Ctr(nc, "hwa")
    hwb = Ctr(nc, "hwb")
    hs = Ctr(nc, "hs")
    gp = Ctr(nc, "gp")
    gpb = Ctr(nc, "gpb")
    pe = Ctr(nc, "pe")
    ve = Ctr(nc, "ve")
    se = Ctr(nc, "se")
    cc = Ctr(nc, "cc")

    xn_fm = seg(0, U).rearrange("p (fc n) -> p fc n", fc=NFC)
    PR = seg(U, U)
    PRf = PR.rearrange("p (a n) -> p a n", a=NFC)
    PRt = PR.rearrange("p (k f) -> p k f", k=NCH)
    att = PR.rearrange("p (k f) -> p k f", k=NCH)
    xtok = seg(65536, U).rearrange("p (k f) -> p k f", k=NCH)
    xf32 = seg(90112, 4096, F32)
    ln_sq = seg(94208, 4096, F32)

    W_A = seg(32768, 8192).rearrange("p (kc o) -> p kc o", kc=NFC)
    W_B = seg(40960, 8192).rearrange("p (kc o) -> p kc o", kc=NFC)
    W_C = seg(49152, 8192).rearrange("p (kc o) -> p kc o", kc=NFC)
    sqb = seg(57344, 2048)
    ROW_LA = seg(59392, 4096, F32)[0:16, :]
    ROW_CUM = seg(63488, 4096, F32)[0:16, :]
    ROW_RNQ = seg(67584, 4096, F32)[0:16, :]
    ROW_NSQ = seg(71680, 4096, F32)[0:16, :]
    ROW_RN = seg(75776, 4096, F32)[0:16, :]
    ROW_DTT = seg(79872, 4096, F32)[0:16, :]
    ROW_SP = seg(83968, 4096, F32)[0:16, :]

    with nc.Block() as blk:
        @blk.sync
        def _(sync):
            hw.inc(sync.dma_start(id_f[:], cid_f[:]))
            hw.inc(sync.dma_start(id_b[:], cid_b[:]))
            hw.inc(sync.dma_start(tri_b[:], ctri_b[:]))
            hw.inc(sync.dma_start(ones_b[:], cones_b[:]))
            hw.inc(sync.dma_start(cos_t[:], ccos[:].rearrange("p b d -> p (b d)")))
            hw.inc(sync.dma_start(sin_t[:], csin[:].rearrange("p b d -> p (b d)")))
            hw.inc(sync.dma_start(blk2[:], cblk2[:]))
            hw.inc(sync.dma_start(blkmap[:], cblkmap[:]))
            hw.inc(sync.dma_start(gb_bc[:], gb_in[:].broadcast_to([128, H])))
            hw.inc(sync.dma_start(eA[:], Alog_in[:]))
            hw.inc(sync.dma_start(dtb[:], dtb_in[:]))
            hw.inc(sync.dma_start(pmask[:], pmask_in[:]))

        @blk.gpsimd
        def _(g):
            gp.inc(g.dma_start(rw_bc[:], rmsw_in[:].broadcast_to([128, D])))
            gp.inc(g.dma_start(W_A, Wq_in[:].rearrange("(kc p) o -> p kc o", p=128)))
            gp.inc(g.dma_start(W_B, Wk_in[:].rearrange("(kc p) o -> p kc o", p=128)))
            gp.inc(g.dma_start(W_C, Wv_in[:].rearrange("(kc p) o -> p kc o", p=128)))
            gp.inc(g.dma_start(Wa_sb[:].rearrange("p (kc h) -> p kc h", kc=NFC),
                               Wa_in[:].rearrange("(kc p) h -> p kc h", p=128)))

        @blk.vector
        def _(v):
            v.memset(eps_ln[:], 1e-5)
            v.memset(eps_e8[:], 1e-8)
            v.memset(eps_rms[:], 1e-6)
            v.memset(eps_w[:], 1e-20)

    with nc.Block() as blk:
        @blk.scalar
        def _(s):
            hw.wait_all(s)
            s.activation(eA[:], eA[:], AF.Exp)

    def ln_pass(load_chunk, dst_tok, nrm_extra=None):
        raise NotImplementedError

    def ln_from_dram(x_dram, dst_fm):
        base_ha = [hwa.val, hwb.val]
        base_ve = ve.val
        base_se = se.val

        with nc.Block() as blk:
            @blk.sync
            def _(sync):
                for k in range(NCH):
                    b, th = k // 2, k % 2
                    sl = [hwa, hwb][k % 2]
                    if k >= 2:
                        ve.wait(sync, base_ve + (k - 1))
                        se.wait(sync, base_se + (k - 1))
                    sl.inc(sync.dma_start(
                        xf32[:, (k % 2) * 1024:(k % 2 + 1) * 1024],
                        x_dram[b, th * 128:(th + 1) * 128, :]))

            @blk.vector
            def _(v):
                for k in range(NCH):
                    [hwa, hwb][k % 2].wait(v, base_ha[k % 2] + (k // 2 + 1) * 16)
                    xv = xf32[:, (k % 2) * 1024:(k % 2 + 1) * 1024]
                    ve.inc(v.reduce_sum(stat_sum[:, k:k + 1], xv,
                                        axis=mybir.AxisListType.X), 1)

            @blk.scalar
            def _(s):
                for k in range(NCH):
                    [hwa, hwb][k % 2].wait(s, base_ha[k % 2] + (k // 2 + 1) * 16)
                    xv = xf32[:, (k % 2) * 1024:(k % 2 + 1) * 1024]
                    sqv = ln_sq[:, (k % 2) * 1024:(k % 2 + 1) * 1024]
                    se.inc(s.activation(sqv, xv, AF.Square,
                                        accum_out=stat_sq[:, k:k + 1]), 1)

        stats_combine()

        base_ha2 = [hwa.val, hwb.val]
        base_ve2 = ve.val
        with nc.Block() as blk:
            @blk.sync
            def _(sync):
                for k in range(NCH):
                    b, th = k // 2, k % 2
                    sl = [hwa, hwb][k % 2]
                    if k >= 2:
                        ve.wait(sync, base_ve2 + (k - 1))
                    sl.inc(sync.dma_start(
                        xf32[:, (k % 2) * 1024:(k % 2 + 1) * 1024],
                        x_dram[b, th * 128:(th + 1) * 128, :]))

            @blk.vector
            def _(v):
                for k in range(NCH):
                    [hwa, hwb][k % 2].wait(v, base_ha2[k % 2] + (k // 2 + 1) * 16)
                    xv = xf32[:, (k % 2) * 1024:(k % 2 + 1) * 1024]
                    ve.inc(v.tensor_scalar(xtok[:, k, :], xv, stat_m[:, k:k + 1],
                                           stat_r[:, k:k + 1],
                                           op0=ALU.subtract, op1=ALU.mult), 1)

        transpose_tok2fm(xtok, dst_fm)

    def stats_combine():
        with nc.Block() as blk:
            @blk.vector
            def _(v):
                v.tensor_scalar_mul(stat_r[:], stat_sq[:], 1.0 / E)
                ve.inc(v.tensor_scalar_mul(stat_m[:], stat_sum[:], 1.0 / E), 1)
                ve.wait_all(v)
                ve.inc(v.tensor_tensor(stat_sum[:], stat_m[:], stat_m[:],
                                       op=ALU.mult), 1)
                ve.wait_all(v)
                v.tensor_sub(stat_r[:], stat_r[:], stat_sum[:])

        with nc.Block() as blk:
            @blk.scalar
            def _(s):
                s.activation(stat_r[:], stat_r[:], AF.Sqrt, bias=eps_ln[:, 0:1])

        with nc.Block() as blk:
            @blk.vector
            def _(v):
                v.reciprocal(stat_r[:], stat_r[:])

    def transpose_tok2fm(src_tok, dst_fm):
        base_pe = pe.val
        base_ve = ve.val
        psv = psA[:].bitcast(BF16)

        with nc.Block() as blk:
            @blk.tensor
            def _(te):
                for idx in range(NCH * NFC):
                    k, fc = idx // NFC, idx % NFC
                    slot = idx % 4
                    if idx >= 4:
                        ve.wait(te, base_ve + (idx - 3))
                    pe.inc(te.transpose(psv[:, slot * 1024:slot * 1024 + 128],
                                        src_tok[:, k, fc * 128:(fc + 1) * 128],
                                        id_b[:]), 1)

            @blk.vector
            def _(v):
                for idx in range(NCH * NFC):
                    k, fc = idx // NFC, idx % NFC
                    slot = idx % 4
                    pe.wait(v, base_pe + idx + 1)
                    ve.inc(v.tensor_copy(dst_fm[:, fc, k * 128:(k + 1) * 128],
                                         psv[:, slot * 1024:slot * 1024 + 128]), 1)

    if BUILD_UPTO >= 1:
        ln_from_dram(x_in, xn_fm)
        t = tap("xn_fm", (128, NFC * N), BF16)
        if t is not None:
            with nc.Block() as blk:
                @blk.sync
                def _(sync):
                    hw.inc(sync.dma_start(t[:], seg(0, U)))
                    hw.wait_all(sync)

    def transpose_rows16(src_rows, dst_tokbuf, cast_bf=False):
        base_pe = pe.val
        base_ve = ve.val
        dtv = dst_tokbuf[:].rearrange("p (k h) -> p k h", k=NCH)
        with nc.Block() as blk:
            @blk.tensor
            def _(te):
                for k in range(NCH):
                    slot = k % 4
                    if k >= 4:
                        ve.wait(te, base_ve + (k - 3))
                    pe.inc(te.transpose(psB[:, slot * 512:slot * 512 + 16],
                                        src_rows[:, k * 128:(k + 1) * 128],
                                        id_f[0:16, 0:16]), 1)

            @blk.vector
            def _(v):
                for k in range(NCH):
                    slot = k % 4
                    pe.wait(v, base_pe + k + 1)
                    ve.inc(v.tensor_copy(dtv[:, k, :],
                                         psB[:, slot * 512:slot * 512 + 16]), 1)

    def wproj_fm(Wv_, dram_out, r_tok_dst, save_rnq):
        for mo in range(NFC):
            base_pe = pe.val
            base_se = se.val
            with nc.Block() as blk:
                @blk.tensor
                def _(te):
                    for kc in range(NFC):
                        for Tt in range(4):
                            inst = te.matmul(psA[:, Tt * 512:(Tt + 1) * 512],
                                             Wv_[:, kc, mo * 128:(mo + 1) * 128],
                                             xn_fm[:, kc, Tt * 512:(Tt + 1) * 512],
                                             start=(kc == 0), stop=(kc == NFC - 1))
                    pe.inc(inst, 1)

                @blk.vector
                def _(v):
                    pe.wait(v, base_pe + 1)
                    for Tt in range(4):
                        ve.inc(v.tensor_copy(PRf[:, mo, Tt * 512:(Tt + 1) * 512],
                                             psA[:, Tt * 512:(Tt + 1) * 512]), 1)

                @blk.scalar
                def _(s):
                    pe.wait(s, base_pe + 1)
                    for Tt in range(4):
                        se.inc(s.activation(sqb[:, Tt * 512:(Tt + 1) * 512],
                                            psA[:, Tt * 512:(Tt + 1) * 512],
                                            AF.Square), 1)

                @blk.tensor
                def _(te):
                    se.wait(te, base_se + 4)
                    for Tt in range(4):
                        pe.inc(te.matmul(psB[0:H, Tt * 512:(Tt + 1) * 512],
                                         blkmap[:, mo * H:(mo + 1) * H],
                                         sqb[:, Tt * 512:(Tt + 1) * 512],
                                         start=(mo == 0), stop=(mo == NFC - 1)), 1)

        with nc.Block() as blk:
            @blk.vector
            def _(v):
                v.tensor_copy(ROW_NSQ[:], psB[0:H, :])

        with nc.Block() as blk:
            @blk.scalar
            def _(s):
                s.activation(ROW_RN[:], ROW_NSQ[:], AF.Sqrt)

        with nc.Block() as blk:
            @blk.vector
            def _(v):
                ve.inc(v.tensor_scalar_max(ROW_RN[:], ROW_RN[:], 1e-12), 1)
                ve.wait_all(v)
                v.reciprocal(ROW_RN[:], ROW_RN[:])
                if save_rnq:
                    v.tensor_copy(ROW_RNQ[:], ROW_RN[:])

        transpose_rows16(ROW_RN, r_tok_dst)

        with nc.Block() as blk:
            @blk.sync
            def _(sync):
                hw.inc(sync.dma_start(dram_out, PR))
                hw.wait_all(sync)

    def vproj_tok(Wv_, dram_out):
        for Tt in range(NCH):
            base_pe = pe.val
            with nc.Block() as blk:
                @blk.tensor
                def _(te):
                    for kc in range(NFC):
                        for no in range(2):
                            inst = te.matmul(psA[:, no * 512:(no + 1) * 512],
                                             xn_fm[:, kc, Tt * 128:(Tt + 1) * 128],
                                             Wv_[:, kc, no * 512:(no + 1) * 512],
                                             start=(kc == 0), stop=(kc == NFC - 1))
                    pe.inc(inst, 1)

                @blk.vector
                def _(v):
                    pe.wait(v, base_pe + 1)
                    for no in range(2):
                        rkap = rk_tok[:, Tt * H + no * 8:Tt * H + (no + 1) * 8]
                        rkap = rkap.unsqueeze(2).broadcast_to([128, 8, 64])
                        pv = psA[:, no * 512:(no + 1) * 512].rearrange(
                            "p (h d) -> p h d", h=8)
                        ov = PRt[:, Tt, no * 512:(no + 1) * 512].rearrange(
                            "p (h d) -> p h d", h=8)
                        ve.inc(v.tensor_tensor(ov, pv, rkap, op=ALU.mult), 1)

        with nc.Block() as blk:
            @blk.sync
            def _(sync):
                hw.inc(sync.dma_start(dram_out, PR))
                hw.wait_all(sync)

    def dt_cum_chain():
        base_pe = pe.val
        with nc.Block() as blk:
            @blk.tensor
            def _(te):
                for Tt in range(4):
                    for kc in range(NFC):
                        inst = te.matmul(psB[0:H, Tt * 512:(Tt + 1) * 512],
                                         Wa_sb[:, kc * H:(kc + 1) * H],
                                         xn_fm[:, kc, Tt * 512:(Tt + 1) * 512],
                                         start=(kc == 0), stop=(kc == NFC - 1))
                    pe.inc(inst, 1)

            @blk.vector
            def _(v):
                pe.wait(v, base_pe + 4)
                ve.inc(v.tensor_scalar(ROW_DTT[:], psB[0:H, :], dtb[:, 0:1], 0.0,
                                       op0=ALU.add, op1=ALU.max), 1)

            @blk.scalar
            def _(s):
                pe.wait(s, base_pe + 4)
                se.inc(s.activation(ROW_SP[:], psB[0:H, :], AF.Abs,
                                    bias=dtb[:, 0:1]), 1)

        with nc.Block() as blk:
            @blk.scalar
            def _(s):
                s.activation(ROW_SP[:], ROW_SP[:], AF.Exp, scale=-1.0)
                s.activation(ROW_SP[:], ROW_SP[:], AF.Ln, bias=1.0)

        with nc.Block() as blk:
            @blk.vector
            def _(v):
                v.tensor_add(ROW_DTT[:], ROW_DTT[:], ROW_SP[:])
                v.tensor_scalar_mul(ROW_DTT[:], ROW_DTT[:], eA[:, 0:1])

        with nc.Block() as blk:
            @blk.scalar
            def _(s):
                s.activation(ROW_LA[:], ROW_DTT[:], AF.Exp, scale=-1.0)
                s.activation(ROW_LA[:], ROW_LA[:], AF.Ln, bias=eps_e8[0:16, 0:1])

        with nc.Block() as blk:
            @blk.vector
            def _(v):
                for b in range(B):
                    inst = v.tensor_tensor_scan(ROW_CUM[:, b * TL:(b + 1) * TL],
                                                ROW_LA[:, b * TL:(b + 1) * TL],
                                                ROW_LA[:, b * TL:(b + 1) * TL],
                                                initial=0.0, op0=ALU.add, op1=ALU.bypass)
                ve.inc(inst, 1)
                ve.wait_all(v)
                ve.inc(v.tensor_copy(ce_sm[:], ROW_CUM.rearrange('h (b t) -> h b t', b=B)[:, :, TL - 1]), 1)

        transpose_rows16(ROW_CUM, cum_tok)

        with nc.Block() as blk:
            @blk.vector
            def _(v):
                cebc = ce_sm[:].unsqueeze(2).broadcast_to([H, B, TL])
                ve.inc(v.tensor_tensor(
                    ROW_SP[:].rearrange("h (b t) -> h b t", b=B),
                    cebc, ROW_CUM[:].rearrange("h (b t) -> h b t", b=B),
                    op=ALU.subtract), 1)

        with nc.Block() as blk:
            @blk.scalar
            def _(s):
                se.inc(s.activation(ROW_SP[:], ROW_SP[:], AF.Exp), 1)

        transpose_rows16(ROW_SP, w_tok)
        base_pe2 = pe.val
        with nc.Block() as blk:
            @blk.tensor
            def _(te):
                pe.inc(te.transpose(psB[0:8, 0:16], ce_sm[:], id_f[0:16, 0:16]), 1)

            @blk.vector
            def _(v):
                pe.wait(v, base_pe2 + 1)
                v.tensor_copy(ceT_sm[:], psB[0:8, 0:16])

        with nc.Block() as blk:
            @blk.scalar
            def _(s):
                s.activation(ROW_DTT[:], ROW_CUM[:], AF.Exp)

        with nc.Block() as blk:
            @blk.vector
            def _(v):
                v.tensor_mul(ROW_DTT[:], ROW_DTT[:], ROW_RNQ[:])

        with nc.Block() as blk:
            @blk.sync
            def _(sync):
                hw.inc(sync.dma_start(cum_d[:], ROW_CUM[:]))
                hw.inc(sync.dma_start(rqec_d[:], ROW_DTT[:]))
                hw.wait_all(sync)

    if BUILD_UPTO >= 2:
        with nc.Block() as blk:
            @blk.gpsimd
            def _(g):
                gp.wait_all(g)
        wproj_fm(W_A, qf_d[:].rearrange("p a n -> p (a n)"), rq_tok, True)
        wproj_fm(W_B, kf_d[:].rearrange("p a n -> p (a n)"), rk_tok, False)
        vproj_tok(W_C, vt_d[:].rearrange("p a n -> p (a n)"))
        dt_cum_chain()

        for nm, src in (("q_raw", qf_d), ("k_raw", kf_d), ("v_tok", vt_d)):
            tt = tap(nm, (128, NFC, N), BF16)
            if tt is not None:
                with nc.Block() as blk:
                    @blk.sync
                    def _(sync):
                        hw.inc(sync.dma_start(
                            tt[:].rearrange("p a n -> p (a n)"),
                            src[:].rearrange("p a n -> p (a n)")))
                        hw.wait_all(sync)
        tt = tap("rows", (16, 4 * N))
        if tt is not None:
            with nc.Block() as blk:
                @blk.sync
                def _(sync):
                    hw.inc(sync.dma_start(tt[0:16, 0:N], ROW_LA[:]))
                    hw.inc(sync.dma_start(tt[0:16, N:2 * N], ROW_CUM[:]))
                    hw.inc(sync.dma_start(tt[0:16, 2 * N:3 * N], ROW_RNQ[:]))
                    hw.inc(sync.dma_start(tt[0:16, 3 * N:4 * N], ROW_RN[:]))
                    hw.wait_all(sync)
        tt = tap("wdbg", (16, N + B))
        if tt is not None:
            with nc.Block() as blk:
                @blk.sync
                def _(sync):
                    hw.inc(sync.dma_start(tt[0:16, 0:N], ROW_SP[:]))
                    hw.inc(sync.dma_start(tt[0:16, N:N + B], ce_sm[:]))
                    hw.wait_all(sync)
        tt = tap("toks", (128, 4 * NCH * H))
        if tt is not None:
            with nc.Block() as blk:
                @blk.vector
                def _(v):
                    v.tensor_copy(xf32[:, 0:NCH * H], rq_tok[:])
                    v.tensor_copy(xf32[:, NCH * H:2 * NCH * H], rk_tok[:])
                    v.tensor_copy(xf32[:, 2 * NCH * H:3 * NCH * H], cum_tok[:])
                    ve.inc(v.tensor_copy(xf32[:, 3 * NCH * H:4 * NCH * H], w_tok[:]), 1)
            with nc.Block() as blk:
                @blk.sync
                def _(sync):
                    hw.inc(sync.dma_start(tt[:], xf32[:, 0:4 * NCH * H]))
                    hw.wait_all(sync)

    QB = seg(32768, 2048).rearrange("p (a t) -> p a t", a=NFC)
    KB = seg(34816, 2048).rearrange("p (a t) -> p a t", a=NFC)
    VB = seg(36864, 2048).rearrange("p (th f) -> p th f", th=2)
    KTB = seg(38912, 2048).rearrange("p (th f) -> p th f", th=2)
    SCT = [seg(40960, 2048), seg(43008, 2048), seg(45056, 2048)]
    DBC = seg(47104, 4096, F32)
    DBC2 = seg(83968, 4096, F32)
    DSUB = seg(51200, 4096, F32)
    S_C = seg(55296, 8192, F32).rearrange("p (b f) -> p b f", b=B)
    S_IN = seg(63488, 8192, F32)
    S_INB = seg(71680, 4096).rearrange("p (b f) -> p b f", b=B)
    S_STG = seg(75776, 8192, F32)
    WQB = seg(88064, 4096, F32)

    def hslice(ap3, h, fc_axis_len=None):
        par = h % 2
        return ap3[par * 64:(par + 1) * 64, h // 2, :]

    def l0_attention():
        for b in range(B):
            base_ha = [hwa.val, hwb.val]
            base_hs = hs.val
            base_pe = pe.val
            base_ve = ve.val
            base_se = se.val
            with nc.Block() as blk:
                @blk.sync
                def _(sync):
                    hwa.inc(sync.dma_start(QB, qf_d[:, :, b * TL:(b + 1) * TL]))
                    hwa.inc(sync.dma_start(KB, kf_d[:, :, b * TL:(b + 1) * TL]))
                    hwa.inc(sync.dma_start(VB, vt_d[:, 2 * b:2 * b + 2, :]))

                @blk.sync
                def _(sync):
                    for c, dst in ((0, DBC), (1, DBC2)):
                        sl = cum_d[:, b * TL + c * 128:b * TL + (c + 1) * 128]
                        sl = sl.unsqueeze(0).broadcast_to([128, H, 128])
                        hs.inc(sync.dma_start(
                            dst[:].rearrange("p (h t) -> p h t", h=H), sl))

                @blk.tensor
                def _(te):
                    hwa.wait(te, base_ha[0] + 32)
                    psv = psA[:].bitcast(BF16)
                    for idx in range(16):
                        th, fc = idx // 8, idx % 8
                        slot = idx % 4
                        if idx >= 4:
                            ve.wait(te, base_ve + (idx - 3))
                        pe.inc(te.transpose(psv[:, slot * 1024:slot * 1024 + 128],
                                            KB[:, fc, th * 128:(th + 1) * 128],
                                            id_b[:]), 1)

                @blk.vector
                def _(v):
                    psv = psA[:].bitcast(BF16)
                    for idx in range(16):
                        th, fc = idx // 8, idx % 8
                        slot = idx % 4
                        pe.wait(v, base_pe + idx + 1)
                        ve.inc(v.tensor_copy(KTB[:, th, fc * 128:(fc + 1) * 128],
                                             psv[:, slot * 1024:slot * 1024 + 128]), 1)
                    for th in range(2):
                        wap = w_tok[:, (2 * b + th) * H:(2 * b + th + 1) * H]
                        wap = wap.unsqueeze(2).broadcast_to([128, H, D])
                        ve.inc(v.tensor_tensor(
                            KTB[:, th, :].rearrange("p (h d) -> p h d", h=H),
                            KTB[:, th, :].rearrange("p (h d) -> p h d", h=H),
                            wap, op=ALU.mult), 1)

                if SUBPH < 1:
                    continue
                @blk.tensor
                def _(te):
                    ve.wait(te, base_ve + 18)
                    hwa.wait(te, base_ha[0] + 48)
                    for h in range(H):
                        par = h % 2
                        for th in range(2):
                            inst = te.matmul(
                                psB[par * 64:(par + 1) * 64,
                                    1536 + (h // 2) * 64:1536 + (h // 2) * 64 + 64],
                                KTB[:, th, h * D:(h + 1) * D],
                                VB[:, th, h * D:(h + 1) * D],
                                start=(th == 0), stop=(th == 1))
                    pe.inc(inst, 1)

                @blk.vector
                def _(v):
                    pe.wait(v, base_pe + 17)
                    ve.inc(v.tensor_copy(S_C[:, b, :], psB[:, 1536:2048]), 1)

                if SUBPH < 2:
                    continue
                if SUBPH >= 2.2:
                    @blk.tensor
                    def _(te):
                        for ip, (r, c) in enumerate(((0, 0), (0, 1), (1, 1))):
                            for par in range(2):
                                for h in range(par, H, 2):
                                    inst = te.matmul(
                                        psA[:, h * 128:(h + 1) * 128],
                                        KB[par * 64:(par + 1) * 64, h // 2,
                                           r * 128:(r + 1) * 128],
                                        QB[par * 64:(par + 1) * 64, h // 2,
                                           c * 128:(c + 1) * 128],
                                        start=True, stop=True)
                                pe.inc(inst, 1)
                                pe.wait_all(te)
                            if SUBPH >= 3 and ip == 0:
                                ve.wait(te, base_ve + 23)
                            elif SUBPH >= 3 and ip == 1:
                                ve.wait(te, base_ve + 26)

                if SUBPH >= 2.4:
                    @blk.vector
                    def _(v):
                        for ip, (r, c) in enumerate(((0, 0), (0, 1), (1, 1))):
                            dbc = DBC if c == 0 else DBC2
                            hs.wait(v, base_hs + (16 if c == 0 else 32))
                            cap = cum_tok[:, (2 * b + r) * H:(2 * b + r + 1) * H]
                            cap = cap.unsqueeze(2).broadcast_to([128, H, 128])
                            ve.inc(v.tensor_tensor(
                                DSUB[:].rearrange("p (h t) -> p h t", h=H),
                                dbc[:].rearrange("p (h t) -> p h t", h=H),
                                cap, op=ALU.subtract), 1)
                            ve.inc(v.tensor_scalar_min(DSUB[:], DSUB[:], 0.0), 1)
                            if SUBPH >= 2.6:
                                se.wait(v, base_se + ip + 1)
                                if r == c:
                                    tap_ = tri_b[:].unsqueeze(1).broadcast_to(
                                        [128, H, 128])
                                    ve.inc(v.tensor_tensor(
                                        SCT[ip].rearrange("p (h t) -> p h t", h=H),
                                        SCT[ip].rearrange("p (h t) -> p h t", h=H),
                                        tap_, op=ALU.min), 1)
                            else:
                                ve.inc(v.engine_nop(), 1) if r == c else None
                            if SUBPH >= 3:
                                pe.wait(v, base_pe + 17 + 2 * (ip + 1))
                                ve.inc(v.tensor_tensor(SCT[ip], SCT[ip], psA[:],
                                                       op=ALU.mult), 1)
                            else:
                                ve.inc(v.engine_nop(), 1)

                if SUBPH >= 2.6:
                    @blk.scalar
                    def _(s):
                        for ip, wv in ((0, 21), (1, 25), (2, 28)):
                            ve.wait(s, base_ve + wv)
                            se.inc(s.activation(SCT[ip], DSUB[:], AF.Exp), 1)

            if SUBPH < 3:
                continue
            base_pe2 = pe.val
            with nc.Block() as blk:
                @blk.tensor
                def _(te):
                    for h in range(H):
                        te.matmul(psB[:, h * D:(h + 1) * D],
                                  SCT[0][:, h * 128:(h + 1) * 128],
                                  VB[:, 0, h * D:(h + 1) * D],
                                  start=True, stop=True)
                    for h in range(H):
                        te.matmul(psB[:, 1024 + h * D:1024 + (h + 1) * D],
                                  SCT[1][:, h * 128:(h + 1) * 128],
                                  VB[:, 0, h * D:(h + 1) * D],
                                  start=True, stop=False)
                        inst = te.matmul(psB[:, 1024 + h * D:1024 + (h + 1) * D],
                                         SCT[2][:, h * 128:(h + 1) * 128],
                                         VB[:, 1, h * D:(h + 1) * D],
                                         start=False, stop=True)
                    pe.inc(inst, 1)

                @blk.vector
                def _(v):
                    pe.wait(v, base_pe2 + 1)
                    for th in range(2):
                        rap = rq_tok[:, (2 * b + th) * H:(2 * b + th + 1) * H]
                        rap = rap.unsqueeze(2).broadcast_to([128, H, D])
                        ve.inc(v.tensor_tensor(
                            att[:, 2 * b + th, :].rearrange("p (h d) -> p h d", h=H),
                            psB[:, th * 1024:(th + 1) * 1024].rearrange(
                                "p (h d) -> p h d", h=H),
                            rap, op=ALU.mult), 1)

        if BUILD_UPTO < 3.5:
            return
        with nc.Block() as blk:
            @blk.gpsimd
            def _(g):
                gp.inc(g.dma_start(
                    cc_in[0, 0:PAY].rearrange("(p n) -> p n", p=128),
                    S_C[:].rearrange("p b f -> p (b f)")))
                gp.inc(g.dma_start(
                    cc_in[0, PAY:PAYT].rearrange("(b h) -> b h", b=8), ceT_sm[:]))
                gp.wait_all(g)
                g.collective_compute(
                    "AllGather", ALU.bypass,
                    replica_groups=[list(range(NCORE))],
                    ins=[cc_in[:]], outs=[cc_out[:]]).then_inc(cc.sem, 1)
                cc.val += 1

    def l0_correction():
        with nc.Block() as blk:
            @blk.sync
            def _(sync):
                cc.wait_all(sync)
                for cp in range(NCORE):
                    off = cp * PAYT + PAY
                    hw.inc(sync.dma_start(
                        CE_all[:, cp * H:(cp + 1) * H],
                        cc_out[0, off:off + 8 * H].rearrange("(b h) -> b h", b=8)))
                hw.wait_all(sync)

        with nc.Block() as blk:
            @blk.vector
            def _(v):
                ve.inc(v.memset(dec_acc[:], 0.0), 1)

            for cp in range(6, -1, -1):
                @blk.scalar
                def _(s, cp=cp):
                    ve.wait_all(s)
                    se.inc(s.activation(dec_exp[:], dec_acc[:], AF.Exp), 1)

                @blk.vector
                def _(v, cp=cp):
                    se.wait_all(v)
                    ve.inc(v.tensor_scalar_mul(dec_exp[:], dec_exp[:],
                                               pmask[:, cp:cp + 1]), 1)
                    ve.wait_all(v)

                @blk.sync
                def _(sync, cp=cp):
                    ve.wait_all(sync)
                    hw.inc(sync.dma_start(
                        dec_dram[cp, :].rearrange("(b h) -> b h", b=8), dec_exp[:]))

                @blk.vector
                def _(v, cp=cp):
                    if cp > 0:
                        hw.wait_all(v)
                        ve.inc(v.tensor_scalar_mul(dec_exp[:],
                                                   CE_all[:, cp * H:(cp + 1) * H],
                                                   pmask[:, cp:cp + 1]), 1)
                        ve.wait_all(v)
                        ve.inc(v.tensor_add(dec_acc[:], dec_acc[:], dec_exp[:]), 1)
                        ve.wait_all(v)

        if SUBPH < 4.2:
            return
        with nc.Block() as blk:
            @blk.vector
            def _(v):
                ve.inc(v.memset(S_IN[:], 0.0), 1)

        for cp in range(7):
            base_ha = hwa.val
            base_hs = hs.val
            with nc.Block() as blk:
                @blk.sync
                def _(sync):
                    off = cp * PAYT
                    hwa.inc(sync.dma_start(
                        S_STG[:], cc_out[0, off:off + PAY].rearrange(
                            "(p n) -> p n", p=128)))

                @blk.scalar
                def _(s):
                    dv = dec_dram[cp, :].rearrange("(b fc2 par) -> b fc2 par", fc2=8, par=2)
                    for par in range(2):
                        sl = dv[:, :, par].unsqueeze(0).broadcast_to([64, 8, 8])
                        hs.inc(s.dma_start(
                            dec_bcv[par * 64:(par + 1) * 64, :].rearrange(
                                "p (b f) -> p b f", b=8), sl,
                            allow_slow_non_contiguous=True))

                @blk.vector
                def _(v):
                    hwa.wait(v, base_ha + 16)
                    hs.wait(v, base_hs + 32)
                    for half in range(2):
                        dap = dec_bcv[:, half * 32:(half + 1) * 32]
                        dap = dap.rearrange("p (b f) -> p b f", b=4)
                        dap = dap.unsqueeze(3).broadcast_to([128, 4, 8, 64])
                        sv = S_STG[:, half * 2048:(half + 1) * 2048].rearrange(
                            "p (b f d) -> p b f d", b=4, f=8)
                        dst = DSUB if half else DBC
                        ve.inc(v.tensor_tensor(
                            dst[:].rearrange("p (b f d) -> p b f d", b=4, f=8),
                            sv, dap, op=ALU.mult), 1)
                    ve.wait_all(v)
                    ve.inc(v.tensor_add(S_IN[:, 0:2048], S_IN[:, 0:2048], DBC[:]), 1)
                    ve.inc(v.tensor_add(S_IN[:, 2048:4096], S_IN[:, 2048:4096],
                                        DSUB[:]), 1)

        if SUBPH < 4.3:
            return
        with nc.Block() as blk:
            @blk.vector
            def _(v):
                ve.inc(v.tensor_copy(S_INB[:].rearrange("p b f -> p (b f)"),
                                     S_IN[:]), 1)

        if SUBPH < 4.4:
            return
        for b in range(B):
            base_ha = [hwa.val, hwb.val]
            base_hs = hs.val
            base_pe = pe.val
            base_ve = ve.val
            with nc.Block() as blk:
                @blk.sync
                def _(sync):
                    hwa.inc(sync.dma_start(QB, qf_d[:, :, b * TL:(b + 1) * TL]))

                @blk.sync
                def _(sync):
                    src = rqec_d[:].rearrange("(fc par) n -> par fc n", par=2)
                    for par in range(2):
                        sl = src[par, :, b * TL:(b + 1) * TL]
                        sl = sl.unsqueeze(0).broadcast_to([64, NFC, TL])
                        hs.inc(sync.dma_start(
                            WQB[par * 64:(par + 1) * 64, :].rearrange(
                                "p (a t) -> p a t", a=NFC), sl))

                @blk.vector
                def _(v):
                    hwa.wait(v, base_ha[0] + 16)
                    hs.wait(v, base_hs + 32)
                    ve.inc(v.tensor_tensor(
                        KB[:].rearrange("p a t -> p (a t)"),
                        QB[:].rearrange("p a t -> p (a t)"),
                        WQB[:], op=ALU.mult), 1)

                @blk.tensor
                def _(te):
                    if SUBPH < 4.6:
                        return
                    ve.wait(te, base_ve + 1)
                    for th in range(2):
                        for par in range(2):
                            for h in range(par, H, 2):
                                inst = te.matmul(
                                    psA[:, th * 1024 + h * D:th * 1024 + (h + 1) * D],
                                    KB[par * 64:(par + 1) * 64, h // 2,
                                       th * 128:(th + 1) * 128],
                                    S_INB[par * 64:(par + 1) * 64, b,
                                          (h // 2) * 64:(h // 2) * 64 + 64],
                                    start=True, stop=True)
                            pe.inc(inst, 1)
                            pe.wait_all(te)

                @blk.vector
                def _(v):
                    if SUBPH < 4.7:
                        return
                    for th in range(2):
                        pe.wait(v, base_pe + 2 * (th + 1))
                        ve.inc(v.tensor_tensor(
                            att[:, 2 * b + th, :], att[:, 2 * b + th, :],
                            psA[:, th * 1024:(th + 1) * 1024], op=ALU.add), 1)

    if BUILD_UPTO >= 3:
        l0_attention()
    if BUILD_UPTO >= 4:
        l0_correction()
        tt = tap("att", (128, NCH * E), BF16)
        if tt is not None:
            with nc.Block() as blk:
                @blk.sync
                def _(sync):
                    hw.inc(sync.dma_start(tt[:], seg(U, U)))
                    hw.wait_all(sync)
        tt = tap("pmdbg", (8, 8 + H))
        if tt is not None:
            with nc.Block() as blk:
                @blk.sync
                def _(sync):
                    hw.inc(sync.dma_start(tt[0:8, 0:8], pmask[:]))
                    hw.inc(sync.dma_start(tt[0:8, 8:8 + H], dec_acc[:]))
                    hw.wait_all(sync)
        tt = tap("decd", (8, NCORE * H))
        if tt is not None:
            with nc.Block() as blk:
                @blk.sync
                def _(sync):
                    for cp in range(NCORE):
                        hw.inc(sync.dma_start(
                            xf32[0:8, cp * H:(cp + 1) * H],
                            dec_dram[cp, :].rearrange("(b h) -> b h", b=8)))
                    hw.wait_all(sync)
            with nc.Block() as blk:
                @blk.sync
                def _(sync):
                    hw.inc(sync.dma_start(tt[:], xf32[0:8, 0:NCORE * H]))
                    hw.wait_all(sync)
        tt = tap("ce2", (8, NCORE * H))
        if tt is not None:
            with nc.Block() as blk:
                @blk.sync
                def _(sync):
                    hw.inc(sync.dma_start(tt[:], CE_all[:]))
                    hw.wait_all(sync)
        tt = tap("S_in", (128, B * 512))
        if tt is not None:
            with nc.Block() as blk:
                @blk.sync
                def _(sync):
                    hw.inc(sync.dma_start(tt[:], S_IN[:]))
                    hw.wait_all(sync)
        tt = tap("S_c", (128, B * 512))
        if tt is not None:
            with nc.Block() as blk:
                @blk.sync
                def _(sync):
                    hw.inc(sync.dma_start(tt[:], S_C[:].rearrange("p b f -> p (b f)")))
                    hw.wait_all(sync)

    GSC = seg(49152, 2048)
    FMT = seg(51200, 1024)
    s0_tok = att

    moeW0 = {}

    def l0_outproj():
        with nc.Block() as blk:
            @blk.gpsimd
            def _(g):
                gp.inc(g.dma_start(W_A, Wg_in[:].rearrange("(kc p) o -> p kc o", p=128)))
                gp.inc(g.dma_start(W_B, Wo_in[:].rearrange("(kc p) o -> p kc o", p=128)))
                gp.inc(g.dma_start(WG, wgate_in[0].rearrange(
                    "(kc p) m -> p kc m", p=128)))
                gp.inc(g.dma_start(WU, wup_in[0].rearrange(
                    "(kc p) m -> p kc m", p=128)))
                gp.inc(g.dma_start(WD, wdown_in[0].rearrange(
                    "(kc p) m -> p kc m", p=128)))
                moeW0["wgwu"] = gp.val - 16
                moeW0["wd"] = gp.val

        base_ve = ve.val
        with nc.Block() as blk:
            @blk.vector
            def _(v):
                for k in range(NCH):
                    sqv = xf32[:, (k % 2) * 1024:(k % 2 + 1) * 1024]
                    ve.inc(v.tensor_tensor(sqv, att[:, k, :], att[:, k, :],
                                           op=ALU.mult), 1)
                    ve.inc(v.reduce_sum(
                        rr_tok[:, k * H:(k + 1) * H],
                        sqv.rearrange("p (h d) -> p h d", h=H),
                        axis=mybir.AxisListType.X), 1)

        with nc.Block() as blk:
            @blk.scalar
            def _(s):
                se.inc(s.activation(rr_tok[:], rr_tok[:], AF.Sqrt,
                                    scale=1.0 / D, bias=eps_rms[:, 0:1]), 1)

        with nc.Block() as blk:
            @blk.vector
            def _(v):
                ve.inc(v.reciprocal(rr_tok[:], rr_tok[:]), 1)
                ve.wait_all(v)
                for k in range(NCH):
                    rap = rr_tok[:, k * H:(k + 1) * H].unsqueeze(2)
                    rap = rap.broadcast_to([128, H, D])
                    av = att[:, k, :].rearrange("p (h d) -> p h d", h=H)
                    ve.inc(v.tensor_tensor(av, av, rap, op=ALU.mult), 1)
                    wap = rw_bc[:].unsqueeze(1).broadcast_to([128, H, D])
                    ve.inc(v.tensor_tensor(av, av, wap, op=ALU.mult), 1)

        ttn = tap("attn_n", (128, NCH * E), BF16)
        if ttn is not None:
            with nc.Block() as blk:
                @blk.sync
                def _(sync):
                    hw.inc(sync.dma_start(ttn[:], seg(U, U)))
                    hw.wait_all(sync)

        for k in range(NCH):
            base_pe = pe.val
            base_ve = ve.val
            base_se = se.val
            with nc.Block() as blk:
                @blk.tensor
                def _(te):
                    gp.wait_all(te)
                    for kc in range(NFC):
                        for no in range(2):
                            inst = te.matmul(psA[:, no * 512:(no + 1) * 512],
                                             xn_fm[:, kc, k * 128:(k + 1) * 128],
                                             W_A[:, kc, no * 512:(no + 1) * 512],
                                             start=(kc == 0), stop=(kc == NFC - 1))
                    pe.inc(inst, 1)

                @blk.scalar
                def _(s):
                    pe.wait(s, base_pe + 1)
                    se.inc(s.activation(GSC[:, 0:1024], psA[:, 0:1024], AF.Silu), 1)

                @blk.vector
                def _(v):
                    se.wait(v, base_se + 1)
                    ve.inc(v.tensor_tensor(GSC[:, 1024:2048], att[:, k, :],
                                           GSC[:, 0:1024], op=ALU.mult), 1)

                def jslot(fc):
                    pA = psA[:].bitcast(BF16)
                    pB = psB[:].bitcast(BF16)
                    return [pA[:, 2048:2176], pA[:, 3072:3200],
                            pB[:, 2048:2176], pB[:, 3072:3200]][fc % 4]

                @blk.tensor
                def _(te):
                    ve.wait(te, base_ve + 1)
                    for fc in range(NFC):
                        if fc >= 4:
                            ve.wait(te, base_ve + 1 + (fc - 3))
                        pe.inc(te.transpose(
                            jslot(fc),
                            GSC[:, 1024 + fc * 128:1024 + (fc + 1) * 128],
                            id_b[:]), 1)

                @blk.vector
                def _(v):
                    for fc in range(NFC):
                        pe.wait(v, base_pe + 2 + fc)
                        ve.inc(v.tensor_copy(
                            FMT[:, fc * 128:(fc + 1) * 128], jslot(fc)), 1)

                @blk.tensor
                def _(te):
                    ve.wait(te, base_ve + 9)
                    for kc in range(NFC):
                        for no in range(2):
                            inst = te.matmul(psB[:, no * 512:(no + 1) * 512],
                                             FMT[:, kc * 128:(kc + 1) * 128],
                                             W_B[:, kc, no * 512:(no + 1) * 512],
                                             start=(kc == 0), stop=(kc == NFC - 1))
                    pe.inc(inst, 1)

                @blk.vector
                def _(v):
                    pe.wait(v, base_pe + 10)
                    ve.inc(v.tensor_copy(s0_tok[:, k, :], psB[:, 0:1024]), 1)

            if k == 0:
                tgd = tap("gdbg", (128, 3072), BF16)
                if tgd is not None:
                    with nc.Block() as blk:
                        @blk.sync
                        def _(sync):
                            hw.inc(sync.dma_start(tgd[:, 0:2048], GSC[:]))
                            hw.inc(sync.dma_start(tgd[:, 2048:3072], FMT[:]))
                            hw.wait_all(sync)
                tpb = tap("psb0", (128, 1024))
                if tpb is not None:
                    with nc.Block() as blk:
                        @blk.vector
                        def _(v):
                            ve.inc(v.tensor_copy(xf32[:, 0:1024], psB[:, 0:1024]), 1)
                    with nc.Block() as blk:
                        @blk.sync
                        def _(sync):
                            hw.inc(sync.dma_start(tpb[:], xf32[:, 0:1024]))
                            hw.wait_all(sync)

    gk_sb = sb("gk_sb", [128, NFC * H], BF16)
    C_FM = seg(88064, 2048)[0:16, :]

    def vchain(v, inst):
        ve.inc(inst, 1)
        ve.wait_all(v)

    def moe_routing():
        with nc.Block() as blk:
            @blk.gpsimd
            def _(g):
                gp.inc(g.dma_start(gk_sb[:].rearrange("p (kc h) -> p kc h", kc=NFC),
                                   gk_in[:].rearrange("(kc p) h -> p kc h", p=128)))

        base_pe = pe.val
        base_ve0 = ve.val
        with nc.Block() as blk:
            @blk.tensor
            def _(te):
                gp.wait_all(te)
                for k in range(NCH):
                    if k >= 4:
                        ve.wait(te, base_ve0 + (k - 3))
                    for kc in range(NFC):
                        inst = te.matmul(psB[:, (k % 4) * 512:(k % 4) * 512 + H],
                                         xn_fm[:, kc, k * 128:(k + 1) * 128],
                                         gk_sb[:, kc * H:(kc + 1) * H],
                                         start=(kc == 0), stop=(kc == NFC - 1))
                    pe.inc(inst, 1)

            @blk.vector
            def _(v):
                for k in range(NCH):
                    pe.wait(v, base_pe + k + 1)
                    ve.inc(v.tensor_copy(rt_sc[:, k * H:(k + 1) * H],
                                         psB[:, (k % 4) * 512:(k % 4) * 512 + H]), 1)

        with nc.Block() as blk:
            @blk.scalar
            def _(s):
                se.inc(s.activation(rt_sc[:], rt_sc[:], AF.Sigmoid), 1)

        with nc.Block() as blk:
            @blk.vector
            def _(v):
                gv = gb_bc[:].unsqueeze(1).broadcast_to([128, NCH, H])
                sv = rt_sc[:].rearrange("p (k h) -> p k h", k=NCH)
                vchain(v, v.tensor_tensor(sv, sv, gv, op=ALU.add))
                pairs = [(0, 1), (0, 2), (0, 3), (1, 2), (1, 3), (2, 3)]
                s4 = rt_sc[:].rearrange("p (k g e) -> p k g e", k=NCH, g=4)
                first = True
                for i, j in pairs:
                    dst = rt_g2[:].rearrange("p (k g) -> p k g", k=NCH)
                    vchain(v, v.tensor_tensor(dst, s4[:, :, :, i], s4[:, :, :, j],
                                              op=ALU.add))
                    g1v = rt_g1[:].rearrange("p (k g) -> p k g", k=NCH)
                    if first:
                        vchain(v, v.tensor_copy(g1v, dst))
                        first = False
                    else:
                        vchain(v, v.tensor_tensor(g1v, g1v, dst, op=ALU.max))
                g1v = rt_g1[:].rearrange("p (k g) -> p k g", k=NCH)
                first = True
                for i, j in pairs:
                    dst = rt_g2[:].rearrange("p (k g) -> p k g", k=NCH)[:, :, 0:1]
                    tmp = rt_cnt[:, 0:NCH].rearrange("p (k o) -> p k o", k=NCH)
                    vchain(v, v.tensor_tensor(tmp, g1v[:, :, i:i + 1],
                                              g1v[:, :, j:j + 1], op=ALU.min))
                    if first:
                        vchain(v, v.tensor_copy(rt_w[:].unsqueeze(2), tmp))
                        first = False
                    else:
                        vchain(v, v.tensor_tensor(rt_w[:].unsqueeze(2),
                                                  rt_w[:].unsqueeze(2), tmp,
                                                  op=ALU.max))
                vchain(v, v.tensor_copy(rt_cmb[:], rt_g1[:]))
                gm2b = rt_w[:].unsqueeze(2).broadcast_to([128, NCH, 4])
                vchain(v, v.tensor_tensor(g1v, g1v, gm2b, op=ALU.is_ge))
                gmb = rt_g1[:].rearrange("p (k g) -> p k g", k=NCH)
                gmb = gmb.unsqueeze(3).broadcast_to([128, NCH, 4, 4])
                msv = rt_ms[:].rearrange("p (k g e) -> p k g e", k=NCH, g=4)
                vchain(v, v.tensor_tensor(msv, s4, gmb, op=ALU.mult))
                vchain(v, v.memset(rt_cnt[:], 0.0))
                mall = rt_ms[:].rearrange("p (k e) -> p k e", k=NCH)
                call = rt_cnt[:].rearrange("p (k e) -> p k e", k=NCH)
                for j in range(H):
                    mj = mall[:, :, j:j + 1].broadcast_to([128, NCH, H])
                    g2 = rt_g2[:].rearrange("p (k g) -> p k g", k=NCH)
                    tmp16 = rt_sc2[:].rearrange("p (k e) -> p k e", k=NCH)
                    vchain(v, v.tensor_tensor(tmp16, mj, mall, op=ALU.is_gt))
                    vchain(v, v.tensor_tensor(call, call, tmp16, op=ALU.add))
                vchain(v, v.tensor_single_scalar(call, call, 3.5, op=ALU.is_le))
                vchain(v, v.tensor_tensor(mall, mall, call, op=ALU.mult))
                vchain(v, v.reduce_sum(rt_w[:], mall, axis=mybir.AxisListType.X))
                vchain(v, v.tensor_scalar_add(rt_w[:], rt_w[:], 1e-20))
                vchain(v, v.reciprocal(rt_w[:], rt_w[:]))
                vchain(v, v.tensor_scalar_mul(rt_w[:], rt_w[:], 2.5))
                wb = rt_w[:].unsqueeze(2).broadcast_to([128, NCH, H])
                vchain(v, v.tensor_tensor(mall, mall, wb, op=ALU.mult))

        base_pe2 = pe.val
        base_ve2 = ve.val
        with nc.Block() as blk:
            @blk.tensor
            def _(te):
                for k in range(NCH):
                    if k >= 4:
                        ve.wait(te, base_ve2 + (k - 3))
                    pe.inc(te.transpose(psB[0:16, (k % 4) * 512:(k % 4) * 512 + 128],
                                        rt_ms[:, k * H:(k + 1) * H], id_f[:]), 1)

            @blk.vector
            def _(v):
                for k in range(NCH):
                    pe.wait(v, base_pe2 + k + 1)
                    ve.inc(v.tensor_copy(C_FM[:, k * 128:(k + 1) * 128],
                                         psB[0:16, (k % 4) * 512:(k % 4) * 512 + 128]), 1)

        with nc.Block() as blk:
            @blk.sync
            def _(sync):
                hw.inc(sync.dma_start(c_d[:], C_FM[:]))
                hw.wait_all(sync)

    WG = seg(65536, 4096).rearrange("p (kc m) -> p kc m", kc=NFC)
    WU = seg(69632, 4096).rearrange("p (kc m) -> p kc m", kc=NFC)
    WD = seg(73728, 4096).rearrange("p (kc m) -> p kc m", kc=4)
    INTER = seg(77824, 8192).rearrange("p (mc n) -> p mc n", mc=4)
    CBC = seg(86016, 2048)
    UCS = seg(90112, 2048)
    SIL = seg(92160, 2048)
    Y = seg(32768, 32768, F32).rearrange("p (mo n) -> p mo n", mo=NFC)

    WU2 = seg(94208, 4096).rearrange("p (kc m) -> p kc m", kc=NFC)

    def moe_phase():
        srcs = []
        for e in range(16):
            srcs.append((wgate_in[e].rearrange("(kc p) m -> p kc m", p=128),
                         wup_in[e].rearrange("(kc p) m -> p kc m", p=128),
                         wdown_in[e].rearrange("(kc p) m -> p kc m", p=128),
                         c_d[e:e + 1, :]))
        for half in range(2):
            srcs.append((
                swg_in[:, half * 512:(half + 1) * 512].rearrange(
                    "(kc p) m -> p kc m", p=128),
                swu_in[:, half * 512:(half + 1) * 512].rearrange(
                    "(kc p) m -> p kc m", p=128),
                swd_in[half * 512:(half + 1) * 512, :].rearrange(
                    "(kc p) m -> p kc m", p=128),
                None))
        WUb = [WU, WU2]

        with nc.Block() as blk:
            @blk.sync
            def _(sync):
                hw.inc(sync.dma_start(CBC[:], srcs[0][3].broadcast_to([128, N])))
        wgwu_ready = moeW0["wgwu"]
        wd_ready = moeW0["wd"]
        cbc_ready = (hw, hw.val)

        for i, (wgs, wus, wds, csrc) in enumerate(srcs):
            accumulate = i > 0
            base_pe, base_se, base_ve = pe.val, se.val, ve.val
            wg_i, wu_i = WG, WUb[i % 2]
            my_wgwu, my_wd, my_cbc = wgwu_ready, wd_ready, cbc_ready
            last = i + 1 >= len(srcs)
            with nc.Block() as blk:
                @blk.tensor
                def _(te):
                    gp.wait(te, my_wgwu)
                    for mc in range(4):
                        if mc >= 1:
                            se.wait(te, base_se + 4 * mc)
                            ve.wait(te, base_ve + 2 * mc - 1)
                        for kc in range(NFC):
                            for Tt in range(4):
                                inst = te.matmul(
                                    psA[:, Tt * 512:(Tt + 1) * 512],
                                    wg_i[:, kc, mc * 128:(mc + 1) * 128],
                                    xn_fm[:, kc, Tt * 512:(Tt + 1) * 512],
                                    start=(kc == 0), stop=(kc == NFC - 1))
                        pe.inc(inst, 1)
                        for kc in range(NFC):
                            for Tt in range(4):
                                inst = te.matmul(
                                    psB[:, Tt * 512:(Tt + 1) * 512],
                                    wu_i[:, kc, mc * 128:(mc + 1) * 128],
                                    xn_fm[:, kc, Tt * 512:(Tt + 1) * 512],
                                    start=(kc == 0), stop=(kc == NFC - 1))
                        pe.inc(inst, 1)
                    gp.wait(te, my_wd)
                    ve.wait(te, base_ve + 8)
                    se.wait(te, base_se + 16)
                    for mo in range(NFC):
                        ps = psA if mo % 2 == 0 else psB
                        if mo >= 2:
                            ve.wait(te, base_ve + 8 + mo - 1)
                        for mc in range(4):
                            for Tt in range(4):
                                inst = te.matmul(
                                    ps[:, Tt * 512:(Tt + 1) * 512],
                                    WD[:, mc, mo * 128:(mo + 1) * 128],
                                    INTER[:, mc, Tt * 512:(Tt + 1) * 512],
                                    start=(mc == 0), stop=(mc == 3))
                        pe.inc(inst, 1)

                @blk.scalar
                def _(s):
                    for mc in range(4):
                        pe.wait(s, base_pe + 2 * mc + 1)
                        if mc >= 1:
                            ve.wait(s, base_ve + 2 * mc)
                        for Tt in range(4):
                            se.inc(s.activation(SIL[:, Tt * 512:(Tt + 1) * 512],
                                                psA[:, Tt * 512:(Tt + 1) * 512],
                                                AF.Silu), 1)

                @blk.vector
                def _(v):
                    for mc in range(4):
                        pe.wait(v, base_pe + 2 * mc + 2)
                        if csrc is not None:
                            if mc == 0:
                                my_cbc[0].wait(v, my_cbc[1])
                            ve.inc(v.tensor_tensor(UCS[:], psB[:], CBC[:],
                                                   op=ALU.mult), 1)
                        else:
                            ve.inc(v.tensor_copy(UCS[:], psB[:]), 1)
                        se.wait(v, base_se + 4 * (mc + 1))
                        ve.inc(v.tensor_tensor(INTER[:, mc, :], SIL[:], UCS[:],
                                               op=ALU.mult), 1)
                    for mo in range(NFC):
                        ps = psA if mo % 2 == 0 else psB
                        pe.wait(v, base_pe + 8 + mo + 1)
                        if accumulate:
                            ve.inc(v.tensor_tensor(Y[:, mo, :], Y[:, mo, :],
                                                   ps[:], op=ALU.add), 1)
                        else:
                            ve.inc(v.tensor_copy(Y[:, mo, :], ps[:]), 1)

                if not last:
                    ngs, nus, nds, ncs = srcs[i + 1]

                    @blk.gpsimd
                    def _(g):
                        gp.inc(g.dma_start(WUb[(i + 1) % 2], nus))
                        g.wait_ge(pe.sem, base_pe + 7)
                        gp.inc(g.dma_start(WG, ngs))
                        if ncs is not None:
                            g.wait_ge(ve.sem, base_ve + 7)
                            gp.inc(g.dma_start(
                                CBC[:], ncs.broadcast_to([128, N])))
                        g.wait_ge(pe.sem, base_pe + 16)
                        gp.inc(g.dma_start(WD, nds))
            if not last:
                wd_ready = gp.val
                wgwu_ready = wd_ready - (32 if srcs[i + 1][3] is not None else 16)
                cbc_ready = (gp, wd_ready - 16)

        G0FM = seg(65536, U).rearrange("p (fc n) -> p fc n", fc=NFC)
        with nc.Block() as blk:
            @blk.scalar
            def _(s):
                for mo in range(NFC):
                    for Tt in range(4):
                        se.inc(s.activation(
                            G0FM[:, mo, Tt * 512:(Tt + 1) * 512],
                            Y[:, mo, Tt * 512:(Tt + 1) * 512], AF.Silu), 1)

        G0TOK = seg(32768, U).rearrange("p (k f) -> p k f", k=NCH)
        transpose_tok2fm_inv(G0FM, G0TOK)

    def transpose_tok2fm_inv(src_fm, dst_tok):
        base_pe = pe.val
        base_ve = ve.val
        psv = psA[:].bitcast(BF16)
        with nc.Block() as blk:
            @blk.tensor
            def _(te):
                for idx in range(NCH * NFC):
                    k, fc = idx // NFC, idx % NFC
                    slot = idx % 4
                    if idx >= 4:
                        ve.wait(te, base_ve + (idx - 3))
                    pe.inc(te.transpose(psv[:, slot * 1024:slot * 1024 + 128],
                                        src_fm[:, fc, k * 128:(k + 1) * 128],
                                        id_b[:]), 1)

            @blk.vector
            def _(v):
                for idx in range(NCH * NFC):
                    k, fc = idx // NFC, idx % NFC
                    slot = idx % 4
                    pe.wait(v, base_pe + idx + 1)
                    ve.inc(v.tensor_copy(dst_tok[:, k, fc * 128:(fc + 1) * 128],
                                         psv[:, slot * 1024:slot * 1024 + 128]), 1)

    def residual_x1():
        G0TOK = seg(32768, U).rearrange("p (k f) -> p k f", k=NCH)
        X1 = seg(65536, U).rearrange("p (k f) -> p k f", k=NCH)
        base_ha = [hwa.val, hwb.val]
        base_ve = ve.val
        with nc.Block() as blk:
            @blk.sync
            def _(sync):
                for k in range(NCH):
                    b, th = k // 2, k % 2
                    sl = [hwa, hwb][k % 2]
                    if k >= 2:
                        ve.wait(sync, base_ve + 2 * (k - 1))
                    sl.inc(sync.dma_start(
                        xf32[:, (k % 2) * 1024:(k % 2 + 1) * 1024],
                        x_in[b, th * 128:(th + 1) * 128, :]))

            @blk.vector
            def _(v):
                for k in range(NCH):
                    [hwa, hwb][k % 2].wait(v, base_ha[k % 2] + (k // 2 + 1) * 16)
                    ve.inc(v.tensor_tensor(GSC[:, 0:1024], s0_tok[:, k, :],
                                           G0TOK[:, k, :], op=ALU.mult), 1)
                    ve.inc(v.tensor_tensor(
                        X1[:, k, :], xf32[:, (k % 2) * 1024:(k % 2 + 1) * 1024],
                        GSC[:, 0:1024], op=ALU.add), 1)

    if BUILD_UPTO >= 5:
        tt = tap("att_raw", (128, NCH * E), BF16)
        if tt is not None:
            with nc.Block() as blk:
                @blk.sync
                def _(sync):
                    hw.inc(sync.dma_start(tt[:], seg(U, U)))
                    hw.wait_all(sync)
        l0_outproj()
        if BUILD_UPTO >= 5.5:
            moe_routing()
        tt = tap("s0", (128, NCH * E), BF16)
        if tt is not None:
            with nc.Block() as blk:
                @blk.sync
                def _(sync):
                    hw.inc(sync.dma_start(tt[:], seg(U, U)))
                    hw.wait_all(sync)
        tt = tap("scb", (128, NCH * H))
        if tt is not None:
            with nc.Block() as blk:
                @blk.sync
                def _(sync):
                    hw.inc(sync.dma_start(tt[:], rt_cnt[:]))
                    hw.wait_all(sync)
        tt = tap("rdbg", (128, NCH * H + NCH * 4 + NCH * 4))
        if tt is not None:
            with nc.Block() as blk:
                @blk.sync
                def _(sync):
                    hw.inc(sync.dma_start(tt[:, 0:NCH * H], rt_sc[:]))
                    hw.inc(sync.dma_start(tt[:, NCH * H:NCH * H + NCH * 4], rt_g1[:]))
                    hw.inc(sync.dma_start(tt[:, NCH * H + NCH * 4:], rt_cmb[:]))
                    hw.wait_all(sync)
        tt = tap("cmb", (128, NCH * H))
        if tt is not None:
            with nc.Block() as blk:
                @blk.sync
                def _(sync):
                    hw.inc(sync.dma_start(tt[:], rt_ms[:]))
                    hw.wait_all(sync)
    if BUILD_UPTO >= 6:
        moe_phase()
        residual_x1()
        tt = tap("x1", (128, NCH * E), BF16)
        if tt is not None:
            with nc.Block() as blk:
                @blk.sync
                def _(sync):
                    hw.inc(sync.dma_start(
                        tt[:], seg(65536, U)))
                    hw.wait_all(sync)

    X1 = seg(65536, U).rearrange("p (k f) -> p k f", k=NCH)
    XT1 = seg(U, U).rearrange("p (k f) -> p k f", k=NCH)
    QKV = seg(U, 49152).rearrange("p (k f) -> p k f", k=NCH)
    W3RD = seg(81920, 8192).rearrange("p (kc o) -> p kc o", kc=NFC)
    SCR_F = seg(94208, 2048, F32)
    AV_T = seg(93184, 1024)
    T0_B = seg(94208, 1024)
    BQ_BC = seg(90112, 3072)
    BO_BC = seg(90112, 1024)
    scpl = seg(81920, 256, F32)
    wpl = seg(82176, 128)
    smx = seg(82304, 32, F32)
    ssum = seg(82336, 32, F32)
    pooledt = seg(96256, 2048, F32)
    pooledf = seg(94208, 2048)[0:64, :]
    disha_sb = seg(51200, 1024)[0:64, :]

    def ln1_phase():
        with nc.Block() as blk:
            @blk.vector
            def _(v):
                for k in range(NCH):
                    ve.inc(v.reduce_sum(stat_sum[:, k:k + 1], X1[:, k, :],
                                        axis=mybir.AxisListType.X), 1)

            @blk.scalar
            def _(s):
                for k in range(NCH):
                    sqv = ln_sq[:, (k % 2) * 1024:(k % 2 + 1) * 1024]
                    se.inc(s.activation(sqv, X1[:, k, :], AF.Square,
                                        accum_out=stat_sq[:, k:k + 1]), 1)

        stats_combine()

        with nc.Block() as blk:
            @blk.vector
            def _(v):
                pv = pooledt[:].rearrange("p (k d) -> p k d", k=NCH)
                for k in range(NCH):
                    ve.inc(v.tensor_scalar(XT1[:, k, :], X1[:, k, :],
                                           stat_m[:, k:k + 1], stat_r[:, k:k + 1],
                                           op0=ALU.subtract, op1=ALU.mult), 1)
                    v.reduce_sum(pv[:, k, :],
                                 XT1[:, k, :].rearrange("p (s d) -> p d s", s=16),
                                 axis=mybir.AxisListType.X)

        transpose_tok2fm(XT1, xn_fm)

    def qkv_phase():
        with nc.Block() as blk:
            @blk.gpsimd
            def _(g):
                gp.inc(g.dma_start(BQ_BC[:], bqkv_in[:].broadcast_to([128, 3 * E])))

        for p_ in range(3):
            with nc.Block() as blk:
                @blk.gpsimd
                def _(g):
                    gp.inc(g.dma_start(
                        W3RD, Wqkv_in[:, p_ * E:(p_ + 1) * E].rearrange(
                            "(kc p) o -> p kc o", p=128)))
                    if p_ == 1:
                        gp.inc(g.dma_start(wffT_d[:], wff_in[:]))

            base_pe = pe.val
            base_ve = ve.val
            nv = 1 if p_ == 2 else 4
            with nc.Block() as blk:
                @blk.tensor
                def _(te):
                    gp.wait_all(te)
                    for Tt in range(NCH):
                        ps = psA if Tt % 2 == 0 else psB
                        if Tt >= 2:
                            ve.wait(te, base_ve + (Tt - 1) * nv)
                        for kc in range(NFC):
                            for no in range(2):
                                inst = te.matmul(
                                    ps[:, no * 512:(no + 1) * 512],
                                    xn_fm[:, kc, Tt * 128:(Tt + 1) * 128],
                                    W3RD[:, kc, no * 512:(no + 1) * 512],
                                    start=(kc == 0), stop=(kc == NFC - 1))
                        pe.inc(inst, 1)

                @blk.vector
                def _(v):
                    for Tt in range(NCH):
                        ps = psA if Tt % 2 == 0 else psB
                        b = Tt // 2
                        pe.wait(v, base_pe + Tt + 1)
                        dst = QKV[:, Tt, p_ * E:(p_ + 1) * E]
                        bia = BQ_BC[:, p_ * E:(p_ + 1) * E]
                        if p_ < 2:
                            ve.inc(v.tensor_tensor(T0_B[:], ps[:, 0:1024], bia,
                                                   op=ALU.add), 1)
                            cosap = cos_t[:, b * D:(b + 1) * D].unsqueeze(1)
                            cosap = cosap.broadcast_to([128, H, D])
                            tv = T0_B[:].rearrange("p (h d) -> p h d", h=H)
                            sw4 = T0_B[:].rearrange(
                                "p (h pr two) -> p h pr two", h=H, two=2)[:, :, :, ::-1]
                            sin4 = sin_t[:, b * D:(b + 1) * D].rearrange(
                                "p (pr two) -> p pr two", two=2)
                            sin4 = sin4.unsqueeze(1).broadcast_to([128, H, D // 2, 2])
                            av4 = AV_T[:].rearrange(
                                "p (h pr two) -> p h pr two", h=H, two=2)
                            dv = dst.rearrange("p (h d) -> p h d", h=H)
                            ve.inc(v.tensor_tensor(dv, tv, cosap, op=ALU.mult), 1)
                            ve.inc(v.tensor_tensor(av4, sw4, sin4, op=ALU.mult), 1)
                            ve.inc(v.tensor_tensor(dst, dst, AV_T[:], op=ALU.add), 1)
                        else:
                            ve.inc(v.tensor_tensor(dst, ps[:, 0:1024], bia,
                                                   op=ALU.add), 1)

    def attn1_phase():
        for b in range(B):
            for th in range(2):
                k = 2 * b + th
                with nc.Block() as blk:
                    @blk.vector
                    def _(v):
                        qc = QKV[:, k, 0:E]
                        for b2 in range(b + 1):
                            kc2 = QKV[:, 2 * b2 + th, E:2 * E]
                            vchain(v, v.tensor_tensor(
                                SCR_F[:, 0:1024], qc, kc2, op=ALU.mult))
                            vchain(v, v.reduce_sum(
                                scpl[:, b2 * H:(b2 + 1) * H],
                                SCR_F[:, 0:1024].rearrange("p (h d) -> p h d", h=H),
                                axis=mybir.AxisListType.X))
                        vchain(v, v.tensor_copy(smx[:], scpl[:, 0:H]))
                        for b2 in range(1, b + 1):
                            vchain(v, v.tensor_tensor(
                                smx[:], smx[:], scpl[:, b2 * H:(b2 + 1) * H],
                                op=ALU.max))
                        for b2 in range(b + 1):
                            vchain(v, v.tensor_tensor(
                                scpl[:, b2 * H:(b2 + 1) * H],
                                scpl[:, b2 * H:(b2 + 1) * H], smx[:], op=ALU.subtract))

                    @blk.scalar
                    def _(s):
                        ve.wait_all(s)
                        for b2 in range(b + 1):
                            se.inc(s.activation(scpl[:, b2 * H:(b2 + 1) * H],
                                                scpl[:, b2 * H:(b2 + 1) * H],
                                                AF.Exp, scale=0.125), 1)

                    @blk.vector
                    def _(v):
                        se.wait_all(v)
                        vchain(v, v.tensor_copy(ssum[:], scpl[:, 0:H]))
                        for b2 in range(1, b + 1):
                            vchain(v, v.tensor_tensor(
                                ssum[:], ssum[:], scpl[:, b2 * H:(b2 + 1) * H],
                                op=ALU.add))
                        vchain(v, v.reciprocal(ssum[:], ssum[:]))
                        for b2 in range(b + 1):
                            vchain(v, v.tensor_tensor(
                                wpl[:, b2 * H:(b2 + 1) * H],
                                scpl[:, b2 * H:(b2 + 1) * H], ssum[:], op=ALU.mult))
                        out = QKV[:, k, 0:E]
                        for b2 in range(b + 1):
                            wap = wpl[:, b2 * H:(b2 + 1) * H].unsqueeze(2)
                            wap = wap.broadcast_to([128, H, D])
                            vc = QKV[:, 2 * b2 + th, 2 * E:3 * E].rearrange(
                                "p (h d) -> p h d", h=H)
                            if b2 == 0:
                                vchain(v, v.tensor_tensor(
                                    out.rearrange("p (h d) -> p h d", h=H),
                                    vc, wap, op=ALU.mult))
                            else:
                                vchain(v, v.tensor_tensor(
                                    AV_T[:].rearrange("p (h d) -> p h d", h=H),
                                    vc, wap, op=ALU.mult))
                                vchain(v, v.tensor_tensor(out, out, AV_T[:],
                                                          op=ALU.add))

    def wout_phase():
        FMT2 = seg(51200, 1024)
        with nc.Block() as blk:
            @blk.gpsimd
            def _(g):
                gp.inc(g.dma_start(W3RD, Wout_in[:].rearrange(
                    "(kc p) o -> p kc o", p=128)))
                gp.inc(g.dma_start(BO_BC[:], bout_in[:].broadcast_to([128, E])))

        for k in range(NCH):
            base_pe = pe.val
            base_ve = ve.val
            with nc.Block() as blk:
                def jslot(fc):
                    pA = psA[:].bitcast(BF16)
                    pB = psB[:].bitcast(BF16)
                    return [pA[:, 2048:2176], pA[:, 3072:3200],
                            pB[:, 2048:2176], pB[:, 3072:3200]][fc % 4]

                @blk.tensor
                def _(te):
                    gp.wait_all(te)
                    for fc in range(NFC):
                        if fc >= 4:
                            ve.wait(te, base_ve + (fc - 3))
                        pe.inc(te.transpose(
                            jslot(fc), QKV[:, k, fc * 128:(fc + 1) * 128],
                            id_b[:]), 1)

                @blk.vector
                def _(v):
                    for fc in range(NFC):
                        pe.wait(v, base_pe + fc + 1)
                        ve.inc(v.tensor_copy(
                            FMT2[:, fc * 128:(fc + 1) * 128], jslot(fc)), 1)

                @blk.tensor
                def _(te):
                    ve.wait(te, base_ve + NFC)
                    for kc in range(NFC):
                        for no in range(2):
                            inst = te.matmul(
                                psA[:, no * 512:(no + 1) * 512],
                                FMT2[:, kc * 128:(kc + 1) * 128],
                                W3RD[:, kc, no * 512:(no + 1) * 512],
                                start=(kc == 0), stop=(kc == NFC - 1))
                    pe.inc(inst, 1)

                @blk.vector
                def _(v):
                    pe.wait(v, base_pe + NFC + 1)
                    ve.inc(v.tensor_tensor(QKV[:, k, 0:E], psA[:, 0:1024],
                                           BO_BC[:], op=ALU.add), 1)

    def ffn1_phase():
        WT = W3RD
        with nc.Block() as blk:
            @blk.gpsimd
            def _(g):
                gp.inc(g.dma_start(disha_sb[0:64, :], disha_in[:]))

            @blk.sync
            def _(sync):
                gp.wait(sync, gp.val)
                for dc in range(NFC):
                    hw.inc(sync.dma_start_transpose(
                        WT[:, dc, :], wffT_d[0:E, dc * 128:(dc + 1) * 128]))

        base_pe = pe.val
        base_ve = ve.val
        with nc.Block() as blk:
            @blk.tensor
            def _(te):
                for k in range(NCH):
                    slot = k % 4
                    if k >= 4:
                        ve.wait(te, base_ve + (k - 3))
                    pe.inc(te.transpose(psB[0:64, slot * 512:slot * 512 + 128],
                                        pooledt[:, k * D:(k + 1) * D], id_f[:]), 1)

            @blk.vector
            def _(v):
                for k in range(NCH):
                    slot = k % 4
                    pe.wait(v, base_pe + k + 1)
                    ve.inc(v.tensor_copy(pooledf[0:64, k * 128:(k + 1) * 128],
                                         psB[0:64, slot * 512:slot * 512 + 128]), 1)

        base_ha = [hwa.val, hwb.val]
        for k in range(NCH):
            b, th = k // 2, k % 2
            base_pe = pe.val
            base_se = se.val
            with nc.Block() as blk:
                @blk.tensor
                def _(te):
                    hw.wait_all(te)
                    gp.wait_all(te)
                    for kc in range(NFC):
                        for no in range(2):
                            inst = te.matmul(psA[:, no * 512:(no + 1) * 512],
                                             xn_fm[:, kc, k * 128:(k + 1) * 128],
                                             WT[:, kc, no * 512:(no + 1) * 512],
                                             start=(kc == 0), stop=False)
                    pe.inc(inst, 1)
                    pe.wait(te, base_pe + 1)
                    for no in range(2):
                        inst = te.matmul(psA[:, no * 512:(no + 1) * 512],
                                         pooledf[0:64, k * 128:(k + 1) * 128],
                                         disha_sb[0:64, no * 512:(no + 1) * 512],
                                         start=False, stop=True)
                    pe.inc(inst, 1)

                @blk.scalar
                def _(s):
                    pe.wait(s, base_pe + 2)
                    se.inc(s.activation(GSC[:, 0:1024], psA[:, 0:1024], AF.Silu), 1)

                @blk.vector
                def _(v):
                    se.wait(v, base_se + 1)
                    ve.inc(v.tensor_tensor(GSC[:, 0:1024], QKV[:, k, 0:E],
                                           GSC[:, 0:1024], op=ALU.mult), 1)
                    ve.wait_all(v)
                    if k >= 2:
                        [hwa, hwb][k % 2].wait(v, base_ha[k % 2] + (k // 2) * 16)
                    ve.inc(v.tensor_tensor(
                        xf32[:, (k % 2) * 1024:(k % 2 + 1) * 1024],
                        X1[:, k, :], GSC[:, 0:1024], op=ALU.add), 1)

                @blk.sync
                def _(sync):
                    ve.wait_all(sync)
                    [hwa, hwb][k % 2].inc(sync.dma_start(
                        o_out[b, th * 128:(th + 1) * 128, :],
                        xf32[:, (k % 2) * 1024:(k % 2 + 1) * 1024]))

    if BUILD_UPTO >= 7:
        ln1_phase()
        tt = tap("xn1_fm", (128, NFC * N), BF16)
        if tt is not None:
            with nc.Block() as blk:
                @blk.sync
                def _(sync):
                    hw.inc(sync.dma_start(tt[:], seg(0, U)))
                    hw.wait_all(sync)
    if BUILD_UPTO >= 7.5:
        qkv_phase()
        tt = tap("qkv", (128, NCH * 3 * E), BF16)
        if tt is not None:
            with nc.Block() as blk:
                @blk.sync
                def _(sync):
                    hw.inc(sync.dma_start(tt[:], seg(U, 49152)))
                    hw.wait_all(sync)
    if BUILD_UPTO >= 8:
        attn1_phase()
        tt = tap("attn1", (128, NCH, E), BF16)
        if tt is not None:
            with nc.Block() as blk:
                @blk.sync
                def _(sync):
                    hw.inc(sync.dma_start(tt[:], QKV[:, :, 0:E]))
                    hw.wait_all(sync)
    if BUILD_UPTO >= 8.5:
        wout_phase()
        tt = tap("s1", (128, NCH, E), BF16)
        if tt is not None:
            with nc.Block() as blk:
                @blk.sync
                def _(sync):
                    hw.inc(sync.dma_start(tt[:], QKV[:, :, 0:E]))
                    hw.wait_all(sync)
    if BUILD_UPTO >= 9:
        ffn1_phase()

    with nc.Block() as blk:
        @blk.sync
        def _(sync):
            hw.wait_all(sync)
            hwa.wait_all(sync)
            hwb.wait_all(sync)

        @blk.gpsimd
        def _(g):
            gp.wait_all(g)

        @blk.scalar
        def _(s):
            hs.wait_all(s)

    return nc, taps


def host_inputs(inputs, core):
    consts = build_consts()
    m = {
        "Wq": inputs["L0_attn_Wq"], "Wk": inputs["L0_attn_Wk"], "Wv": inputs["L0_attn_Wv"],
        "Wa": inputs["L0_attn_Wa"], "Wg": inputs["L0_attn_Wg"], "Wo": inputs["L0_attn_Wo"],
        "A_log": np.asarray(inputs["L0_attn_A_log"]).reshape(H, 1),
        "dt_bias": np.asarray(inputs["L0_attn_dt_bias"]).reshape(H, 1),
        "rms_w": np.asarray(inputs["L0_attn_rms_w"]).reshape(1, D),
        "gate_kernel": inputs["L0_ffn_gate_kernel"],
        "gate_bias": np.asarray(inputs["L0_ffn_gate_bias"]).reshape(1, H),
        "w_gate": inputs["L0_ffn_w_gate"], "w_up": inputs["L0_ffn_w_up"],
        "w_down": inputs["L0_ffn_w_down"],
        "sw_gate": inputs["L0_ffn_sw_gate"], "sw_up": inputs["L0_ffn_sw_up"],
        "sw_down": inputs["L0_ffn_sw_down"],
        "Wqkv": inputs["L1_attn_Wqkv"],
        "bqkv": np.asarray(inputs["L1_attn_bqkv"]).reshape(1, 3 * E),
        "Wout": inputs["L1_attn_Wout"],
        "bout": np.asarray(inputs["L1_attn_bout"]).reshape(1, E),
        "wff": inputs["L1_ffn_weight"], "disha": inputs["L1_ffn_disha"],
    }
    m = {k: np.ascontiguousarray(np.asarray(v, dtype=np.float32)) for k, v in m.items()}
    pm = np.broadcast_to((np.arange(8) < core).astype(np.float32)[None, :], (8, 8))
    m["pmask"] = np.ascontiguousarray(pm)
    m.update(consts)
    x = np.asarray(inputs["L0_attn_x"])
    m["x"] = np.ascontiguousarray(x[:, core * TL:(core + 1) * TL, :], dtype=np.float32)
    return m


def kernel(**inputs):
    nc, taps = build_program()
    in_maps = [host_inputs(inputs, c) for c in range(NCORE)]
    res = run_bass_kernel_spmd(nc, in_maps, core_ids=list(range(NCORE)))
    outs = [np.asarray(r["o"]) for r in res.results]
    full = np.concatenate(outs, axis=1)
    if DEBUG_TAPS:
        kernel.tap_results = [
            {k: np.asarray(v) for k, v in r.items() if k.startswith("tap_")}
            for r in res.results]
    return full
